# version 06
# baseline (speedup 1.0000x reference)
import numpy as np
import ml_dtypes
import concourse.bass as bass
import concourse.mybir as mybir
from concourse.bass_utils import run_bass_kernel_spmd
from concourse.bass_core_rust import IndirectOffsetOnAxis

F32 = mybir.dt.float32
BF16 = mybir.dt.bfloat16
AF = mybir.ActivationFunctionType
ALU = mybir.AluOpType
AX = mybir.AxisListType

C = 1024
BLK = 512
QK_SCALE_A = 1.0 / float(np.sqrt(2.0 * C))
QK_SCALE_B = 0.125
EPS = 1e-5
NEG_BIG = -3.0e38
THETA = 100000.0
CONST_VALS = [1e-5, 1e-30, float(np.log(QK_SCALE_A)), float(np.log(QK_SCALE_B))]

ENGINES = ("tensor", "vector", "scalar", "gpsimd", "sync")


class Rec:
    def __init__(self):
        self.ops = []
        self.barriers = []

    def op(self, engine, fn, r=(), w=(), dma=False):
        assert engine in ENGINES
        self.ops.append((engine, fn, tuple(r), tuple(w), dma))

    def barrier(self):
        self.barriers.append(len(self.ops))

    def emit(self, nc):
        n = len(self.ops)
        stream, op_pos = [], []
        sidx = {}
        for (eng, fn, r, w, dma) in self.ops:
            st = ("d", eng) if dma else ("e", eng)
            stream.append(st)
            op_pos.append(sidx.get(st, 0))
            sidx[st] = op_pos[-1] + 1

        last_w, readers = {}, {}
        deps = [set() for _ in range(n)]
        bar_iter = iter(self.barriers + [n + 1])
        next_bar = next(bar_iter)
        bar_deps = []
        bar_pending = {}
        last_of_stream = {}
        for i, (eng, fn, r, w, dma) in enumerate(self.ops):
            while i >= next_bar:
                bar_deps = list(last_of_stream.values())
                bar_pending = {}
                next_bar = next(bar_iter)
            st = stream[i]
            if bar_deps and st not in bar_pending:
                for j in bar_deps:
                    if j != i:
                        deps[i].add(j)
                bar_pending[st] = True
            for k in r:
                if k in last_w:
                    deps[i].add(last_w[k])
            for k in w:
                if k in last_w:
                    deps[i].add(last_w[k])
                for j in readers.get(k, ()):
                    deps[i].add(j)
            for k in r:
                readers.setdefault(k, []).append(i)
            for k in w:
                last_w[k] = i
                readers[k] = []
            last_of_stream[st] = i

        def needs_edge(i, j):
            return not (stream[i] == stream[j] == ("e", "tensor"))

        need_inc = set()
        for j in range(n):
            for i in deps[j]:
                if needs_edge(i, j):
                    need_inc.add(i)
        for i in range(n):
            if stream[i][0] == "d":
                need_inc.add(i)

        NSLOT = 4
        rank, cnt = {}, {}
        dslot, duse = {}, {}
        for i in range(n):
            st = stream[i]
            if st[0] == "d":
                k = op_pos[i]
                dslot[i] = k % NSLOT
                duse[i] = k // NSLOT
            elif i in need_inc:
                cnt[st] = cnt.get(st, 0) + 1
                rank[i] = cnt[st]

        per_eng = {e: [] for e in ENGINES}
        for i in range(n):
            per_eng[self.ops[i][0]].append(i)

        import contextlib
        with contextlib.ExitStack() as stack:
            esem = {e: stack.enter_context(nc.semaphore(f"S_{e}")) for e in ENGINES}
            dsem = {e: [stack.enter_context(nc.semaphore(f"D_{e}_{s}")) for s in range(NSLOT)]
                    for e in ENGINES}
            block = stack.enter_context(nc.Block())

            def sem_of(st, i):
                if st[0] == "d":
                    return dsem[st[1]][dslot[i]], 16 * (duse[i] + 1)
                return esem[st[1]], rank[i]

            def make_body(eng):
                idxs = per_eng[eng]

                def body(E):
                    water = {}
                    for j in idxs:
                        _, fn, r, w, dma = self.ops[j]
                        waits = {}
                        for i in deps[j]:
                            if i not in need_inc or not needs_edge(i, j):
                                continue
                            sem, v = sem_of(stream[i], i)
                            if waits.get(sem.name, (None, 0))[1] < v:
                                waits[sem.name] = (sem, v)
                        if dma:
                            sem = dsem[eng][dslot[j]]
                            v = 16 * duse[j]
                            if v > 0 and waits.get(sem.name, (None, 0))[1] < v:
                                waits[sem.name] = (sem, v)
                        for name in sorted(waits):
                            sem, v = waits[name]
                            if water.get(name, 0) >= v:
                                continue
                            E.wait_ge(sem, v)
                            water[name] = v
                        inst = fn(E)
                        if j in need_inc:
                            st = stream[j]
                            if st[0] == "d":
                                inst.then_inc(dsem[st[1]][dslot[j]], 16)
                            else:
                                inst.then_inc(esem[st[1]], 1)
                return body

            for eng in ENGINES:
                getattr(block, eng)(make_body(eng))



def build_nc(core, stage="full", taps=False):
    nc = bass.Bass("TRN2", target_bir_lowering=False)
    R = Rec()

    def din(name, shape):
        return nc.dram_tensor(name, shape, F32, kind="ExternalInput").ap()

    xrows = din("xrows", [128, 16 * C])
    wqkv = din("wqkv", [128, 8 * 5120])
    qkvb = din("qkvb", [128, 40])
    i128bf = din("i128bf", [128, 128])
    maskTi = din("maskTi", [128, 128])
    maskTx = din("maskTx", [128, 128])
    blk2 = din("blk2", [128, 2])
    ones1 = din("ones1", [1, 128])
    cvals = din("cvals", [128, 8])
    onescol = din("onescol", [128, 1])
    grp4 = din("grp4", [128, 4])
    sel8 = din("sel8", [8, 8 * 128])
    ropesc = din("ropesc", [128, 4 * 128])
    wqp = din("wqp", [128, 8 * C])
    wkva = din("wkva", [128, 8 * 160])
    wln1c = din("wln1c", [128, 8])
    wln2c = din("wln2c", [128, 8])
    wkvbka = din("wkvbka", [128, 128])
    wkvbtok = din("wkvbtok", [128, 66])
    idx8 = nc.dram_tensor("idx8", [128, 8], mybir.dt.int32, kind="ExternalInput").ap()
    woe = din("woe", [128, 4 * C])
    wgt = din("wgt", [128, 8 * 8])
    ebias = din("ebias", [8, 1])
    cq = din("cq", [128, 2048])
    sq_t = din("sq_t", [128, 2048])
    ck = din("ck", [32, 2048])
    sk_t = din("sk_t", [32, 2048])
    wexp = din("wexp", [128, 9 * 12288])

    o_x = nc.dram_tensor("o_x", [128, 16 * C], F32, kind="ExternalOutput").ap()
    xscr = nc.dram_tensor("xscr", [128, 16 * C], F32).ap()
    xascr = nc.dram_tensor("xascr", [128, 16 * C], F32).ap()
    ccin = nc.dram_tensor("ccin", [160, 2048], BF16).ap()
    ccout = nc.dram_tensor("ccout", [8 * 160, 2048], BF16, addr_space="Shared").ap()
    qgin = nc.dram_tensor("qgin", [128, 16384], BF16).ap()
    qgout = nc.dram_tensor("qgout", [1024, 16384], BF16, addr_space="Shared").ap()
    ygin = nc.dram_tensor("ygin", [128, 8192], BF16).ap()
    ygout = nc.dram_tensor("ygout", [1024, 8192], BF16, addr_space="Shared").ap()

    ARENA = 206000
    _ar = nc.bump_sbuf(ARENA)
    ARENA_BASE = _ar[0]
    alloc_at = nc.alloc_sbuf_tensor_at
    cur = [0]

    def place(name, shape, dtype, at=None):
        nbytes = int(np.prod(shape[1:])) * (4 if dtype in (F32, mybir.dt.int32) else 2)
        if at is None:
            at = cur[0]
            cur[0] = at + ((nbytes + 63) // 64) * 64
        return alloc_at(name, shape, dtype, offset=ARENA_BASE + at)

    i128b = place("i128b", [128, 128], BF16)
    i128f = place("i128f", [128, 128], F32)
    mTi = place("mTi", [128, 128], BF16)
    mTx = place("mTx", [128, 128], BF16)
    blk2_s = place("blk2_s", [128, 2], BF16)
    ones1_s = place("ones1_s", [1, 128], BF16)
    cvals_s = place("cvals_s", [128, 8], F32)
    onescol_s = place("onescol_s", [128, 1], BF16)
    grp4_s = place("grp4_s", [128, 4], BF16)
    sel8_s = place("sel8_s", [8, 8 * 128], BF16)
    ropesc_s = place("ropesc_s", [128, 4 * 128], BF16)
    stats = place("stats", [128, 12], F32)
    mv = place("mv", [128, 2], F32)
    lnt = place("lnt", [128, 4], F32)
    lnres = place("lnres", [128, C], F32)
    recb = place("recb", [128, 4], F32)
    mstab = place("mstab", [128, 8], F32)
    mstabb = place("mstabb", [1, 16], BF16)
    negM = place("negM", [128, 1], F32)
    negMB = place("negMB", [128, 16], F32)
    qkmax2 = place("qkmax2", [128, 24], F32)
    sqb = place("sqb", [128, BLK], BF16)
    ln2st = place("ln2st", [128, 32], F32)
    Pb = [place(f"Pb{i}", [128, 1536], BF16) for i in range(2)]
    xu = place("xu", [128, 4 * C], F32)
    qn16 = place("qn16", [128, 16], F32)
    kn16 = place("kn16", [128, 16], F32)
    gtop = place("gtop", [128, 8], F32)
    gsc = place("gsc", [128, 32], F32)
    combf = place("combf", [8, 4 * BLK], BF16)
    combz = place("combz", [128, 2], F32)
    wgt_s = place("wgt_s", [128, 8 * 8], BF16)
    ebias_s = place("ebias_s", [8, 1], F32)
    wln2c_s = place("wln2c_s", [128, 8], F32)

    for ci, cv in enumerate(CONST_VALS):
        nc.const_aps.register(F32, float(cv), cvals_s[:, ci:ci + 1])

    PERSIST_END = cur[0]
    assert PERSIST_END < 45000, PERSIST_END

    cur[0] = PERSIST_END
    wqkv_s = place("wqkv_s", [128, 8 * 5120], BF16)
    qkv = place("qkv", [128, 40 * BLK], BF16)
    x1T = place("x1T", [128, 8 * BLK], BF16)
    x1u = place("x1u", [128, 4 * C], BF16)
    vtok = place("vtok", [128, 4 * 16 * 65], BF16)
    attnA = place("attnA", [128, 4 * C], F32)
    qkvb_s = place("qkvb_s", [128, 40], F32)
    accs = place("accs", [128, BLK], F32)
    assert cur[0] <= ARENA, cur[0]

    cur[0] = PERSIST_END
    latent = place("latent", [128, 16384], BF16)
    krope4 = place("krope4", [128, 4096], BF16)
    wkvbka_s = place("wkvbka_s", [128, 128], BF16)
    wkvbtok_s = place("wkvbtok_s", [128, 66], BF16)
    wo_s = place("wo_s", [128, 4 * C], BF16)
    wln1c_s = place("wln1c_s", [128, 8], F32)
    idx8_s = place("idx8_s", [128, 8], mybir.dt.int32)
    BSH_END = cur[0]

    cur[0] = BSH_END
    x2nT = place("x2nT", [128, 8 * 2048], BF16)
    q_fm = place("q_fm", [128, 8 * 2048], BF16)
    wq_s = place("wq_s", [128, 8 * C], BF16)
    wkva_s = place("wkva_s", [128, 8 * 160], BF16)
    cqs = place("cqs", [128, 2048], BF16)
    sqs = place("sqs", [128, 2048], BF16)
    qrs = place("qrs", [128, 2 * 2048], BF16)
    CKVR_AT = cur[0]
    ckvr = place("ckvr", [128, 2048], BF16)
    kropeo_t = place("kropeo_t", [32, 2048], BF16)
    cks_t = place("cks_t", [32, 2048], BF16)
    sks_t = place("sks_t", [32, 2048], BF16)
    kropeo = kropeo_t[0:32, :]
    cks = cks_t[0:32, :]
    sks = sks_t[0:32, :]
    x2f = place("x2f", [128, C], F32, at=CKVR_AT)
    assert cur[0] <= ARENA, cur[0]

    cur[0] = BSH_END
    q_all = place("q_all", [128, 16384], BF16)
    KA0A = cur[0]
    ka_own = place("ka_own", [128, 2 * 16 * 512], BF16)
    ytmp = place("ytmp", [128, 8192], BF16, at=KA0A)
    kvt_own = place("kvt_own", [128, 2 * 16 * 264], BF16)
    ytok = place("ytok", [128, 128 * 64], BF16)
    YTOK0 = cur[0] - 16384
    acc2 = place("accs2", [128, 1024], F32)
    yfm = place("yfm", [128, 4 * BLK], BF16)
    xab = place("xab", [128, C], F32)
    qpad_he = place("qpad_he", [128, 2 * 512], BF16)
    qpad_ho = place("qpad_ho", [128, 2 * 512], BF16)
    yfull = place("yfull", [128, 16 * BLK], BF16, at=YTOK0)
    assert cur[0] <= ARENA, cur[0]

    cur[0] = PERSIST_END
    xanT = place("xanT", [128, 8 * 2048], BF16)
    moeacc = place("moeacc", [128, 16 * C], F32)
    wg_s = [place(f"wg_s{i}", [128, 8 * BLK], BF16) for i in range(2)]
    wu_s = [place(f"wu_s{i}", [128, 8 * BLK], BF16) for i in range(2)]
    wd_s = [place(f"wd_s{i}", [128, 4 * C], BF16) for i in range(2)]
    mid_s = place("mid_s", [128, 4 * BLK], BF16)
    silu_s = place("silu_s", [128, BLK], BF16)
    gatef = place("gatef", [8, 4 * BLK], F32)
    xau = place("xau", [128, C], F32)
    assert cur[0] <= ARENA, cur[0]

    ps = nc.alloc_psum_tensor
    S0 = ps("S0", [128, 1536], F32)
    S1 = ps("S1", [128, 1536], F32)
    Sg = [S0, S1]
    Sbf = [S0[:].bitcast(BF16), S1[:].bitcast(BF16)]
    accp = ps("accp", [128, 512], F32)
    auxp = ps("auxp", [128, 512], F32)

    R.op("gpsimd", lambda E: E.dma_start(out=wqkv_s[:], in_=wqkv[:]), w=["wqkv_s"], dma=True)
    R.op("sync", lambda E: E.dma_start(out=qkvb_s[:], in_=qkvb[:]), w=["qkvb_s"], dma=True)
    R.op("gpsimd", lambda E: E.dma_start(out=i128b[:], in_=i128bf[:]), w=["i128b"], dma=True)
    R.op("sync", lambda E: E.dma_start(out=i128f[:], in_=i128bf[:]), w=["i128f"], dma=True)
    R.op("gpsimd", lambda E: E.dma_start(out=mTi[:], in_=maskTi[:]), w=["mTi"], dma=True)
    R.op("gpsimd", lambda E: E.dma_start(out=mTx[:], in_=maskTx[:]), w=["mTx"], dma=True)
    R.op("gpsimd", lambda E: E.dma_start(out=blk2_s[:], in_=blk2[:]), w=["blk2_s"], dma=True)
    R.op("gpsimd", lambda E: E.dma_start(out=ones1_s[:], in_=ones1[:]), w=["ones1_s"], dma=True)
    R.op("sync", lambda E: E.dma_start(out=cvals_s[:], in_=cvals[:]), w=["cvals_s"], dma=True)
    R.op("gpsimd", lambda E: E.dma_start(out=onescol_s[:], in_=onescol[:]), w=["onescol_s"], dma=True)
    R.op("gpsimd", lambda E: E.dma_start(out=grp4_s[:], in_=grp4[:]), w=["grp4_s"], dma=True)
    R.op("gpsimd", lambda E: E.dma_start(out=sel8_s[:], in_=sel8[:]), w=["sel8_s"], dma=True)
    R.op("gpsimd", lambda E: E.dma_start(out=ropesc_s[:], in_=ropesc[:]), w=["ropesc_s"], dma=True)
    R.op("vector", lambda E: E.memset(mstab[:], 0.0),
         w=["ms0", "ms1", "ms23", "ms45", "ms6", "ms7"])

    def ln_stats(src_keys, src_ap_fn):
        R.op("vector", lambda E: E.bn_stats(out=stats[:, 0:6], in_=src_ap_fn()[:, 0:512]),
             r=src_keys, w=["stats"])
        R.op("vector", lambda E: E.bn_stats(out=stats[:, 6:12], in_=src_ap_fn()[:, 512:1024]),
             r=src_keys, w=["stats2"])
        R.op("vector", lambda E: E.bn_aggr(out=mv[:], in_=stats[:]), r=["stats", "stats2"], w=["mv"])
        R.op("scalar", lambda E: E.activation(out=lnt[:, 2:3], in_=mv[:, 1:2], func=AF.Ln, bias=float(EPS)),
             r=["mv", "cvals_s"], w=["lnt2"])
        R.op("scalar", lambda E: E.activation(out=lnt[:, 0:1], in_=lnt[:, 2:3], func=AF.Exp, scale=-0.5),
             r=["lnt2"], w=["lnt0"])
        R.op("vector", lambda E: E.tensor_scalar(out=lnt[:, 1:2], in0=mv[:, 0:1], scalar1=lnt[:, 0:1],
                                                 scalar2=-1.0, op0=ALU.mult, op1=ALU.mult),
             r=["mv", "lnt0"], w=["lnt1"])

    def rms_inv(src_keys, src_ap_fn):
        R.op("vector", lambda E: E.bn_stats(out=stats[:, 0:6], in_=src_ap_fn()[:, 0:512]),
             r=src_keys, w=["stats"])
        R.op("vector", lambda E: E.bn_stats(out=stats[:, 6:12], in_=src_ap_fn()[:, 512:1024]),
             r=src_keys, w=["stats2"])
        R.op("vector", lambda E: E.bn_aggr(out=mv[:], in_=stats[:]), r=["stats", "stats2"], w=["mv"])
        R.op("vector", lambda E: E.tensor_scalar(out=lnt[:, 3:4], in0=mv[:, 0:1],
                                                 scalar1=mv[:, 0:1], scalar2=None, op0=ALU.mult),
             r=["mv"], w=["lnt3"])
        R.op("vector", lambda E: E.tensor_add(out=lnt[:, 2:3], in0=mv[:, 1:2], in1=lnt[:, 3:4]),
             r=["mv", "lnt3"], w=["lnt2"])
        R.op("scalar", lambda E: E.activation(out=lnt[:, 3:4], in_=lnt[:, 2:3], func=AF.Ln, bias=float(EPS)),
             r=["lnt2", "cvals_s"], w=["lnt3"])
        R.op("scalar", lambda E: E.activation(out=lnt[:, 2:3], in_=lnt[:, 3:4], func=AF.Exp, scale=-0.5),
             r=["lnt3"], w=["lnt2"])

    def emit_flash(specs, scale_const, bias_fn, bias_key, acc_start=True, acc_stop=True):
        banks = []
        bcur, bused = [], 0
        for sp_ in specs:
            if bused + sp_["w"] > 512:
                banks.append(bcur)
                bcur, bused = [], 0
            bcur.append((sp_, bused))
            bused += sp_["w"]
        if bcur:
            banks.append(bcur)
        grps = [banks[i:i + 3] for i in range(0, len(banks), 3)]
        n_av = sum(len(b) for b in banks)
        av_cnt = [0]

        def emit_qk(gi):
            slot = gi % 2
            Sp = Sg[slot]
            skey = f"S{slot}"
            for bi, bank in enumerate(grps[gi]):
                for (sp_, lc) in bank:
                    col = bi * 512 + lc
                    R.op("tensor", lambda E, sp_=sp_, col=col, Sp=Sp: sp_["qk"](E, Sp, col),
                         r=sp_["qkr"], w=[skey])

        def emit_exp(gi):
            slot = gi % 2
            Sp, Pp = Sg[slot], Pb[slot]
            skey, pkey = f"S{slot}", f"P{slot}"
            runs = []
            for bi, bank in enumerate(grps[gi]):
                for (sp_, lc) in bank:
                    col = bi * 512 + lc
                    if runs and runs[-1][1] == col:
                        runs[-1][1] = col + sp_["w"]
                    else:
                        runs.append([col, col + sp_["w"]])
            for (c0, c1) in runs:
                R.op("scalar", lambda E, c0=c0, c1=c1, Sp=Sp, Pp=Pp: E.activation(
                    out=Pp[:, c0:c1], in_=Sp[:, c0:c1], func=AF.Exp,
                    scale=float(scale_const), bias=bias_fn()),
                    r=[skey, bias_key], w=[pkey])

        def emit_mask(gi):
            slot = gi % 2
            Pp = Pb[slot]
            pkey = f"P{slot}"
            mi = 0
            for bi, bank in enumerate(grps[gi]):
                for (sp_, lc) in bank:
                    if sp_["mask"] is None:
                        continue
                    col = bi * 512 + lc
                    eng = "vector" if mi % 2 == 0 else "gpsimd"
                    mi += 1
                    R.op(eng, lambda E, sp_=sp_, col=col, Pp=Pp: E.tensor_mul(
                        out=Pp[:, col:col + 128], in0=Pp[:, col:col + 128],
                        in1=sp_["mask"]()),
                        r=[pkey] + sp_["maskr"], w=[pkey])

        def emit_av(gi):
            slot = gi % 2
            Pp = Pb[slot]
            pkey = f"P{slot}"
            for bi, bank in enumerate(grps[gi]):
                for (sp_, lc) in bank:
                    col = bi * 512 + lc
                    av_cnt[0] += 1
                    R.op("tensor", lambda E, sp_=sp_, col=col, Pp=Pp,
                         st=(av_cnt[0] == 1 and acc_start),
                         sp2=(av_cnt[0] == n_av and acc_stop): sp_["av"](E, Pp, col, st, sp2),
                         r=sp_["avr"] + [pkey], w=["accp"])

        n = len(grps)
        emit_qk(0)
        emit_exp(0)
        emit_mask(0)
        for gi in range(1, n):
            emit_qk(gi)
            emit_exp(gi)
            emit_mask(gi)
            emit_av(gi - 1)
        emit_av(n - 1)

    tilesA = [(0, gp, 512) for gp in range(16)]
    tilesA += [(2, gp, 256) for gp in range(16)]
    for i in range(16):
        tilesA += [(1, i, 384), (3, i, 128)]

    def kslice(gp):
        j = gp // 2
        return (16 + j) if gp % 2 == 0 else (24 + j)

    rot1 = [(i + 1) % 32 for i in range(32)]
    rot2 = [(i + 2) % 32 for i in range(32)]
    rot_sw = []
    for i in range(16):
        rot_sw += [2 * i + 1, 2 * i]

    def emit_unit_stageA(u):
        R.op("sync", lambda E, u=u: E.dma_start(out=xu[:], in_=xrows[:, 4 * u * C: 4 * (u + 1) * C]),
             w=[f"xu:{t}" for t in range(4)], dma=True)

        for t in range(4):
            ln_stats([f"xu:{t}"], lambda t=t: xu[:, t * C:(t + 1) * C])
            R.op("vector", lambda E, t=t: E.tensor_scalar(
                out=x1u[:, t * C:(t + 1) * C], in0=xu[:, t * C:(t + 1) * C],
                scalar1=lnt[:, 0:1], scalar2=lnt[:, 1:2], op0=ALU.mult, op1=ALU.add),
                r=[f"xu:{t}", "lnt0", "lnt1"], w=[f"x1u:{t}"])

        for t in range(4):
            for cch in range(8):
                slot = (t * 8 + cch) % 2
                bv = Sbf[slot]
                R.op("tensor", lambda E, t=t, c=cch, bv=bv: E.transpose(
                    out=bv[:, 0:128], in_=x1u[:, t * C + c * 128: t * C + (c + 1) * 128],
                    identity=i128b[:]),
                    r=[f"x1u:{t}", "i128b"], w=[f"S{slot}"])
                R.op("vector", lambda E, t=t, c=cch, bv=bv: E.tensor_copy(
                    out=x1T[:, c * BLK + t * 128: c * BLK + (t + 1) * 128], in_=bv[:, 0:128]),
                    r=[f"S{slot}"], w=[f"x1T:{cch}"])

        for oc in range(40):
            slot = oc % 2
            Sp = Sg[slot]
            for kc in range(8):
                R.op("tensor", lambda E, oc=oc, kc=kc, Sp=Sp: E.matmul(
                    out=Sp[:, 0:512],
                    lhsT=wqkv_s[:, kc * 5120 + oc * 128: kc * 5120 + (oc + 1) * 128],
                    rhs=x1T[:, kc * BLK:(kc + 1) * BLK],
                    start=(kc == 0), stop=(kc == 7)),
                    r=["wqkv_s", f"x1T:{kc}"], w=[f"S{slot}"])
            R.op("vector", lambda E, oc=oc, Sp=Sp: E.tensor_scalar(
                out=qkv[:, oc * BLK:(oc + 1) * BLK], in0=Sp[:, 0:512],
                scalar1=qkvb_s[:, oc:oc + 1], scalar2=None, op0=ALU.add),
                r=[f"S{slot}", "qkvb_s"], w=[f"qkv:{oc}"])

        R.op("vector", lambda E: E.memset(
            vtok[:].rearrange("p (tg c) -> p tg c", c=65)[:, :, 64:65], 1.0),
            w=["vtok_ones"])
        for t in range(4):
            for i in range(8):
                oc = 32 + i
                slot = (t * 8 + i) % 2
                bv = Sbf[slot]
                base = t * 1040
                R.op("tensor", lambda E, t=t, oc=oc, bv=bv: E.transpose(
                    out=bv[:, 0:128],
                    in_=qkv[:, oc * BLK + t * 128: oc * BLK + (t + 1) * 128],
                    identity=i128b[:]),
                    r=[f"qkv:{oc}", "i128b"], w=[f"S{slot}"])
                R.op("vector", lambda E, i=i, bv=bv, base=base: E.tensor_copy(
                    out=vtok[:, base + 65 * (2 * i): base + 65 * (2 * i) + 64],
                    in_=bv[:, 0:64]),
                    r=[f"S{slot}"], w=[f"vtok:{base // 1040}:{2*i}"])
                R.op("vector", lambda E, i=i, bv=bv, base=base: E.tensor_copy(
                    out=vtok[:, base + 65 * (2 * i + 1): base + 65 * (2 * i + 1) + 64],
                    in_=bv[:, 64:128]),
                    r=[f"S{slot}"], w=[f"vtok:{base // 1040}:{2*i+1}"])

        for ch in range(24):
            R.op("vector", lambda E, ch=ch: E.tensor_mul(
                out=sqb[:], in0=qkv[:, ch * BLK:(ch + 1) * BLK], in1=qkv[:, ch * BLK:(ch + 1) * BLK]),
                r=[f"qkv:{ch}"], w=["sqb"])
            R.op("tensor", lambda E: E.matmul(
                out=auxp[0:2, 0:512], lhsT=blk2_s[:], rhs=sqb[:], start=True, stop=True),
                r=["blk2_s", "sqb"], w=["auxp"])
            R.op("vector", lambda E, ch=ch: E.tensor_reduce(
                out=qkmax2[0:2, ch:ch + 1], in_=auxp[0:2, 0:512], axis=AX.X, op=ALU.max),
                r=["auxp"], w=[f"qkmax2:{ch}"])
        R.op("vector", lambda E: E.tensor_reduce(
            out=mstab[0:2, 0:1], in_=qkmax2[0:2, 0:16], axis=AX.X, op=ALU.max),
            r=[f"qkmax2:{ch}" for ch in range(16)], w=["ms0"])
        R.op("vector", lambda E: E.tensor_reduce(
            out=mstab[0:2, 1:2], in_=qkmax2[0:2, 16:24], axis=AX.X, op=ALU.max),
            r=[f"qkmax2:{ch}" for ch in range(16, 24)], w=["ms1"])
        R.op("vector", lambda E: E.stream_shuffle(out=mstab[0:32, 2:4], in_=mstab[0:32, 0:2], mask=rot1),
             r=["ms0", "ms1"], w=["ms23"])
        R.op("vector", lambda E: E.tensor_max(out=mstab[0:1, 4:6], in0=mstab[0:1, 0:2], in1=mstab[0:1, 2:4]),
             r=["ms0", "ms1", "ms23"], w=["ms45"])
        R.op("vector", lambda E: E.tensor_mul(out=mstab[0:1, 6:7], in0=mstab[0:1, 4:5], in1=mstab[0:1, 5:6]),
             r=["ms45"], w=["ms6"])
        R.op("scalar", lambda E: E.activation(out=mstab[0:1, 7:8], in_=mstab[0:1, 6:7], func=AF.Ln, bias=1e-30),
             r=["ms6", "cvals_s"], w=["ms7"])
        R.op("scalar", lambda E: E.activation(out=mstab[0:1, 6:7], in_=mstab[0:1, 7:8], func=AF.Exp,
                                              scale=0.5, bias=float(np.log(QK_SCALE_A))),
             r=["ms7", "cvals_s"], w=["ms6"])
        R.op("vector", lambda E: E.tensor_scalar(out=mstabb[:, 0:1], in0=mstab[0:1, 6:7], scalar1=-1.0,
                                                 scalar2=None, op0=ALU.mult),
             r=["ms6"], w=["mstabb"])
        R.op("tensor", lambda E: E.matmul(out=auxp[:, 0:1], lhsT=ones1_s[:], rhs=mstabb[:, 0:1],
                                          start=True, stop=True),
             r=["ones1_s", "mstabb"], w=["auxp"])
        R.op("vector", lambda E: E.tensor_copy(out=negM[:], in_=auxp[:, 0:1]),
             r=["auxp"], w=["negM"])

        for g in range(16):
            specs = []
            for (m, gp, w) in tilesA:
                kch = kslice(gp)
                mT, mk = (mTi, "mTi") if gp <= g else (mTx, "mTx")

                def qk(E, Sp, col, m=m, w=w, kch=kch, g=g):
                    return E.matmul(
                        out=Sp[:, col:col + w],
                        lhsT=qkv[:, kch * BLK + m * 128: kch * BLK + (m + 1) * 128],
                        rhs=qkv[:, g * BLK + m * 128: g * BLK + 512],
                        start=True, stop=True)

                def av(E, Pp, col, st, sp2, m=m, gp=gp, w=w):
                    return E.matmul(
                        out=accp[0:65, m * 128: m * 128 + w],
                        lhsT=vtok[:, m * 1040 + 65 * gp: m * 1040 + 65 * gp + 65],
                        rhs=Pp[:, col:col + w], start=st, stop=sp2)

                specs.append(dict(w=w, qk=qk, qkr=[f"qkv:{kch}", f"qkv:{g}"],
                                  mask=(lambda mT=mT: mT[:]), maskr=[mk],
                                  av=av, avr=[f"vtok:{m}:{gp}", "vtok_ones"]))
            emit_flash(specs, QK_SCALE_A, lambda: negM[:], "negM")

            R.op("vector", lambda E: E.tensor_copy(out=accs[0:65, :], in_=accp[0:65, :]),
                 r=["accp"], w=["accs"])
            for t in range(4):
                R.op("tensor", lambda E, t=t: E.transpose(
                    out=auxp[:, t * 65: t * 65 + 65], in_=accs[0:65, t * 128:(t + 1) * 128],
                    identity=i128f[0:65, 0:65]),
                    r=["accs", "i128f"], w=["auxp"])
            for t in range(4):
                R.op("vector", lambda E, t=t: E.reciprocal(
                    out=recb[:, t:t + 1], in_=auxp[:, t * 65 + 64: t * 65 + 65]),
                    r=["auxp"], w=[f"recb:{t}"])
                R.op("vector", lambda E, t=t, g=g: E.tensor_scalar(
                    out=attnA[:, t * C + g * 64: t * C + g * 64 + 64],
                    in0=auxp[:, t * 65: t * 65 + 64],
                    scalar1=recb[:, t:t + 1], scalar2=None, op0=ALU.mult),
                    r=["auxp", f"recb:{t}"], w=[f"attnA:{t}:{g}"])

        for t in range(4):
            aks = [f"attnA:{t}:{g}" for g in range(16)]
            ln_stats(aks, lambda t=t: attnA[:, t * C:(t + 1) * C])
            R.op("vector", lambda E, t=t: E.tensor_scalar(
                out=lnres[:], in0=attnA[:, t * C:(t + 1) * C],
                scalar1=lnt[:, 0:1], scalar2=lnt[:, 1:2], op0=ALU.mult, op1=ALU.add),
                r=aks + ["lnt0", "lnt1"], w=["lnres"])
            R.op("vector", lambda E, t=t: E.tensor_add(
                out=xu[:, t * C:(t + 1) * C], in0=xu[:, t * C:(t + 1) * C], in1=lnres[:]),
                r=["lnres", f"xu:{t}"], w=[f"xu:{t}"])
        dst = o_x if stage == "A" else xscr
        R.op("sync", lambda E, u=u, dst=dst: E.dma_start(
            out=dst[:, 4 * u * C: 4 * (u + 1) * C], in_=xu[:]),
            r=[f"xu:{t}" for t in range(4)], w=[f"xscr:{u}"], dma=True)

    for u in range(4):
        emit_unit_stageA(u)

    if stage == "A":
        R.op("gpsimd", lambda E: E.nop(), r=[f"xscr:{u}" for u in range(4)])
        R.emit(nc)
        return nc

    R.barrier()

    R.op("gpsimd", lambda E: E.dma_start(out=wq_s[:], in_=wqp[:]), w=["wq_s"], dma=True)
    R.op("gpsimd", lambda E: E.dma_start(out=wkva_s[:], in_=wkva[:]), w=["wkva_s"], dma=True)
    R.op("sync", lambda E: E.dma_start(out=wln1c_s[:], in_=wln1c[:]), w=["wln1c_s"], dma=True)
    R.op("sync", lambda E: E.dma_start(out=wln2c_s[:], in_=wln2c[:]), w=["wln2c_s"], dma=True)
    R.op("gpsimd", lambda E: E.dma_start(out=wkvbka_s[:], in_=wkvbka[:]), w=["wkvbka_s"], dma=True)
    R.op("gpsimd", lambda E: E.dma_start(out=wkvbtok_s[:], in_=wkvbtok[:]), w=["wkvbtok_s"], dma=True)
    R.op("sync", lambda E: E.dma_start(out=idx8_s[:], in_=idx8[:]), w=["idx8_s"], dma=True)
    R.op("gpsimd", lambda E: E.dma_start(out=wo_s[:], in_=woe[:]), w=["wo_s"], dma=True)
    R.op("gpsimd", lambda E: E.dma_start(out=wgt_s[:], in_=wgt[:]), w=["wgt_raw"], dma=True)
    R.op("sync", lambda E: E.dma_start(out=ebias_s[:], in_=ebias[:]), w=["ebias_s"], dma=True)
    R.op("gpsimd", lambda E: E.dma_start(out=cqs[:], in_=cq[:]), w=["cqs"], dma=True)
    R.op("gpsimd", lambda E: E.dma_start(out=sqs[:], in_=sq_t[:]), w=["sqs"], dma=True)
    R.op("gpsimd", lambda E: E.dma_start(out=cks, in_=ck[:]), w=["cks"], dma=True)
    R.op("gpsimd", lambda E: E.dma_start(out=sks, in_=sk_t[:]), w=["sks"], dma=True)

    for kc in range(8):
        R.op("vector", lambda E, kc=kc: E.tensor_scalar(
            out=wq_s[:, kc * C:(kc + 1) * C], in0=wq_s[:, kc * C:(kc + 1) * C],
            scalar1=wln1c_s[:, kc:kc + 1], scalar2=None, op0=ALU.mult),
            r=["wq_s", "wln1c_s"], w=["wq_s"])
        R.op("vector", lambda E, kc=kc: E.tensor_scalar(
            out=wkva_s[:, kc * 160:(kc + 1) * 160], in0=wkva_s[:, kc * 160:(kc + 1) * 160],
            scalar1=wln1c_s[:, kc:kc + 1], scalar2=None, op0=ALU.mult),
            r=["wkva_s", "wln1c_s"], w=["wkva_s"])
        R.op("vector", lambda E, kc=kc: E.tensor_scalar(
            out=wgt_s[:, kc * 8:(kc + 1) * 8], in0=wgt_s[:, kc * 8:(kc + 1) * 8],
            scalar1=wln2c_s[:, kc:kc + 1], scalar2=None, op0=ALU.mult),
            r=["wgt_raw", "wln2c_s"], w=["wgt_s"])

    for u in range(4):
        R.op("sync", lambda E, u=u: E.dma_start(out=xu[:], in_=xscr[:, 4 * u * C: 4 * (u + 1) * C]),
             r=[f"xscr:{u}"], w=[f"xu:{t}" for t in range(4)], dma=True)
        for t in range(4):
            tt = 4 * u + t
            ln_stats([f"xu:{t}"], lambda t=t: xu[:, t * C:(t + 1) * C])
            R.op("vector", lambda E, tt=tt: E.tensor_copy(
                out=ln2st[:, 2 * tt: 2 * tt + 2], in_=lnt[:, 0:2]),
                r=["lnt0", "lnt1"], w=[f"ln2st:{tt}"])
            R.op("vector", lambda E, t=t: E.tensor_scalar(
                out=x2f[:], in0=xu[:, t * C:(t + 1) * C],
                scalar1=lnt[:, 0:1], scalar2=lnt[:, 1:2], op0=ALU.mult, op1=ALU.add),
                r=[f"xu:{t}", "lnt0", "lnt1"], w=["x2f"])
            rms_inv(["x2f"], lambda: x2f)
            R.op("vector", lambda E: E.tensor_scalar(
                out=sqb[:, 0:512], in0=x2f[:, 0:512], scalar1=lnt[:, 2:3],
                scalar2=None, op0=ALU.mult),
                r=["x2f", "lnt2"], w=["sqb"])
            R.op("vector", lambda E: E.tensor_scalar(
                out=qrs[:, 0:512], in0=x2f[:, 512:1024], scalar1=lnt[:, 2:3],
                scalar2=None, op0=ALU.mult),
                r=["x2f", "lnt2"], w=["qrs0"])
            for cch in range(8):
                slot = cch % 2
                bv = Sbf[slot]
                if cch < 4:
                    src_fn = (lambda c=cch: sqb[:, c * 128:(c + 1) * 128])
                    skey2 = "sqb"
                else:
                    src_fn = (lambda c=cch: qrs[:, (c - 4) * 128:(c - 4) * 128 + 128])
                    skey2 = "qrs0"
                R.op("tensor", lambda E, src_fn=src_fn, bv=bv: E.transpose(
                    out=bv[:, 0:128], in_=src_fn(), identity=i128b[:]),
                    r=[skey2, "i128b"], w=[f"S{slot}"])
                R.op("vector", lambda E, c=cch, bv=bv, tt=tt: E.tensor_copy(
                    out=x2nT[:, c * 2048 + tt * 128: c * 2048 + (tt + 1) * 128], in_=bv[:, 0:128]),
                    r=[f"S{slot}"], w=[f"x2nT:{cch}"])

    for tt4 in range(4):
        slot = tt4 % 2
        Sp = Sg[slot]
        for kc in range(8):
            R.op("tensor", lambda E, tt4=tt4, kc=kc, Sp=Sp: E.matmul(
                out=Sp[:, 0:512], lhsT=wkva_s[:, kc * 160: kc * 160 + 128],
                rhs=x2nT[:, kc * 2048 + tt4 * 512: kc * 2048 + (tt4 + 1) * 512],
                start=(kc == 0), stop=(kc == 7)),
                r=["wkva_s", f"x2nT:{kc}"], w=[f"S{slot}"])
        R.op("vector", lambda E, tt4=tt4, Sp=Sp: E.tensor_copy(
            out=ckvr[:, tt4 * 512:(tt4 + 1) * 512], in_=Sp[:, 0:512]),
            r=[f"S{slot}"], w=["ckvr"])
        for kc in range(8):
            R.op("tensor", lambda E, tt4=tt4, kc=kc, Sp=Sp: E.matmul(
                out=Sp[0:32, 512:1024], lhsT=wkva_s[:, kc * 160 + 128: kc * 160 + 160],
                rhs=x2nT[:, kc * 2048 + tt4 * 512: kc * 2048 + (tt4 + 1) * 512],
                start=(kc == 0), stop=(kc == 7)),
                r=["wkva_s", f"x2nT:{kc}"], w=[f"S{slot}"])
        R.op("vector", lambda E, tt4=tt4, Sp=Sp: E.tensor_copy(
            out=kropeo[:, tt4 * 512:(tt4 + 1) * 512], in_=Sp[0:32, 512:1024]),
            r=[f"S{slot}"], w=["kropeo"])

    R.op("vector", lambda E: E.stream_shuffle(out=qrs[0:32, 0:2048], in_=kropeo, mask=rot_sw),
         r=["kropeo"], w=["qrs0"])
    R.op("vector", lambda E: E.tensor_mul(out=qrs[0:32, 2048:4096], in0=kropeo, in1=cks),
         r=["kropeo", "cks"], w=["qrs1"])
    R.op("vector", lambda E: E.tensor_mul(out=qrs[0:32, 0:2048], in0=qrs[0:32, 0:2048], in1=sks),
         r=["qrs0", "sks"], w=["qrs0"])
    R.op("vector", lambda E: E.tensor_add(out=kropeo, in0=qrs[0:32, 2048:4096], in1=qrs[0:32, 0:2048]),
         r=["qrs0", "qrs1"], w=["kropeo"])

    R.op("sync", lambda E: E.dma_start(out=ccin[0:128, :], in_=ckvr[:]), r=["ckvr"], w=["ccin"], dma=True)
    R.op("sync", lambda E: E.dma_start(out=ccin[128:160, :], in_=kropeo), r=["kropeo"], w=["ccin2"], dma=True)
    R.op("gpsimd", lambda E: E.collective_compute(
        "AllGather", ALU.bypass, replica_groups=[list(range(8))],
        ins=[ccin[:]], outs=[ccout[:]]),
        r=["ccin", "ccin2"], w=["ccout"])

    for oc in range(8):
        for tt4 in range(4):
            slot = (oc * 4 + tt4) % 2
            Sp = Sg[slot]
            for kc in range(8):
                R.op("tensor", lambda E, oc=oc, tt4=tt4, kc=kc, Sp=Sp: E.matmul(
                    out=Sp[:, 0:512], lhsT=wq_s[:, kc * C + oc * 128: kc * C + (oc + 1) * 128],
                    rhs=x2nT[:, kc * 2048 + tt4 * 512: kc * 2048 + (tt4 + 1) * 512],
                    start=(kc == 0), stop=(kc == 7)),
                    r=["wq_s", f"x2nT:{kc}"], w=[f"S{slot}"])
            R.op("vector", lambda E, oc=oc, tt4=tt4, Sp=Sp: E.tensor_copy(
                out=q_fm[:, oc * 2048 + tt4 * 512: oc * 2048 + (tt4 + 1) * 512], in_=Sp[:, 0:512]),
                r=[f"S{slot}"], w=[f"q_fm:{oc}"])

    for oc in range(8):
        R.op("vector", lambda E, oc=oc: E.stream_shuffle(
            out=qrs[:, 0:2048], in_=q_fm[:, oc * 2048:(oc + 1) * 2048], mask=rot_sw),
            r=[f"q_fm:{oc}"], w=["qrs0"])
        R.op("vector", lambda E, oc=oc: E.tensor_mul(
            out=qrs[:, 2048:4096], in0=q_fm[:, oc * 2048:(oc + 1) * 2048], in1=cqs[:]),
            r=[f"q_fm:{oc}", "cqs"], w=["qrs1"])
        R.op("vector", lambda E: E.tensor_mul(out=qrs[:, 0:2048], in0=qrs[:, 0:2048], in1=sqs[:]),
             r=["qrs0", "sqs"], w=["qrs0"])
        R.op("vector", lambda E, oc=oc: E.tensor_add(
            out=q_fm[:, oc * 2048:(oc + 1) * 2048], in0=qrs[:, 2048:4096], in1=qrs[:, 0:2048]),
            r=["qrs0", "qrs1"], w=[f"q_fm:{oc}"])

    R.op("sync", lambda E: E.dma_start(out=qgin[:], in_=q_fm[:]),
         r=[f"q_fm:{oc}" for oc in range(8)], w=["qgin"], dma=True)
    R.op("gpsimd", lambda E: E.collective_compute(
        "AllGather", ALU.bypass, replica_groups=[list(range(8))],
        ins=[qgin[:]], outs=[qgout[:]]),
        r=["qgin"], w=["qgout"])

    R.op("vector", lambda E: E.tensor_mul(
        out=sqb[:, 0:128], in0=wkvbka_s[:], in1=wkvbka_s[:]),
        r=["wkvbka_s"], w=["sqb"])
    R.op("tensor", lambda E: E.matmul(out=auxp[0:1, 0:128], lhsT=onescol_s[:], rhs=sqb[:, 0:128],
                                      start=True, stop=True),
         r=["onescol_s", "sqb"], w=["auxp"])
    R.op("vector", lambda E: E.tensor_reduce(
        out=kn16[0:1, 0:1], in_=auxp[0:1, 0:32], axis=AX.X, op=ALU.add),
        r=["auxp"], w=["kn16:0"])
    R.op("vector", lambda E: E.tensor_reduce(
        out=kn16[0:1, 1:2], in_=auxp[0:1, 96:128], axis=AX.X, op=ALU.add),
        r=["auxp"], w=["kn16:1"])

    for r_ in range(8):
        R.op("sync", lambda E, r_=r_: E.dma_start(
            out=latent[:, r_ * 2048:(r_ + 1) * 2048], in_=ccout[160 * r_:160 * r_ + 128, :]),
            r=["ccout"], w=[f"latent:{r_}"], dma=True)
        R.op("sync", lambda E, r_=r_: E.dma_start(
            out=krope4[32 * (r_ // 2):32 * (r_ // 2) + 32,
                       (r_ % 2) * 2048:(r_ % 2) * 2048 + 2048],
            in_=ccout[160 * r_ + 128:160 * (r_ + 1), :]),
            r=["ccout"], w=[f"krope4:{r_}"], dma=True)

    R.op("vector", lambda E: E.memset(mstab[0:32, 0:2], 0.0), w=["ms0", "ms1"])
    for chk in range(32):
        R.op("vector", lambda E, chk=chk: E.tensor_mul(
            out=sqb[:], in0=latent[:, chk * 512:(chk + 1) * 512], in1=latent[:, chk * 512:(chk + 1) * 512]),
            r=[f"latent:{chk // 4}"], w=["sqb"])
        R.op("tensor", lambda E: E.matmul(out=auxp[0:1, 0:512], lhsT=onescol_s[:], rhs=sqb[:],
                                          start=True, stop=True),
             r=["onescol_s", "sqb"], w=["auxp"])
        R.op("vector", lambda E: E.tensor_reduce(out=mstab[0:1, 2:3], in_=auxp[0:1, 0:512],
                                                 axis=AX.X, op=ALU.max),
             r=["auxp"], w=["ms23"])
        R.op("vector", lambda E: E.tensor_max(out=mstab[0:1, 0:1], in0=mstab[0:1, 0:1], in1=mstab[0:1, 2:3]),
             r=["ms0", "ms23"], w=["ms0"])
    for chk in range(8):
        R.op("vector", lambda E, chk=chk: E.tensor_mul(
            out=sqb[:], in0=krope4[:, chk * 512:(chk + 1) * 512], in1=krope4[:, chk * 512:(chk + 1) * 512]),
            r=[f"krope4:{r}" for r in range(8)], w=["sqb"])
        R.op("tensor", lambda E: E.matmul(out=auxp[0:4, 0:512], lhsT=grp4_s[:], rhs=sqb[:],
                                          start=True, stop=True),
             r=["grp4_s", "sqb"], w=["auxp"])
        R.op("vector", lambda E: E.tensor_reduce(out=mstab[0:4, 2:3], in_=auxp[0:4, 0:512],
                                                 axis=AX.X, op=ALU.max),
             r=["auxp"], w=["ms23"])
        R.op("vector", lambda E: E.tensor_max(out=mstab[0:4, 1:2], in0=mstab[0:4, 1:2], in1=mstab[0:4, 2:3]),
             r=["ms1", "ms23"], w=["ms1"])
    R.op("vector", lambda E: E.stream_shuffle(out=mstab[0:32, 2:3], in_=mstab[0:32, 1:2], mask=rot2),
         r=["ms1"], w=["ms23"])
    R.op("vector", lambda E: E.tensor_max(out=mstab[0:32, 1:2], in0=mstab[0:32, 1:2], in1=mstab[0:32, 2:3]),
         r=["ms1", "ms23"], w=["ms1"])
    R.op("vector", lambda E: E.stream_shuffle(out=mstab[0:32, 2:3], in_=mstab[0:32, 1:2], mask=rot1),
         r=["ms1"], w=["ms23"])
    R.op("vector", lambda E: E.tensor_max(out=mstab[0:1, 1:2], in0=mstab[0:1, 1:2], in1=mstab[0:1, 2:3]),
         r=["ms1", "ms23"], w=["ms1"])

    R.barrier()

    qgv = qgout.rearrange("(s p) (c t) -> (s p c) t", p=128, t=2048)
    for s in range(8):
        R.op("gpsimd", lambda E, s=s: E.indirect_dma_start(
            out=q_all[:, s * 2048:(s + 1) * 2048], out_offset=None,
            in_=qgv[:, :],
            in_offset=IndirectOffsetOnAxis(ap=idx8_s[:, s:s + 1], axis=0)),
            r=["qgout", "idx8_s"], w=[f"q_all:{s}"], dma=True)

    R.op("vector", lambda E: E.memset(qkmax2[0:32, 0:4], 0.0), w=["qm2", "qm1", "qm3"])
    for chk in range(32):
        R.op("vector", lambda E, chk=chk: E.tensor_mul(
            out=sqb[:], in0=q_all[:, chk * 512:(chk + 1) * 512], in1=q_all[:, chk * 512:(chk + 1) * 512]),
            r=[f"q_all:{chk // 4}"], w=["sqb"])
        R.op("tensor", lambda E: E.matmul(
            out=auxp[0:2, 0:512], lhsT=blk2_s[:], rhs=sqb[:], start=True, stop=True),
            r=["blk2_s", "sqb"], w=["auxp"])
        R.op("vector", lambda E: E.tensor_reduce(
            out=qkmax2[0:2, 1:2], in_=auxp[0:2, 0:512], axis=AX.X, op=ALU.max),
            r=["auxp"], w=["qm1"])
        R.op("vector", lambda E: E.tensor_max(out=qkmax2[0:2, 0:1], in0=qkmax2[0:2, 0:1],
                                              in1=qkmax2[0:2, 1:2]),
             r=["qm1", "qm2"], w=["qm2"])
    R.op("vector", lambda E: E.tensor_scalar(
        out=kn16[0:1, 0:2], in0=kn16[0:1, 0:2], scalar1=mstab[0:1, 0:1], scalar2=mstab[0:1, 1:2],
        op0=ALU.mult, op1=ALU.add),
        r=["kn16:0", "kn16:1", "ms0", "ms1"], w=["kn16f"])
    R.op("vector", lambda E: E.tensor_copy(out=qn16[0:1, 0:1], in_=qkmax2[0:1, 0:1]),
         r=["qm2"], w=["qn16a"])
    R.op("vector", lambda E: E.stream_shuffle(out=qkmax2[0:32, 2:3], in_=qkmax2[0:32, 0:1], mask=rot1),
         r=["qm2"], w=["qm3"])
    R.op("vector", lambda E: E.tensor_copy(out=qn16[0:1, 1:2], in_=qkmax2[0:1, 2:3]),
         r=["qm3"], w=["qn16b"])
    R.op("vector", lambda E: E.tensor_mul(out=qn16[0:1, 0:2], in0=qn16[0:1, 0:2], in1=kn16[0:1, 0:2]),
         r=["qn16a", "qn16b", "kn16f"], w=["qn16f"])
    R.op("scalar", lambda E: E.activation(out=kn16[0:1, 0:2], in_=qn16[0:1, 0:2], func=AF.Ln, bias=1e-30),
         r=["qn16f", "cvals_s"], w=["kn16f2"])
    R.op("scalar", lambda E: E.activation(out=qn16[0:1, 0:2], in_=kn16[0:1, 0:2], func=AF.Exp,
                                          scale=0.5, bias=float(np.log(QK_SCALE_B))),
         r=["kn16f2", "cvals_s"], w=["qn16f2"])
    R.op("vector", lambda E: E.tensor_scalar(out=mstabb[:, 0:2], in0=qn16[0:1, 0:2], scalar1=-1.0,
                                             scalar2=None, op0=ALU.mult),
         r=["qn16f2"], w=["mstabb"])
    R.op("tensor", lambda E: E.matmul(out=auxp[:, 0:2], lhsT=ones1_s[:], rhs=mstabb[:, 0:2],
                                      start=True, stop=True),
         r=["ones1_s", "mstabb"], w=["auxp"])
    R.op("vector", lambda E: E.tensor_copy(out=negMB[:, 0:2], in_=auxp[:, 0:2]),
         r=["auxp"], w=["negMB"])

    def block_col(b, kb):
        r_ = min(kb, 15 - kb)
        qslot = 2 * b + (0 if kb < 8 else 1)
        return 2048 * r_ + 512 * qslot, r_

    for b in (0, 1):
        for kb in range(16):
            col0, r_ = block_col(b, kb)
            g4 = col0 // 4096
            kcol = col0 % 4096
            bi = b * 16 + kb
            slot = bi % 2
            Sp = Sg[slot]
            R.op("tensor", lambda E, col0=col0, Sp=Sp: E.matmul(
                out=Sp[:, 0:512], lhsT=wkvbka_s[:],
                rhs=latent[:, col0:col0 + 512], start=True, stop=False),
                r=["wkvbka_s", f"latent:{col0 // 2048}"], w=[f"S{slot}"])
            rb = 32 * g4 if g4 < 2 else 64
            rk = 32 if g4 < 2 else 64
            R.op("tensor", lambda E, g4=g4, kcol=kcol, Sp=Sp, rb=rb, rk=rk: E.matmul(
                out=Sp[:, 0:512],
                lhsT=ropesc_s[rb:rb + rk, g4 * 128:(g4 + 1) * 128],
                rhs=krope4[rb:rb + rk, kcol:kcol + 512],
                start=False, stop=True),
                r=["ropesc_s", f"krope4:{2*g4}", f"krope4:{2*g4+1}"], w=[f"S{slot}"])
            R.op("vector", lambda E, bi=bi, Sp=Sp: E.tensor_copy(
                out=ka_own[:, bi * 512:(bi + 1) * 512], in_=Sp[:, 0:512]),
                r=[f"S{slot}"], w=[f"ka:{bi}"])
            for tau in range(4):
                R.op("tensor", lambda E, tau=tau, col0=col0, Sp=Sp: E.matmul(
                    out=Sp[:, 512:578], lhsT=latent[:, col0 + tau * 128: col0 + (tau + 1) * 128],
                    rhs=wkvbtok_s[:], start=True, stop=True),
                    r=["wkvbtok_s", f"latent:{col0 // 2048}"], w=[f"S{slot}"])
                R.op("vector", lambda E, bi=bi, tau=tau, Sp=Sp: E.tensor_copy(
                    out=kvt_own[:, bi * 264 + tau * 66: bi * 264 + (tau + 1) * 66],
                    in_=Sp[:, 512:578]),
                    r=[f"S{slot}"], w=[f"kvt:{bi}"])
            R.op("vector", lambda E, bi=bi: E.memset(
                kvt_own[:, bi * 264:(bi + 1) * 264].rearrange("p (x c) -> p x c", c=33)[:, :, 32:33], 1.0),
                r=[f"kvt:{bi}"], w=[f"kvto:{bi}"])

    R.op("vector", lambda E: E.memset(qpad_he[64:128, :], 0.0), w=["qpadzh"])
    R.op("vector", lambda E: E.memset(qpad_ho[0:64, :], 0.0), w=["qpadzl"])
    for b in (0, 1):
        for jq in range(16):
            qcol0, _ = block_col(b, jq)
            par = (b * 16 + jq) % 2
            R.op("vector", lambda E, qcol0=qcol0, par=par: E.tensor_copy(
                out=qpad_he[0:64, par * 512:(par + 1) * 512],
                in_=q_all[0:64, qcol0:qcol0 + 512]),
                r=[f"q_all:{qcol0 // 2048}"], w=[f"qpe:{par}"])
            R.op("vector", lambda E, qcol0=qcol0, par=par: E.tensor_copy(
                out=qpad_ho[64:128, par * 512:(par + 1) * 512],
                in_=q_all[64:128, qcol0:qcol0 + 512]),
                r=[f"q_all:{qcol0 // 2048}"], w=[f"qpo:{par}"])
            for h2 in range(2):
                hb = 64 * h2
                qp = qpad_he if h2 == 0 else qpad_ho
                qpk = (f"qpe:{par}", "qpadzh") if h2 == 0 else (f"qpo:{par}", "qpadzl")
                specs = []
                for kb in range(jq + 1):
                    bi = b * 16 + kb
                    diag = (kb == jq)
                    for tau in range(4):
                        w = 512 - tau * 128 if diag else 512
                        scol = tau * 128 if diag else 0

                        def qk(E, Sp, col, bi=bi, tau=tau, w=w, scol=scol,
                               qp=qp, par=par):
                            return E.matmul(
                                out=Sp[:, col:col + w],
                                lhsT=ka_own[:, bi * 512 + tau * 128: bi * 512 + (tau + 1) * 128],
                                rhs=qp[:, par * 512 + scol: par * 512 + 512],
                                start=True, stop=True)

                        if diag:
                            mask, maskr = (lambda: mTi[:]), ["mTi"]
                        else:
                            mask, maskr = None, []

                        def av(E, Pp, col, st, sp2, bi=bi, tau=tau, w=w, scol=scol, hb=hb, h2=h2):
                            return E.matmul(
                                out=accp[hb:hb + 33, scol: scol + w],
                                lhsT=kvt_own[:, bi * 264 + tau * 66 + 33 * h2:
                                             bi * 264 + tau * 66 + 33 * h2 + 33],
                                rhs=Pp[:, col:col + w], start=st, stop=sp2)

                        specs.append(dict(
                            w=w, qk=qk, qkr=[f"ka:{bi}", qpk[0], qpk[1]],
                            mask=mask, maskr=maskr, av=av,
                            avr=[f"kvt:{bi}", f"kvto:{bi}"]))
                emit_flash(specs, QK_SCALE_B, (lambda h2=h2: negMB[:, h2:h2 + 1]), "negMB")
            R.op("vector", lambda E: E.tensor_copy(out=acc2[0:33, 0:512], in_=accp[0:33, :]),
                 r=["accp"], w=["acc2a"])
            R.op("vector", lambda E: E.tensor_copy(out=acc2[64:97, 512:1024], in_=accp[64:97, :]),
                 r=["accp"], w=["acc2b"])
            for h2 in range(2):
                hb = 64 * h2
                akey = "acc2a" if h2 == 0 else "acc2b"
                for t in range(4):
                    R.op("tensor", lambda E, h2=h2, hb=hb, t=t: E.transpose(
                        out=auxp[:, t * 33: t * 33 + 33],
                        in_=acc2[hb:hb + 33, h2 * 512 + t * 128: h2 * 512 + (t + 1) * 128],
                        identity=i128f[hb:hb + 33, hb:hb + 33]),
                        r=[akey, "i128f"], w=["auxp"])
                for t in range(4):
                    gt = qcol0 // 128 + t
                    R.op("vector", lambda E, t=t: E.reciprocal(
                        out=recb[:, t:t + 1], in_=auxp[:, t * 33 + 32: t * 33 + 33]),
                        r=["auxp"], w=[f"recb:{t}"])
                    R.op("vector", lambda E, t=t, gt=gt, h2=h2: E.tensor_scalar(
                        out=ytok[:, gt * 64 + 32 * h2: gt * 64 + 32 * h2 + 32],
                        in0=auxp[:, t * 33: t * 33 + 32],
                        scalar1=recb[:, t:t + 1], scalar2=None, op0=ALU.mult),
                        r=["auxp", f"recb:{t}"], w=[f"ytok:{gt}:{h2}"])

    R.op("sync", lambda E: E.dma_start(out=ygin[:], in_=ytok[:]),
         r=[f"ytok:{gt}:{h2}" for gt in range(128) for h2 in range(2)], w=["ygin"], dma=True)
    R.op("gpsimd", lambda E: E.collective_compute(
        "AllGather", ALU.bypass, replica_groups=[list(range(8))],
        ins=[ygin[:]], outs=[ygout[:]]),
        r=["ygin"], w=["ygout"])
    R.barrier()
    ygv = ygout.rearrange("(s p) (c t) -> (s p c) t", p=128, t=1024)
    for s in range(8):
        R.op("gpsimd", lambda E, s=s: E.indirect_dma_start(
            out=ytmp[:, s * 1024:(s + 1) * 1024],
            out_offset=None,
            in_=ygv[:, :],
            in_offset=IndirectOffsetOnAxis(ap=idx8_s[:, s:s + 1], axis=0)),
            r=["ygout", "idx8_s"], w=[f"ytmp:{s}"], dma=True)
    for s in range(8):
        R.op("vector", lambda E, s=s: E.tensor_copy(
            out=yfull[:].rearrange("p (t c) -> p t c", c=512)[:, :, 64 * s: 64 * s + 64],
            in_=ytmp[:, s * 1024:(s + 1) * 1024].rearrange("p (t d) -> p t d", d=64)),
            r=[f"ytmp:{s}"], w=[f"yfull:{s}"], )

    if stage == "tap_yf":
        o_yf = nc.dram_tensor("o_yf", [128, 8192], F32, kind="ExternalOutput").ap()
        R.op("gpsimd", lambda E: E.dma_start(out=o_yf[:], in_=yfull[:]),
             r=[f"yfull:{s}" for s in range(8)], w=["oyf"], dma=True)

    for u in range(4):
        R.op("sync", lambda E, u=u: E.dma_start(out=xu[:], in_=xscr[:, 4 * u * C: 4 * (u + 1) * C]),
             r=[f"xscr:{u}"], w=[f"xbl:{t}" for t in range(4)], dma=True)
        for t in range(4):
            tt = 4 * u + t
            for yc in range(4):
                slot = (t * 4 + yc) % 2
                bv = Sbf[slot]
                R.op("tensor", lambda E, tt=tt, yc=yc, bv=bv: E.transpose(
                    out=bv[:, 0:128], in_=yfull[:, tt * 512 + yc * 128: tt * 512 + (yc + 1) * 128],
                    identity=i128b[:]),
                    r=[f"yfull:{s}" for s in range(8)] + ["i128b"],
                    w=[f"S{slot}"])
                R.op("vector", lambda E, t=t, yc=yc, bv=bv: E.tensor_copy(
                    out=yfm[:, yc * 512 + t * 128: yc * 512 + (t + 1) * 128], in_=bv[:, 0:128]),
                    r=[f"S{slot}"], w=[f"yfm:{yc}"])
        for t in range(4):
            tt = 4 * u + t
            for half in range(2):
                slot = half
                Sp = Sg[slot]
                for yc in range(4):
                    R.op("tensor", lambda E, t=t, half=half, yc=yc, Sp=Sp: E.matmul(
                        out=Sp[:, 0:512],
                        lhsT=yfm[:, yc * 512 + t * 128: yc * 512 + (t + 1) * 128],
                        rhs=wo_s[:, yc * C + half * 512: yc * C + (half + 1) * 512],
                        start=(yc == 0), stop=(yc == 3)),
                        r=[f"yfm:{yc}", "wo_s"], w=[f"S{slot}"])
                R.op("vector", lambda E, t=t, half=half, tt=tt: E.tensor_scalar(
                    out=lnres[:, 0:512],
                    in0=xu[:, t * C + half * 512: t * C + (half + 1) * 512],
                    scalar1=ln2st[:, 2 * tt: 2 * tt + 1], scalar2=ln2st[:, 2 * tt + 1: 2 * tt + 2],
                    op0=ALU.mult, op1=ALU.add),
                    r=[f"xbl:{t}", f"ln2st:{tt}"], w=["lnres"])
                R.op("vector", lambda E, half=half, Sp=Sp: E.tensor_add(
                    out=xab[:, half * 512:(half + 1) * 512],
                    in0=lnres[:, 0:512], in1=Sp[:, 0:512]),
                    r=["lnres", f"S{slot}"], w=[f"xab:{half}"])
            R.op("sync", lambda E, tt=tt: E.dma_start(
                out=xascr[:, tt * C:(tt + 1) * C], in_=xab[:]),
                r=["xab:0", "xab:1"], w=[f"xascr:{tt}"], dma=True)

    R.barrier()

    for u in range(4):
        R.op("sync", lambda E, u=u: E.dma_start(out=xu[:], in_=xascr[:, 4 * u * C: 4 * (u + 1) * C]),
             r=[f"xascr:{4*u+t}" for t in range(4)], w=[f"xu:{t}" for t in range(4)], dma=True)
        for t in range(4):
            tt = 4 * u + t
            rms_inv([f"xu:{t}"], lambda t=t: xu[:, t * C:(t + 1) * C])
            R.op("vector", lambda E, t=t: E.tensor_scalar(
                out=sqb[:, 0:512], in0=xu[:, t * C: t * C + 512], scalar1=lnt[:, 2:3],
                scalar2=None, op0=ALU.mult),
                r=[f"xu:{t}", "lnt2"], w=["sqb"])
            R.op("vector", lambda E, t=t: E.tensor_scalar(
                out=silu_s[:, 0:512], in0=xu[:, t * C + 512:(t + 1) * C], scalar1=lnt[:, 2:3],
                scalar2=None, op0=ALU.mult),
                r=[f"xu:{t}", "lnt2"], w=["silu_s"])
            for cch in range(8):
                slot = cch % 2
                bv = Sbf[slot]
                if cch < 4:
                    src_fn = (lambda c=cch: sqb[:, c * 128:(c + 1) * 128])
                    skey2 = "sqb"
                else:
                    src_fn = (lambda c=cch: silu_s[:, (c - 4) * 128:(c - 4) * 128 + 128])
                    skey2 = "silu_s"
                R.op("tensor", lambda E, src_fn=src_fn, bv=bv: E.transpose(
                    out=bv[:, 0:128], in_=src_fn(), identity=i128b[:]),
                    r=[skey2, "i128b"], w=[f"S{slot}"])
                R.op("vector", lambda E, c=cch, bv=bv, tt=tt: E.tensor_copy(
                    out=xanT[:, c * 2048 + tt * 128: c * 2048 + (tt + 1) * 128], in_=bv[:, 0:128]),
                    r=[f"S{slot}"], w=[f"xanT:{cch}"])

    for G in range(4):
        slot = G % 2
        Sp = Sg[slot]
        for kc in range(8):
            R.op("tensor", lambda E, G=G, kc=kc, Sp=Sp: E.matmul(
                out=Sp[0:8, 0:512], lhsT=wgt_s[:, kc * 8:(kc + 1) * 8],
                rhs=xanT[:, kc * 2048 + G * 512: kc * 2048 + (G + 1) * 512],
                start=(kc == 0), stop=(kc == 7)),
                r=["wgt_s", f"xanT:{kc}"], w=[f"S{slot}"])
        R.op("vector", lambda E, G=G, Sp=Sp: E.tensor_scalar(
            out=gatef[0:8, G * 512:(G + 1) * 512], in0=Sp[0:8, 0:512],
            scalar1=ebias_s[0:8, 0:1], scalar2=None, op0=ALU.add),
            r=[f"S{slot}", "ebias_s"], w=[f"gatef:{G}"])
        for st4 in range(4):
            slot2 = st4 % 2
            R.op("tensor", lambda E, G=G, st4=st4, slot2=slot2: E.transpose(
                out=Sg[slot2][:, 0:8],
                in_=gatef[0:8, G * 512 + st4 * 128: G * 512 + (st4 + 1) * 128],
                identity=i128f[0:8, 0:8]),
                r=[f"gatef:{G}", "i128f"], w=[f"S{slot2}"])
            R.op("vector", lambda E, slot2=slot2: E.tensor_copy(out=gtop[:], in_=Sg[slot2][:, 0:8]),
                 r=[f"S{slot2}"], w=["glog"])
            R.op("vector", lambda E: E.max(out=gsc[:, 0:8], in_=gtop[:]),
                 r=["glog"], w=["gm"])
            R.op("vector", lambda E: E.tensor_scalar(out=gsc[:, 8:9], in0=gsc[:, 0:1], scalar1=-1.0,
                                                     scalar2=None, op0=ALU.mult),
                 r=["gm"], w=["gnm"])
            R.op("scalar", lambda E: E.activation(out=gsc[:, 16:24], in_=gtop[:], func=AF.Exp,
                                                  bias=gsc[:, 8:9]),
                 r=["glog", "gnm"], w=["ge"])
            R.op("vector", lambda E: E.tensor_scalar(out=gsc[:, 24:32], in0=gtop[:], scalar1=gsc[:, 1:2],
                                                     scalar2=None, op0=ALU.is_ge),
                 r=["glog", "gm"], w=["gsel"])
            R.op("vector", lambda E: E.tensor_mul(out=gsc[:, 16:24], in0=gsc[:, 16:24], in1=gsc[:, 24:32]),
                 r=["ge", "gsel"], w=["ge"])
            R.op("vector", lambda E: E.tensor_reduce(out=combz[:, 0:1], in_=gsc[:, 16:24],
                                                     axis=AX.X, op=ALU.add),
                 r=["ge"], w=["gz"])
            R.op("vector", lambda E: E.reciprocal(out=combz[:, 1:2], in_=combz[:, 0:1]),
                 r=["gz"], w=["gzr"])
            R.op("vector", lambda E: E.tensor_scalar(out=gtop[:], in0=gsc[:, 16:24], scalar1=combz[:, 1:2],
                                                     scalar2=None, op0=ALU.mult),
                 r=["ge", "gzr"], w=["glog"])
            slot3 = (st4 + 1) % 2
            R.op("tensor", lambda E, slot3=slot3: E.transpose(
                out=Sg[slot3][0:8, 0:128], in_=gtop[:, 0:8], identity=i128f[:, 0:128]),
                r=["glog", "i128f"], w=[f"S{slot3}"])
            R.op("vector", lambda E, G=G, st4=st4, slot3=slot3: E.tensor_copy(
                out=combf[0:8, G * 512 + st4 * 128: G * 512 + (st4 + 1) * 128],
                in_=Sg[slot3][0:8, 0:128]),
                r=[f"S{slot3}"], w=[f"combf:{G}"])

    R.barrier()
    for e in range(9):
        sl = e % 2
        wgk, wuk, wdk = f"wg{sl}", f"wu{sl}", f"wd{sl}"
        off = e * 12288
        R.op("gpsimd", lambda E, off=off, sl=sl: E.dma_start(
            out=wg_s[sl][:], in_=wexp[:, off: off + 4096]), w=[wgk], dma=True)
        R.op("gpsimd", lambda E, off=off, sl=sl: E.dma_start(
            out=wu_s[sl][:], in_=wexp[:, off + 4096: off + 8192]), w=[wuk], dma=True)
        R.op("gpsimd", lambda E, off=off, sl=sl: E.dma_start(
            out=wd_s[sl][:], in_=wexp[:, off + 8192: off + 12288]), w=[wdk], dma=True)
        for kc in range(8):
            R.op("vector", lambda E, kc=kc, sl=sl: E.tensor_scalar(
                out=wg_s[sl][:, kc * 512:(kc + 1) * 512], in0=wg_s[sl][:, kc * 512:(kc + 1) * 512],
                scalar1=wln2c_s[:, kc:kc + 1], scalar2=None, op0=ALU.mult),
                r=[wgk, "wln2c_s"], w=[wgk])
            R.op("vector", lambda E, kc=kc, sl=sl: E.tensor_scalar(
                out=wu_s[sl][:, kc * 512:(kc + 1) * 512], in0=wu_s[sl][:, kc * 512:(kc + 1) * 512],
                scalar1=wln2c_s[:, kc:kc + 1], scalar2=None, op0=ALU.mult),
                r=[wuk, "wln2c_s"], w=[wuk])
        for G in range(4):
            if e > 0:
                R.op("tensor", lambda E, e=e, G=G: E.matmul(
                    out=S0[:, 512:1024], lhsT=sel8_s[:, (e - 1) * 128: e * 128],
                    rhs=combf[0:8, G * 512:(G + 1) * 512], start=True, stop=True),
                    r=["sel8_s", f"combf:{G}"], w=["m_bc"])
            for ic in range(4):
                par = ic % 2
                gcol = 0 if par == 0 else 1024
                gk, uk2 = f"m_g{par}", f"m_u{par}"
                sqk = f"sqm{par}"
                for kc in range(8):
                    R.op("tensor", lambda E, G=G, ic=ic, kc=kc, sl=sl, gcol=gcol: E.matmul(
                        out=S0[:, gcol:gcol + 512], lhsT=wg_s[sl][:, kc * 512 + ic * 128: kc * 512 + (ic + 1) * 128],
                        rhs=xanT[:, kc * 2048 + G * 512: kc * 2048 + (G + 1) * 512],
                        start=(kc == 0), stop=(kc == 7)),
                        r=[wgk, f"xanT:{kc}"], w=[gk])
                    R.op("tensor", lambda E, G=G, ic=ic, kc=kc, sl=sl, gcol=gcol: E.matmul(
                        out=S1[:, gcol:gcol + 512], lhsT=wu_s[sl][:, kc * 512 + ic * 128: kc * 512 + (ic + 1) * 128],
                        rhs=xanT[:, kc * 2048 + G * 512: kc * 2048 + (G + 1) * 512],
                        start=(kc == 0), stop=(kc == 7)),
                        r=[wuk, f"xanT:{kc}"], w=[uk2])
                R.op("scalar", lambda E, gcol=gcol: E.activation(
                    out=silu_s[:], in_=S0[:, gcol:gcol + 512], func=AF.Silu),
                     r=[gk], w=["silu_s"])
                R.op("vector", lambda E, gcol=gcol, par=par: E.scalar_tensor_tensor(
                    out=sqb[:] if par == 0 else lnres[:, 0:512].bitcast(BF16)[:, 0:512],
                    in0=S1[:, gcol:gcol + 512], scalar=1.0, in1=silu_s[:],
                    op0=ALU.mult, op1=ALU.mult),
                    r=[uk2, "silu_s"], w=[sqk])
                mm_in = (lambda par=par: sqb[:] if par == 0 else lnres[:, 0:512].bitcast(BF16)[:, 0:512])
                if e > 0:
                    R.op("vector", lambda E, ic=ic, mm_in=mm_in: E.tensor_mul(
                        out=mid_s[:, ic * 512:(ic + 1) * 512], in0=mm_in(), in1=S0[:, 512:1024]),
                        r=[sqk, "m_bc"], w=[f"mid:{ic}"])
                else:
                    R.op("vector", lambda E, ic=ic, mm_in=mm_in: E.tensor_copy(
                        out=mid_s[:, ic * 512:(ic + 1) * 512], in_=mm_in()),
                        r=[sqk], w=[f"mid:{ic}"])
            for tsub in range(4):
                for half in range(2):
                    k8 = tsub * 2 + half
                    dps = accp if k8 % 2 == 0 else auxp
                    skey3 = "accp" if k8 % 2 == 0 else "auxp"
                    for ic in range(4):
                        R.op("tensor", lambda E, tsub=tsub, half=half, ic=ic, sl=sl,
                             dps=dps: E.matmul(
                            out=dps[:, 0:512],
                            lhsT=mid_s[:, ic * 512 + tsub * 128: ic * 512 + (tsub + 1) * 128],
                            rhs=wd_s[sl][:, ic * C + half * 512: ic * C + (half + 1) * 512],
                            start=(ic == 0), stop=(ic == 3)),
                            r=[f"mid:{ic}", wdk], w=[skey3])
                    tt = G * 4 + tsub
                    mkey = f"moe:{tt}:{half}"
                    if e == 0:
                        R.op("vector", lambda E, tt=tt, half=half, dps=dps: E.tensor_copy(
                            out=moeacc[:, tt * C + half * 512: tt * C + (half + 1) * 512],
                            in_=dps[:, 0:512]),
                            r=[skey3], w=[mkey])
                    else:
                        R.op("vector", lambda E, tt=tt, half=half, dps=dps: E.tensor_add(
                            out=moeacc[:, tt * C + half * 512: tt * C + (half + 1) * 512],
                            in0=moeacc[:, tt * C + half * 512: tt * C + (half + 1) * 512],
                            in1=dps[:, 0:512]),
                            r=[skey3, mkey], w=[mkey])

    for u in range(4):
        R.op("sync", lambda E, u=u: E.dma_start(out=xu[:], in_=xscr[:, 4 * u * C: 4 * (u + 1) * C]),
             w=[f"xu:{t}" for t in range(4)], dma=True)
        for t in range(4):
            tt = 4 * u + t
            R.op("sync", lambda E, tt=tt: E.dma_start(out=xau[:], in_=xascr[:, tt * C:(tt + 1) * C]),
                 r=[f"xascr:{tt}"], w=["xaf"], dma=True)
            R.op("vector", lambda E, tt=tt: E.tensor_add(
                out=xau[:], in0=xau[:], in1=moeacc[:, tt * C:(tt + 1) * C]),
                r=["xaf", f"moe:{tt}:0", f"moe:{tt}:1"], w=["xaf"])
            ln_stats(["xaf"], lambda: xau[:])
            R.op("vector", lambda E: E.tensor_scalar(
                out=lnres[:], in0=xau[:, 0:C],
                scalar1=lnt[:, 0:1], scalar2=lnt[:, 1:2], op0=ALU.mult, op1=ALU.add),
                r=["xaf", "lnt0", "lnt1"], w=["lnres"])
            R.op("vector", lambda E, t=t: E.tensor_add(
                out=xu[:, t * C:(t + 1) * C], in0=xu[:, t * C:(t + 1) * C], in1=lnres[:]),
                r=["lnres", f"xu:{t}"], w=[f"xu:{t}"])
        R.op("sync", lambda E, u=u: E.dma_start(
            out=o_x[:, 4 * u * C: 4 * (u + 1) * C], in_=xu[:]),
            r=[f"xu:{t}" for t in range(4)], w=[f"oxd:{u}"], dma=True)

    if taps:
        o_moe = nc.dram_tensor("o_moe", [128, 16 * C], F32, kind="ExternalOutput").ap()
        R.op("gpsimd", lambda E: E.dma_start(out=o_moe[:], in_=moeacc[:]),
             r=[f"moe:{tt}:{h}" for tt in range(16) for h in range(2)], w=["omoe"], dma=True)
        o_xan = nc.dram_tensor("o_xan", [128, 16384], F32, kind="ExternalOutput").ap()
        R.op("gpsimd", lambda E: E.dma_start(out=o_xan[:], in_=xanT[:]),
             r=[f"xanT:{cch}" for cch in range(8)], w=["oxan"], dma=True)
        o_comb = nc.dram_tensor("o_comb", [8, 2048], F32, kind="ExternalOutput").ap()
        R.op("gpsimd", lambda E: E.dma_start(out=o_comb[:], in_=combf[:]),
             r=[f"combf:{G}" for G in range(4)], w=["ocomb"], dma=True)
        o_xs = nc.dram_tensor("o_xs", [128, 16 * C], F32, kind="ExternalOutput").ap()
        o_xa = nc.dram_tensor("o_xa", [128, 16 * C], F32, kind="ExternalOutput").ap()
        o_q = nc.dram_tensor("o_q", [128, 16384], F32, kind="ExternalOutput").ap()
        o_qf = nc.dram_tensor("o_qf", [128, 16384], F32, kind="ExternalOutput").ap()
        R.op("gpsimd", lambda E: E.dma_start(out=o_qf[:], in_=qgin[:]),
             r=["qgin"], w=["oqf"], dma=True)
        o_cc = nc.dram_tensor("o_cc", [1280, 2048], F32, kind="ExternalOutput").ap()
        R.op("gpsimd", lambda E: E.dma_start(out=o_cc[:], in_=ccout[:]),
             r=["ccout"], w=["occ"], dma=True)
        o_yg = nc.dram_tensor("o_yg", [1024, 8192], F32, kind="ExternalOutput").ap()
        R.op("gpsimd", lambda E: E.dma_start(out=o_yg[:], in_=ygout[:]),
             r=["ygout"], w=["oyg"], dma=True)
        R.op("sync", lambda E: E.dma_start(out=o_xs[:], in_=xscr[:]),
             r=[f"xscr:{u}" for u in range(4)], w=["oxs"], dma=True)
        R.op("sync", lambda E: E.dma_start(out=o_xa[:], in_=xascr[:]),
             r=[f"xascr:{tt}" for tt in range(16)], w=["oxa"], dma=True)
        R.op("gpsimd", lambda E: E.dma_start(out=o_q[:], in_=q_all[:]),
             r=[f"q_all:{s}" for s in range(8)], w=["oq"], dma=True)
        R.op("gpsimd", lambda E: E.nop(), r=["oxs", "oxa", "oq", "oqf", "occ", "oyg", "omoe", "oxan", "ocomb"])
    R.op("gpsimd", lambda E: E.nop(), r=[f"oxd:{u}" for u in range(4)])

    R.emit(nc)
    return nc



def _build_wqkv(W, bias):
    Wp = np.zeros((1024, 5120), dtype=np.float32)
    bp = np.zeros(5120, dtype=np.float32)

    def put(dst, src64):
        Wp[:, dst:dst + 64] = W[:, src64:src64 + 64]
        bp[dst:dst + 64] = bias[src64:src64 + 64]

    for j in range(16):
        put(j * 128, 192 * j)
    for j in range(8):
        put(2048 + j * 128, 192 * (2 * j) + 64)
        put(2048 + j * 128 + 64, 192 * (2 * j + 1) + 64)
        put(3072 + j * 128, 192 * (2 * j + 1) + 64)
        put(3072 + j * 128 + 64, 192 * (2 * j) + 64)
        put(4096 + j * 128, 192 * (2 * j) + 128)
        put(4096 + j * 128 + 64, 192 * (2 * j + 1) + 128)
    return Wp, bp


def _chunkmajor(w, nk=8):
    cin, f = w.shape
    assert cin == nk * 128
    return np.ascontiguousarray(w.reshape(nk, 128, f).transpose(1, 0, 2).reshape(128, nk * f))


def _rope_tables(positions, nrows, rope_rows):
    t = np.asarray(positions, dtype=np.float32)
    d = 32
    freqs = (1.0 / THETA ** (np.arange(0, d, 2, dtype=np.float32) / d)).astype(np.float32)
    ang = np.outer(freqs, t)
    cosv, sinv = np.cos(ang).astype(np.float32), np.sin(ang).astype(np.float32)
    Ct = np.ones((nrows, len(t)), dtype=np.float32)
    St = np.zeros((nrows, len(t)), dtype=np.float32)
    for r0 in rope_rows:
        for i in range(16):
            Ct[r0 + 2 * i] = cosv[i]
            Ct[r0 + 2 * i + 1] = cosv[i]
            St[r0 + 2 * i] = -sinv[i]
            St[r0 + 2 * i + 1] = sinv[i]
    return Ct, St


def prep_core_inputs(c, inputs):
    blocks = [c, 15 - c]
    x = np.asarray(inputs["L0_attn_inputs"])
    rows = np.concatenate([x[b, 512 * j:512 * (j + 1)] for b in (0, 1) for j in blocks])
    xrows = np.ascontiguousarray(rows.reshape(16, 128, 1024).transpose(1, 0, 2).reshape(128, 16 * 1024))

    Wp, bp = _build_wqkv(np.asarray(inputs["L0_attn_qkv_kernel"]),
                         np.asarray(inputs["L0_attn_qkv_bias"]).reshape(3072))
    wqkv = _chunkmajor(Wp)
    qkvb = np.ascontiguousarray(bp.reshape(40, 128).T)

    i128 = np.eye(128, dtype=np.float32)
    maskTi = np.triu(np.ones((128, 128), dtype=np.float32))
    maskTx = np.triu(np.ones((128, 128), dtype=np.float32), k=1)
    blk2 = np.zeros((128, 2), dtype=np.float32)
    blk2[0:64, 0] = 1.0
    blk2[64:128, 1] = 1.0
    ones1 = np.ones((1, 128), dtype=np.float32)
    cvals_np = np.zeros((128, 8), dtype=np.float32)
    for ci, cv in enumerate(CONST_VALS):
        cvals_np[:, ci] = np.float32(cv)
    onescol = np.ones((128, 1), dtype=np.float32)
    grp4 = np.zeros((128, 4), dtype=np.float32)
    for g in range(4):
        grp4[32 * g:32 * (g + 1), g] = 1.0
    sel8 = np.zeros((8, 8 * 128), dtype=np.float32)
    for e in range(8):
        sel8[e, e * 128:(e + 1) * 128] = 1.0
    ropesc = np.zeros((128, 4 * 128), dtype=np.float32)
    for g in range(4):
        for r in range(32):
            ropesc[32 * g + r, g * 128 + 32 + r] = 1.0
            ropesc[32 * g + r, g * 128 + 64 + r] = 1.0

    Wq = np.asarray(inputs["L0_ffn_Wq"])
    permq = []
    for h in range(16):
        if h % 2 == 0:
            permq += list(range(64 * h, 64 * h + 64))
        else:
            permq += list(range(64 * h + 32, 64 * h + 64)) + list(range(64 * h, 64 * h + 32))
    wqp = _chunkmajor(Wq[:, permq])
    wkva = _chunkmajor(np.asarray(inputs["L0_ffn_Wkva"]))
    wln1c = np.ascontiguousarray(np.asarray(inputs["L0_ffn_w_ln1"]).reshape(8, 128).T)
    wln2c = np.ascontiguousarray(np.asarray(inputs["L0_ffn_w_ln2"]).reshape(8, 128).T)
    Wkvb = np.asarray(inputs["L0_ffn_Wkvb"])
    wkvbka = np.zeros((128, 128), dtype=np.float32)
    wkvbka[:, 0:32] = Wkvb[:, 32 * (2 * c): 32 * (2 * c) + 32]
    wkvbka[:, 96:128] = Wkvb[:, 32 * (2 * c + 1): 32 * (2 * c + 1) + 32]
    wkvbtok = np.zeros((128, 66), dtype=np.float32)
    wkvbtok[:, 0:32] = Wkvb[:, 32 * (2 * c): 32 * (2 * c) + 32]
    wkvbtok[:, 33:65] = Wkvb[:, 32 * (2 * c + 1): 32 * (2 * c + 1) + 32]
    idx8 = np.zeros((128, 8), dtype=np.int32)
    for s in range(8):
        idx8[:, s] = s * 1024 + np.arange(128) * 8 + c
    Wo = np.asarray(inputs["L0_ffn_Wo"])
    woe_rows = np.concatenate([Wo[64 * h: 64 * h + 32] for h in range(16)])
    woe = _chunkmajor(woe_rows, nk=4)
    wgt = _chunkmajor(np.asarray(inputs["L0_ffn_Wgate"]))
    ebias = np.asarray(inputs["L0_ffn_expert_bias"]).reshape(8, 1).astype(np.float32)

    positions = np.concatenate([np.arange(512 * j, 512 * (j + 1)) for b in (0, 1) for j in blocks])
    cqt, sqt = _rope_tables(positions, 128, [32, 64])
    ckt, skt = _rope_tables(positions, 32, [0])

    Wg = np.asarray(inputs["L0_ffn_Wg"])
    Wu = np.asarray(inputs["L0_ffn_Wu"])
    Wd = np.asarray(inputs["L0_ffn_Wd"])
    parts = []
    for e in range(9):
        if e == 0:
            g_, u_, d_ = (np.asarray(inputs["L0_ffn_sWg"])[0], np.asarray(inputs["L0_ffn_sWu"])[0],
                          np.asarray(inputs["L0_ffn_sWd"])[0])
        else:
            g_, u_, d_ = Wg[e - 1], Wu[e - 1], Wd[e - 1]
        parts += [_chunkmajor(g_), _chunkmajor(u_), _chunkmajor(d_, nk=4)]
    wexp = np.concatenate(parts, axis=1)

    return dict(
        xrows=xrows.astype(np.float32), wqkv=wqkv.astype(np.float32),
        qkvb=qkvb.astype(np.float32), i128bf=i128, maskTi=maskTi, maskTx=maskTx,
        blk2=blk2, ones1=ones1, cvals=cvals_np, onescol=onescol, grp4=grp4,
        sel8=sel8, ropesc=ropesc, wqp=wqp.astype(np.float32),
        wkva=wkva.astype(np.float32), wln1c=wln1c.astype(np.float32),
        wln2c=wln2c.astype(np.float32), wkvbka=wkvbka, wkvbtok=wkvbtok,
        woe=woe.astype(np.float32), wgt=wgt.astype(np.float32), ebias=ebias,
        idx8=idx8,
        cq=cqt, sq_t=sqt, ck=ckt, sk_t=skt, wexp=wexp.astype(np.float32))


def unpack_x(o_x):
    return np.ascontiguousarray(o_x.reshape(128, 16, 1024).transpose(1, 0, 2).reshape(2048, 1024))


def kernel(**inputs):
    import os
    stage = os.environ.get("MK_STAGE", "full")
    nc = build_nc(0, stage=stage)
    in_maps = [prep_core_inputs(c, inputs) for c in range(8)]
    res = run_bass_kernel_spmd(nc, in_maps, core_ids=list(range(8)))
    out = np.zeros((2, 8192, 1024), dtype=np.float32)
    for c in range(8):
        rows = unpack_x(np.asarray(res.results[c]["o_x"]))
        blocks = [c, 15 - c]
        i = 0
        for b in (0, 1):
            for j in blocks:
                out[b, 512 * j:512 * (j + 1)] = rows[i * 512:(i + 1) * 512]
                i += 1
    return out


# version 07
# speedup vs baseline: 1.0572x; 1.0572x over previous
import numpy as np
import ml_dtypes
import concourse.bass as bass
import concourse.mybir as mybir
from concourse.bass_utils import run_bass_kernel_spmd
from concourse.bass_core_rust import IndirectOffsetOnAxis

F32 = mybir.dt.float32
BF16 = mybir.dt.bfloat16
AF = mybir.ActivationFunctionType
ALU = mybir.AluOpType
AX = mybir.AxisListType

C = 1024
BLK = 512
QK_SCALE_A = 1.0 / float(np.sqrt(2.0 * C))
QK_SCALE_B = 0.125
EPS = 1e-5
NEG_BIG = -3.0e38
THETA = 100000.0
CONST_VALS = [1e-5, 1e-30, float(np.log(QK_SCALE_A)), float(np.log(QK_SCALE_B))]

ENGINES = ("tensor", "vector", "scalar", "gpsimd", "sync")


class Rec:
    def __init__(self):
        self.ops = []
        self.barriers = []

    def op(self, engine, fn, r=(), w=(), dma=False):
        assert engine in ENGINES
        self.ops.append((engine, fn, tuple(r), tuple(w), dma))

    def barrier(self):
        self.barriers.append(len(self.ops))

    def emit(self, nc):
        n = len(self.ops)
        stream, op_pos = [], []
        sidx = {}
        for (eng, fn, r, w, dma) in self.ops:
            st = ("d", eng) if dma else ("e", eng)
            stream.append(st)
            op_pos.append(sidx.get(st, 0))
            sidx[st] = op_pos[-1] + 1

        last_w, readers = {}, {}
        deps = [set() for _ in range(n)]
        bar_iter = iter(self.barriers + [n + 1])
        next_bar = next(bar_iter)
        bar_deps = []
        bar_pending = {}
        last_of_stream = {}
        for i, (eng, fn, r, w, dma) in enumerate(self.ops):
            while i >= next_bar:
                bar_deps = list(last_of_stream.values())
                bar_pending = {}
                next_bar = next(bar_iter)
            st = stream[i]
            if bar_deps and st not in bar_pending:
                for j in bar_deps:
                    if j != i:
                        deps[i].add(j)
                bar_pending[st] = True
            for k in r:
                if k in last_w:
                    deps[i].add(last_w[k])
            for k in w:
                if k in last_w:
                    deps[i].add(last_w[k])
                for j in readers.get(k, ()):
                    deps[i].add(j)
            for k in r:
                readers.setdefault(k, []).append(i)
            for k in w:
                last_w[k] = i
                readers[k] = []
            last_of_stream[st] = i

        def needs_edge(i, j):
            return not (stream[i] == stream[j] == ("e", "tensor"))

        need_inc = set()
        for j in range(n):
            for i in deps[j]:
                if needs_edge(i, j):
                    need_inc.add(i)
        for i in range(n):
            if stream[i][0] == "d":
                need_inc.add(i)

        NSLOT = 4
        rank, cnt = {}, {}
        dslot, duse = {}, {}
        for i in range(n):
            st = stream[i]
            if st[0] == "d":
                k = op_pos[i]
                dslot[i] = k % NSLOT
                duse[i] = k // NSLOT
            elif i in need_inc:
                cnt[st] = cnt.get(st, 0) + 1
                rank[i] = cnt[st]

        per_eng = {e: [] for e in ENGINES}
        for i in range(n):
            per_eng[self.ops[i][0]].append(i)

        import contextlib
        with contextlib.ExitStack() as stack:
            esem = {e: stack.enter_context(nc.semaphore(f"S_{e}")) for e in ENGINES}
            dsem = {e: [stack.enter_context(nc.semaphore(f"D_{e}_{s}")) for s in range(NSLOT)]
                    for e in ENGINES}
            block = stack.enter_context(nc.Block())

            def sem_of(st, i):
                if st[0] == "d":
                    return dsem[st[1]][dslot[i]], 16 * (duse[i] + 1)
                return esem[st[1]], rank[i]

            def make_body(eng):
                idxs = per_eng[eng]

                def body(E):
                    water = {}
                    for j in idxs:
                        _, fn, r, w, dma = self.ops[j]
                        waits = {}
                        for i in deps[j]:
                            if i not in need_inc or not needs_edge(i, j):
                                continue
                            sem, v = sem_of(stream[i], i)
                            if waits.get(sem.name, (None, 0))[1] < v:
                                waits[sem.name] = (sem, v)
                        if dma:
                            sem = dsem[eng][dslot[j]]
                            v = 16 * duse[j]
                            if v > 0 and waits.get(sem.name, (None, 0))[1] < v:
                                waits[sem.name] = (sem, v)
                        for name in sorted(waits):
                            sem, v = waits[name]
                            if water.get(name, 0) >= v:
                                continue
                            E.wait_ge(sem, v)
                            water[name] = v
                        inst = fn(E)
                        if j in need_inc:
                            st = stream[j]
                            if st[0] == "d":
                                inst.then_inc(dsem[st[1]][dslot[j]], 16)
                            else:
                                inst.then_inc(esem[st[1]], 1)
                return body

            for eng in ENGINES:
                getattr(block, eng)(make_body(eng))



def build_nc(core, stage="full", taps=False):
    nc = bass.Bass("TRN2", target_bir_lowering=False)
    R = Rec()

    def din(name, shape):
        return nc.dram_tensor(name, shape, F32, kind="ExternalInput").ap()

    xrows = din("xrows", [128, 16 * C])
    wqkv = din("wqkv", [128, 8 * 5120])
    qkvb = din("qkvb", [128, 40])
    i128bf = din("i128bf", [128, 128])
    maskTi = din("maskTi", [128, 128])
    maskTx = din("maskTx", [128, 128])
    blk2 = din("blk2", [128, 2])
    ones1 = din("ones1", [1, 128])
    cvals = din("cvals", [128, 8])
    onescol = din("onescol", [128, 1])
    grp4 = din("grp4", [128, 4])
    sel8 = din("sel8", [8, 8 * 128])
    ropesc = din("ropesc", [128, 4 * 128])
    wqp = din("wqp", [128, 8 * C])
    wkva = din("wkva", [128, 8 * 160])
    wln1c = din("wln1c", [128, 8])
    wln2c = din("wln2c", [128, 8])
    wkvbka = din("wkvbka", [128, 128])
    wkvbtok = din("wkvbtok", [128, 66])
    idx8 = nc.dram_tensor("idx8", [128, 8], mybir.dt.int32, kind="ExternalInput").ap()
    woe = din("woe", [128, 4 * C])
    wgt = din("wgt", [128, 8 * 8])
    ebias = din("ebias", [8, 1])
    cq = din("cq", [128, 2048])
    sq_t = din("sq_t", [128, 2048])
    ck = din("ck", [32, 2048])
    sk_t = din("sk_t", [32, 2048])
    wexp = din("wexp", [128, 9 * 12288])

    o_x = nc.dram_tensor("o_x", [128, 16 * C], F32, kind="ExternalOutput").ap()
    xscr = nc.dram_tensor("xscr", [128, 16 * C], F32).ap()
    xascr = nc.dram_tensor("xascr", [128, 16 * C], F32).ap()
    ccin = nc.dram_tensor("ccin", [160, 2048], BF16).ap()
    ccout = nc.dram_tensor("ccout", [8 * 160, 2048], BF16, addr_space="Shared").ap()
    qgin = nc.dram_tensor("qgin", [128, 16384], BF16).ap()
    qgout = nc.dram_tensor("qgout", [1024, 16384], BF16, addr_space="Shared").ap()
    ygin = nc.dram_tensor("ygin", [128, 8192], BF16).ap()
    ygout = nc.dram_tensor("ygout", [1024, 8192], BF16, addr_space="Shared").ap()

    ARENA = 206000
    _ar = nc.bump_sbuf(ARENA)
    ARENA_BASE = _ar[0]
    alloc_at = nc.alloc_sbuf_tensor_at
    cur = [0]

    def place(name, shape, dtype, at=None):
        nbytes = int(np.prod(shape[1:])) * (4 if dtype in (F32, mybir.dt.int32) else 2)
        if at is None:
            at = cur[0]
            cur[0] = at + ((nbytes + 63) // 64) * 64
        return alloc_at(name, shape, dtype, offset=ARENA_BASE + at)

    i128b = place("i128b", [128, 128], BF16)
    i128f = place("i128f", [128, 128], F32)
    mTi = place("mTi", [128, 128], BF16)
    mTx = place("mTx", [128, 128], BF16)
    blk2_s = place("blk2_s", [128, 2], BF16)
    ones1_s = place("ones1_s", [1, 128], BF16)
    cvals_s = place("cvals_s", [128, 8], F32)
    onescol_s = place("onescol_s", [128, 1], BF16)
    grp4_s = place("grp4_s", [128, 4], BF16)
    sel8_s = place("sel8_s", [8, 8 * 128], BF16)
    ropesc_s = place("ropesc_s", [128, 4 * 128], BF16)
    stats = place("stats", [128, 12], F32)
    mv = place("mv", [128, 2], F32)
    lnt = place("lnt", [128, 4], F32)
    lnres = place("lnres", [128, C], F32)
    recb = place("recb", [128, 4], F32)
    mstab = place("mstab", [128, 8], F32)
    mstabb = place("mstabb", [1, 16], BF16)
    negM = place("negM", [128, 1], F32)
    negMB = place("negMB", [128, 16], F32)
    qkmax2 = place("qkmax2", [128, 24], F32)
    sqb = place("sqb", [128, BLK], BF16)
    ln2st = place("ln2st", [128, 32], F32)
    Pb = [place(f"Pb{i}", [128, 1536], BF16) for i in range(2)]
    xu = place("xu", [128, 4 * C], F32)
    qn16 = place("qn16", [128, 16], F32)
    kn16 = place("kn16", [128, 16], F32)
    gtop = place("gtop", [128, 8], F32)
    gsc = place("gsc", [128, 32], F32)
    combf = place("combf", [8, 4 * BLK], BF16)
    combz = place("combz", [128, 2], F32)
    wgt_s = place("wgt_s", [128, 8 * 8], BF16)
    ebias_s = place("ebias_s", [8, 1], F32)
    wln2c_s = place("wln2c_s", [128, 8], F32)

    for ci, cv in enumerate(CONST_VALS):
        nc.const_aps.register(F32, float(cv), cvals_s[:, ci:ci + 1])

    PERSIST_END = cur[0]
    assert PERSIST_END < 45000, PERSIST_END

    cur[0] = PERSIST_END
    wqkv_s = place("wqkv_s", [128, 8 * 5120], BF16)
    qkv = place("qkv", [128, 40 * BLK], BF16)
    x1T = place("x1T", [128, 8 * BLK], BF16)
    x1u = place("x1u", [128, 4 * C], BF16)
    vtok = place("vtok", [128, 4 * 16 * 65], BF16)
    attnA = place("attnA", [128, 4 * C], F32)
    qkvb_s = place("qkvb_s", [128, 40], F32)
    accs = place("accs", [128, BLK], F32)
    assert cur[0] <= ARENA, cur[0]

    cur[0] = PERSIST_END
    latent = place("latent", [128, 16384], BF16)
    krope4 = place("krope4", [128, 4096], BF16)
    wkvbka_s = place("wkvbka_s", [128, 128], BF16)
    wkvbtok_s = place("wkvbtok_s", [128, 66], BF16)
    wo_s = place("wo_s", [128, 4 * C], BF16)
    wln1c_s = place("wln1c_s", [128, 8], F32)
    idx8_s = place("idx8_s", [128, 8], mybir.dt.int32)
    BSH_END = cur[0]

    cur[0] = BSH_END
    x2nT = place("x2nT", [128, 8 * 2048], BF16)
    q_fm = place("q_fm", [128, 8 * 2048], BF16)
    wq_s = place("wq_s", [128, 8 * C], BF16)
    wkva_s = place("wkva_s", [128, 8 * 160], BF16)
    cqs = place("cqs", [128, 2048], BF16)
    sqs = place("sqs", [128, 2048], BF16)
    qrs = place("qrs", [128, 2 * 2048], BF16)
    CKVR_AT = cur[0]
    ckvr = place("ckvr", [128, 2048], BF16)
    kropeo_t = place("kropeo_t", [32, 2048], BF16)
    cks_t = place("cks_t", [32, 2048], BF16)
    sks_t = place("sks_t", [32, 2048], BF16)
    kropeo = kropeo_t[0:32, :]
    cks = cks_t[0:32, :]
    sks = sks_t[0:32, :]
    x2f = place("x2f", [128, C], F32, at=CKVR_AT)
    assert cur[0] <= ARENA, cur[0]

    cur[0] = BSH_END
    q_all = place("q_all", [128, 16384], BF16)
    KA0A = cur[0]
    ka_own = place("ka_own", [128, 2 * 16 * 512], BF16)
    ytmp = place("ytmp", [128, 8192], BF16, at=KA0A)
    kvt_own = place("kvt_own", [128, 2 * 16 * 264], BF16)
    ytok = place("ytok", [128, 128 * 64], BF16)
    YTOK0 = cur[0] - 16384
    acc2 = place("accs2", [128, 1024], F32)
    yfm = place("yfm", [128, 4 * BLK], BF16)
    xab = place("xab", [128, C], F32)
    qpad_he = place("qpad_he", [128, 2 * 512], BF16)
    qpad_ho = place("qpad_ho", [128, 2 * 512], BF16)
    yfull = place("yfull", [128, 16 * BLK], BF16, at=YTOK0)
    assert cur[0] <= ARENA, cur[0]

    cur[0] = PERSIST_END
    xanT = place("xanT", [128, 8 * 2048], BF16)
    moeacc = place("moeacc", [128, 16 * C], F32)
    wg_s = [place(f"wg_s{i}", [128, 8 * BLK], BF16) for i in range(2)]
    wu_s = [place(f"wu_s{i}", [128, 8 * BLK], BF16) for i in range(2)]
    wd_s = [place(f"wd_s{i}", [128, 4 * C], BF16) for i in range(2)]
    mid_s = place("mid_s", [128, 4 * BLK], BF16)
    silu_s = place("silu_s", [128, BLK], BF16)
    gatef = place("gatef", [8, 4 * BLK], F32)
    xau = place("xau", [128, C], F32)
    assert cur[0] <= ARENA, cur[0]

    ps = nc.alloc_psum_tensor
    S0 = ps("S0", [128, 1536], F32)
    S1 = ps("S1", [128, 1536], F32)
    Sg = [S0, S1]
    Sbf = [S0[:].bitcast(BF16), S1[:].bitcast(BF16)]
    accp = ps("accp", [128, 512], F32)
    auxp = ps("auxp", [128, 512], F32)

    R.op("gpsimd", lambda E: E.dma_start(out=wqkv_s[:], in_=wqkv[:]), w=["wqkv_s"], dma=True)
    R.op("sync", lambda E: E.dma_start(out=qkvb_s[:], in_=qkvb[:]), w=["qkvb_s"], dma=True)
    R.op("gpsimd", lambda E: E.dma_start(out=i128b[:], in_=i128bf[:]), w=["i128b"], dma=True)
    R.op("sync", lambda E: E.dma_start(out=i128f[:], in_=i128bf[:]), w=["i128f"], dma=True)
    R.op("gpsimd", lambda E: E.dma_start(out=mTi[:], in_=maskTi[:]), w=["mTi"], dma=True)
    R.op("gpsimd", lambda E: E.dma_start(out=mTx[:], in_=maskTx[:]), w=["mTx"], dma=True)
    R.op("gpsimd", lambda E: E.dma_start(out=blk2_s[:], in_=blk2[:]), w=["blk2_s"], dma=True)
    R.op("gpsimd", lambda E: E.dma_start(out=ones1_s[:], in_=ones1[:]), w=["ones1_s"], dma=True)
    R.op("sync", lambda E: E.dma_start(out=cvals_s[:], in_=cvals[:]), w=["cvals_s"], dma=True)
    R.op("gpsimd", lambda E: E.dma_start(out=onescol_s[:], in_=onescol[:]), w=["onescol_s"], dma=True)
    R.op("gpsimd", lambda E: E.dma_start(out=grp4_s[:], in_=grp4[:]), w=["grp4_s"], dma=True)
    R.op("gpsimd", lambda E: E.dma_start(out=sel8_s[:], in_=sel8[:]), w=["sel8_s"], dma=True)
    R.op("gpsimd", lambda E: E.dma_start(out=ropesc_s[:], in_=ropesc[:]), w=["ropesc_s"], dma=True)
    R.op("vector", lambda E: E.memset(mstab[:], 0.0),
         w=["ms0", "ms1", "ms23", "ms45", "ms6", "ms7"])

    def ln_stats(src_keys, src_ap_fn):
        R.op("vector", lambda E: E.bn_stats(out=stats[:, 0:6], in_=src_ap_fn()[:, 0:512]),
             r=src_keys, w=["stats"])
        R.op("vector", lambda E: E.bn_stats(out=stats[:, 6:12], in_=src_ap_fn()[:, 512:1024]),
             r=src_keys, w=["stats2"])
        R.op("vector", lambda E: E.bn_aggr(out=mv[:], in_=stats[:]), r=["stats", "stats2"], w=["mv"])
        R.op("scalar", lambda E: E.activation(out=lnt[:, 2:3], in_=mv[:, 1:2], func=AF.Ln, bias=float(EPS)),
             r=["mv", "cvals_s"], w=["lnt2"])
        R.op("scalar", lambda E: E.activation(out=lnt[:, 0:1], in_=lnt[:, 2:3], func=AF.Exp, scale=-0.5),
             r=["lnt2"], w=["lnt0"])
        R.op("vector", lambda E: E.tensor_scalar(out=lnt[:, 1:2], in0=mv[:, 0:1], scalar1=lnt[:, 0:1],
                                                 scalar2=-1.0, op0=ALU.mult, op1=ALU.mult),
             r=["mv", "lnt0"], w=["lnt1"])

    def rms_inv(src_keys, src_ap_fn):
        R.op("vector", lambda E: E.bn_stats(out=stats[:, 0:6], in_=src_ap_fn()[:, 0:512]),
             r=src_keys, w=["stats"])
        R.op("vector", lambda E: E.bn_stats(out=stats[:, 6:12], in_=src_ap_fn()[:, 512:1024]),
             r=src_keys, w=["stats2"])
        R.op("vector", lambda E: E.bn_aggr(out=mv[:], in_=stats[:]), r=["stats", "stats2"], w=["mv"])
        R.op("vector", lambda E: E.tensor_scalar(out=lnt[:, 3:4], in0=mv[:, 0:1],
                                                 scalar1=mv[:, 0:1], scalar2=None, op0=ALU.mult),
             r=["mv"], w=["lnt3"])
        R.op("vector", lambda E: E.tensor_add(out=lnt[:, 2:3], in0=mv[:, 1:2], in1=lnt[:, 3:4]),
             r=["mv", "lnt3"], w=["lnt2"])
        R.op("scalar", lambda E: E.activation(out=lnt[:, 3:4], in_=lnt[:, 2:3], func=AF.Ln, bias=float(EPS)),
             r=["lnt2", "cvals_s"], w=["lnt3"])
        R.op("scalar", lambda E: E.activation(out=lnt[:, 2:3], in_=lnt[:, 3:4], func=AF.Exp, scale=-0.5),
             r=["lnt3"], w=["lnt2"])

    def emit_flash(specs, scale_const, bias_fn, bias_key, acc_start=True, acc_stop=True):
        banks = []
        bcur, bused = [], 0
        for sp_ in specs:
            if bused + sp_["w"] > 512:
                banks.append(bcur)
                bcur, bused = [], 0
            bcur.append((sp_, bused))
            bused += sp_["w"]
        if bcur:
            banks.append(bcur)
        grps = [banks[i:i + 3] for i in range(0, len(banks), 3)]
        n_av = sum(len(b) for b in banks)
        av_cnt = [0]

        def emit_qk(gi):
            slot = gi % 2
            Sp = Sg[slot]
            skey = f"S{slot}"
            for bi, bank in enumerate(grps[gi]):
                for (sp_, lc) in bank:
                    col = bi * 512 + lc
                    R.op("tensor", lambda E, sp_=sp_, col=col, Sp=Sp: sp_["qk"](E, Sp, col),
                         r=sp_["qkr"], w=[skey])

        def emit_exp(gi):
            slot = gi % 2
            Sp, Pp = Sg[slot], Pb[slot]
            skey, pkey = f"S{slot}", f"P{slot}"
            runs = []
            for bi, bank in enumerate(grps[gi]):
                for (sp_, lc) in bank:
                    col = bi * 512 + lc
                    if runs and runs[-1][1] == col:
                        runs[-1][1] = col + sp_["w"]
                    else:
                        runs.append([col, col + sp_["w"]])
            for (c0, c1) in runs:
                R.op("scalar", lambda E, c0=c0, c1=c1, Sp=Sp, Pp=Pp: E.activation(
                    out=Pp[:, c0:c1], in_=Sp[:, c0:c1], func=AF.Exp,
                    scale=float(scale_const), bias=bias_fn()),
                    r=[skey, bias_key], w=[pkey])

        def emit_mask(gi):
            slot = gi % 2
            Pp = Pb[slot]
            pkey = f"P{slot}"
            for bi, bank in enumerate(grps[gi]):
                for (sp_, lc) in bank:
                    if sp_["mask"] is None:
                        continue
                    col = bi * 512 + lc
                    R.op("vector", lambda E, sp_=sp_, col=col, Pp=Pp: E.tensor_mul(
                        out=Pp[:, col:col + 128], in0=Pp[:, col:col + 128],
                        in1=sp_["mask"]()),
                        r=[pkey] + sp_["maskr"], w=[pkey])

        def emit_av(gi):
            slot = gi % 2
            Pp = Pb[slot]
            pkey = f"P{slot}"
            for bi, bank in enumerate(grps[gi]):
                for (sp_, lc) in bank:
                    col = bi * 512 + lc
                    av_cnt[0] += 1
                    R.op("tensor", lambda E, sp_=sp_, col=col, Pp=Pp,
                         st=(av_cnt[0] == 1 and acc_start),
                         sp2=(av_cnt[0] == n_av and acc_stop): sp_["av"](E, Pp, col, st, sp2),
                         r=sp_["avr"] + [pkey], w=["accp"])

        n = len(grps)
        emit_qk(0)
        emit_exp(0)
        emit_mask(0)
        for gi in range(1, n):
            emit_qk(gi)
            emit_exp(gi)
            emit_mask(gi)
            emit_av(gi - 1)
        emit_av(n - 1)

    tilesA = [(0, gp, 512) for gp in range(16)]
    tilesA += [(2, gp, 256) for gp in range(16)]
    for i in range(16):
        tilesA += [(1, i, 384), (3, i, 128)]

    def kslice(gp):
        j = gp // 2
        return (16 + j) if gp % 2 == 0 else (24 + j)

    rot1 = [(i + 1) % 32 for i in range(32)]
    rot2 = [(i + 2) % 32 for i in range(32)]
    rot_sw = []
    for i in range(16):
        rot_sw += [2 * i + 1, 2 * i]

    def emit_unit_stageA(u):
        R.op("sync", lambda E, u=u: E.dma_start(out=xu[:], in_=xrows[:, 4 * u * C: 4 * (u + 1) * C]),
             w=[f"xu:{t}" for t in range(4)], dma=True)

        for t in range(4):
            ln_stats([f"xu:{t}"], lambda t=t: xu[:, t * C:(t + 1) * C])
            R.op("vector", lambda E, t=t: E.tensor_scalar(
                out=x1u[:, t * C:(t + 1) * C], in0=xu[:, t * C:(t + 1) * C],
                scalar1=lnt[:, 0:1], scalar2=lnt[:, 1:2], op0=ALU.mult, op1=ALU.add),
                r=[f"xu:{t}", "lnt0", "lnt1"], w=[f"x1u:{t}"])

        for t in range(4):
            for cch in range(8):
                slot = (t * 8 + cch) % 2
                bv = Sbf[slot]
                R.op("tensor", lambda E, t=t, c=cch, bv=bv: E.transpose(
                    out=bv[:, 0:128], in_=x1u[:, t * C + c * 128: t * C + (c + 1) * 128],
                    identity=i128b[:]),
                    r=[f"x1u:{t}", "i128b"], w=[f"S{slot}"])
                R.op("vector", lambda E, t=t, c=cch, bv=bv: E.tensor_copy(
                    out=x1T[:, c * BLK + t * 128: c * BLK + (t + 1) * 128], in_=bv[:, 0:128]),
                    r=[f"S{slot}"], w=[f"x1T:{cch}"])

        for oc in range(40):
            slot = oc % 2
            Sp = Sg[slot]
            for kc in range(8):
                R.op("tensor", lambda E, oc=oc, kc=kc, Sp=Sp: E.matmul(
                    out=Sp[:, 0:512],
                    lhsT=wqkv_s[:, kc * 5120 + oc * 128: kc * 5120 + (oc + 1) * 128],
                    rhs=x1T[:, kc * BLK:(kc + 1) * BLK],
                    start=(kc == 0), stop=(kc == 7)),
                    r=["wqkv_s", f"x1T:{kc}"], w=[f"S{slot}"])
            R.op("vector", lambda E, oc=oc, Sp=Sp: E.tensor_scalar(
                out=qkv[:, oc * BLK:(oc + 1) * BLK], in0=Sp[:, 0:512],
                scalar1=qkvb_s[:, oc:oc + 1], scalar2=None, op0=ALU.add),
                r=[f"S{slot}", "qkvb_s"], w=[f"qkv:{oc}"])

        R.op("vector", lambda E: E.memset(
            vtok[:].rearrange("p (tg c) -> p tg c", c=65)[:, :, 64:65], 1.0),
            w=["vtok_ones"])
        for t in range(4):
            for i in range(8):
                oc = 32 + i
                slot = (t * 8 + i) % 2
                bv = Sbf[slot]
                base = t * 1040
                R.op("tensor", lambda E, t=t, oc=oc, bv=bv: E.transpose(
                    out=bv[:, 0:128],
                    in_=qkv[:, oc * BLK + t * 128: oc * BLK + (t + 1) * 128],
                    identity=i128b[:]),
                    r=[f"qkv:{oc}", "i128b"], w=[f"S{slot}"])
                R.op("vector", lambda E, i=i, bv=bv, base=base: E.tensor_copy(
                    out=vtok[:, base + 65 * (2 * i): base + 65 * (2 * i) + 64],
                    in_=bv[:, 0:64]),
                    r=[f"S{slot}"], w=[f"vtok:{base // 1040}:{2*i}"])
                R.op("vector", lambda E, i=i, bv=bv, base=base: E.tensor_copy(
                    out=vtok[:, base + 65 * (2 * i + 1): base + 65 * (2 * i + 1) + 64],
                    in_=bv[:, 64:128]),
                    r=[f"S{slot}"], w=[f"vtok:{base // 1040}:{2*i+1}"])

        for ch in range(24):
            R.op("vector", lambda E, ch=ch: E.tensor_mul(
                out=sqb[:], in0=qkv[:, ch * BLK:(ch + 1) * BLK], in1=qkv[:, ch * BLK:(ch + 1) * BLK]),
                r=[f"qkv:{ch}"], w=["sqb"])
            R.op("tensor", lambda E: E.matmul(
                out=auxp[0:2, 0:512], lhsT=blk2_s[:], rhs=sqb[:], start=True, stop=True),
                r=["blk2_s", "sqb"], w=["auxp"])
            R.op("vector", lambda E, ch=ch: E.tensor_reduce(
                out=qkmax2[0:2, ch:ch + 1], in_=auxp[0:2, 0:512], axis=AX.X, op=ALU.max),
                r=["auxp"], w=[f"qkmax2:{ch}"])
        R.op("vector", lambda E: E.tensor_reduce(
            out=mstab[0:2, 0:1], in_=qkmax2[0:2, 0:16], axis=AX.X, op=ALU.max),
            r=[f"qkmax2:{ch}" for ch in range(16)], w=["ms0"])
        R.op("vector", lambda E: E.tensor_reduce(
            out=mstab[0:2, 1:2], in_=qkmax2[0:2, 16:24], axis=AX.X, op=ALU.max),
            r=[f"qkmax2:{ch}" for ch in range(16, 24)], w=["ms1"])
        R.op("vector", lambda E: E.stream_shuffle(out=mstab[0:32, 2:4], in_=mstab[0:32, 0:2], mask=rot1),
             r=["ms0", "ms1"], w=["ms23"])
        R.op("vector", lambda E: E.tensor_max(out=mstab[0:1, 4:6], in0=mstab[0:1, 0:2], in1=mstab[0:1, 2:4]),
             r=["ms0", "ms1", "ms23"], w=["ms45"])
        R.op("vector", lambda E: E.tensor_mul(out=mstab[0:1, 6:7], in0=mstab[0:1, 4:5], in1=mstab[0:1, 5:6]),
             r=["ms45"], w=["ms6"])
        R.op("scalar", lambda E: E.activation(out=mstab[0:1, 7:8], in_=mstab[0:1, 6:7], func=AF.Ln, bias=1e-30),
             r=["ms6", "cvals_s"], w=["ms7"])
        R.op("scalar", lambda E: E.activation(out=mstab[0:1, 6:7], in_=mstab[0:1, 7:8], func=AF.Exp,
                                              scale=0.5, bias=float(np.log(QK_SCALE_A))),
             r=["ms7", "cvals_s"], w=["ms6"])
        R.op("vector", lambda E: E.tensor_scalar(out=mstabb[:, 0:1], in0=mstab[0:1, 6:7], scalar1=-1.0,
                                                 scalar2=None, op0=ALU.mult),
             r=["ms6"], w=["mstabb"])
        R.op("tensor", lambda E: E.matmul(out=auxp[:, 0:1], lhsT=ones1_s[:], rhs=mstabb[:, 0:1],
                                          start=True, stop=True),
             r=["ones1_s", "mstabb"], w=["auxp"])
        R.op("vector", lambda E: E.tensor_copy(out=negM[:], in_=auxp[:, 0:1]),
             r=["auxp"], w=["negM"])

        for g in range(16):
            specs = []
            for (m, gp, w) in tilesA:
                kch = kslice(gp)
                mT, mk = (mTi, "mTi") if gp <= g else (mTx, "mTx")

                def qk(E, Sp, col, m=m, w=w, kch=kch, g=g):
                    return E.matmul(
                        out=Sp[:, col:col + w],
                        lhsT=qkv[:, kch * BLK + m * 128: kch * BLK + (m + 1) * 128],
                        rhs=qkv[:, g * BLK + m * 128: g * BLK + 512],
                        start=True, stop=True)

                def av(E, Pp, col, st, sp2, m=m, gp=gp, w=w):
                    return E.matmul(
                        out=accp[0:65, m * 128: m * 128 + w],
                        lhsT=vtok[:, m * 1040 + 65 * gp: m * 1040 + 65 * gp + 65],
                        rhs=Pp[:, col:col + w], start=st, stop=sp2)

                specs.append(dict(w=w, qk=qk, qkr=[f"qkv:{kch}", f"qkv:{g}"],
                                  mask=(lambda mT=mT: mT[:]), maskr=[mk],
                                  av=av, avr=[f"vtok:{m}:{gp}", "vtok_ones"]))
            emit_flash(specs, QK_SCALE_A, lambda: negM[:], "negM")

            R.op("vector", lambda E: E.tensor_copy(out=accs[0:65, :], in_=accp[0:65, :]),
                 r=["accp"], w=["accs"])
            for t in range(4):
                R.op("tensor", lambda E, t=t: E.transpose(
                    out=auxp[:, t * 65: t * 65 + 65], in_=accs[0:65, t * 128:(t + 1) * 128],
                    identity=i128f[0:65, 0:65]),
                    r=["accs", "i128f"], w=["auxp"])
            for t in range(4):
                R.op("vector", lambda E, t=t: E.reciprocal(
                    out=recb[:, t:t + 1], in_=auxp[:, t * 65 + 64: t * 65 + 65]),
                    r=["auxp"], w=[f"recb:{t}"])
                R.op("vector", lambda E, t=t, g=g: E.tensor_scalar(
                    out=attnA[:, t * C + g * 64: t * C + g * 64 + 64],
                    in0=auxp[:, t * 65: t * 65 + 64],
                    scalar1=recb[:, t:t + 1], scalar2=None, op0=ALU.mult),
                    r=["auxp", f"recb:{t}"], w=[f"attnA:{t}:{g}"])

        for t in range(4):
            aks = [f"attnA:{t}:{g}" for g in range(16)]
            ln_stats(aks, lambda t=t: attnA[:, t * C:(t + 1) * C])
            R.op("vector", lambda E, t=t: E.tensor_scalar(
                out=lnres[:], in0=attnA[:, t * C:(t + 1) * C],
                scalar1=lnt[:, 0:1], scalar2=lnt[:, 1:2], op0=ALU.mult, op1=ALU.add),
                r=aks + ["lnt0", "lnt1"], w=["lnres"])
            R.op("vector", lambda E, t=t: E.tensor_add(
                out=xu[:, t * C:(t + 1) * C], in0=xu[:, t * C:(t + 1) * C], in1=lnres[:]),
                r=["lnres", f"xu:{t}"], w=[f"xu:{t}"])
        dst = o_x if stage == "A" else xscr
        R.op("sync", lambda E, u=u, dst=dst: E.dma_start(
            out=dst[:, 4 * u * C: 4 * (u + 1) * C], in_=xu[:]),
            r=[f"xu:{t}" for t in range(4)], w=[f"xscr:{u}"], dma=True)

    for u in range(4):
        emit_unit_stageA(u)

    if stage == "A":
        R.op("gpsimd", lambda E: E.nop(), r=[f"xscr:{u}" for u in range(4)])
        R.emit(nc)
        return nc

    R.barrier()

    R.op("gpsimd", lambda E: E.dma_start(out=wq_s[:], in_=wqp[:]), w=["wq_s"], dma=True)
    R.op("gpsimd", lambda E: E.dma_start(out=wkva_s[:], in_=wkva[:]), w=["wkva_s"], dma=True)
    R.op("sync", lambda E: E.dma_start(out=wln1c_s[:], in_=wln1c[:]), w=["wln1c_s"], dma=True)
    R.op("sync", lambda E: E.dma_start(out=wln2c_s[:], in_=wln2c[:]), w=["wln2c_s"], dma=True)
    R.op("gpsimd", lambda E: E.dma_start(out=wkvbka_s[:], in_=wkvbka[:]), w=["wkvbka_s"], dma=True)
    R.op("gpsimd", lambda E: E.dma_start(out=wkvbtok_s[:], in_=wkvbtok[:]), w=["wkvbtok_s"], dma=True)
    R.op("sync", lambda E: E.dma_start(out=idx8_s[:], in_=idx8[:]), w=["idx8_s"], dma=True)
    R.op("gpsimd", lambda E: E.dma_start(out=wo_s[:], in_=woe[:]), w=["wo_s"], dma=True)
    R.op("gpsimd", lambda E: E.dma_start(out=wgt_s[:], in_=wgt[:]), w=["wgt_raw"], dma=True)
    R.op("sync", lambda E: E.dma_start(out=ebias_s[:], in_=ebias[:]), w=["ebias_s"], dma=True)
    R.op("gpsimd", lambda E: E.dma_start(out=cqs[:], in_=cq[:]), w=["cqs"], dma=True)
    R.op("gpsimd", lambda E: E.dma_start(out=sqs[:], in_=sq_t[:]), w=["sqs"], dma=True)
    R.op("gpsimd", lambda E: E.dma_start(out=cks, in_=ck[:]), w=["cks"], dma=True)
    R.op("gpsimd", lambda E: E.dma_start(out=sks, in_=sk_t[:]), w=["sks"], dma=True)

    for kc in range(8):
        R.op("vector", lambda E, kc=kc: E.tensor_scalar(
            out=wq_s[:, kc * C:(kc + 1) * C], in0=wq_s[:, kc * C:(kc + 1) * C],
            scalar1=wln1c_s[:, kc:kc + 1], scalar2=None, op0=ALU.mult),
            r=["wq_s", "wln1c_s"], w=["wq_s"])
        R.op("vector", lambda E, kc=kc: E.tensor_scalar(
            out=wkva_s[:, kc * 160:(kc + 1) * 160], in0=wkva_s[:, kc * 160:(kc + 1) * 160],
            scalar1=wln1c_s[:, kc:kc + 1], scalar2=None, op0=ALU.mult),
            r=["wkva_s", "wln1c_s"], w=["wkva_s"])
        R.op("vector", lambda E, kc=kc: E.tensor_scalar(
            out=wgt_s[:, kc * 8:(kc + 1) * 8], in0=wgt_s[:, kc * 8:(kc + 1) * 8],
            scalar1=wln2c_s[:, kc:kc + 1], scalar2=None, op0=ALU.mult),
            r=["wgt_raw", "wln2c_s"], w=["wgt_s"])

    for u in range(4):
        R.op("sync", lambda E, u=u: E.dma_start(out=xu[:], in_=xscr[:, 4 * u * C: 4 * (u + 1) * C]),
             r=[f"xscr:{u}"], w=[f"xu:{t}" for t in range(4)], dma=True)
        for t in range(4):
            tt = 4 * u + t
            ln_stats([f"xu:{t}"], lambda t=t: xu[:, t * C:(t + 1) * C])
            R.op("vector", lambda E, tt=tt: E.tensor_copy(
                out=ln2st[:, 2 * tt: 2 * tt + 2], in_=lnt[:, 0:2]),
                r=["lnt0", "lnt1"], w=[f"ln2st:{tt}"])
            R.op("vector", lambda E, t=t: E.tensor_scalar(
                out=x2f[:], in0=xu[:, t * C:(t + 1) * C],
                scalar1=lnt[:, 0:1], scalar2=lnt[:, 1:2], op0=ALU.mult, op1=ALU.add),
                r=[f"xu:{t}", "lnt0", "lnt1"], w=["x2f"])
            rms_inv(["x2f"], lambda: x2f)
            R.op("vector", lambda E: E.tensor_scalar(
                out=sqb[:, 0:512], in0=x2f[:, 0:512], scalar1=lnt[:, 2:3],
                scalar2=None, op0=ALU.mult),
                r=["x2f", "lnt2"], w=["sqb"])
            R.op("vector", lambda E: E.tensor_scalar(
                out=qrs[:, 0:512], in0=x2f[:, 512:1024], scalar1=lnt[:, 2:3],
                scalar2=None, op0=ALU.mult),
                r=["x2f", "lnt2"], w=["qrs0"])
            for cch in range(8):
                slot = cch % 2
                bv = Sbf[slot]
                if cch < 4:
                    src_fn = (lambda c=cch: sqb[:, c * 128:(c + 1) * 128])
                    skey2 = "sqb"
                else:
                    src_fn = (lambda c=cch: qrs[:, (c - 4) * 128:(c - 4) * 128 + 128])
                    skey2 = "qrs0"
                R.op("tensor", lambda E, src_fn=src_fn, bv=bv: E.transpose(
                    out=bv[:, 0:128], in_=src_fn(), identity=i128b[:]),
                    r=[skey2, "i128b"], w=[f"S{slot}"])
                R.op("vector", lambda E, c=cch, bv=bv, tt=tt: E.tensor_copy(
                    out=x2nT[:, c * 2048 + tt * 128: c * 2048 + (tt + 1) * 128], in_=bv[:, 0:128]),
                    r=[f"S{slot}"], w=[f"x2nT:{cch}"])

    for tt4 in range(4):
        slot = tt4 % 2
        Sp = Sg[slot]
        for kc in range(8):
            R.op("tensor", lambda E, tt4=tt4, kc=kc, Sp=Sp: E.matmul(
                out=Sp[:, 0:512], lhsT=wkva_s[:, kc * 160: kc * 160 + 128],
                rhs=x2nT[:, kc * 2048 + tt4 * 512: kc * 2048 + (tt4 + 1) * 512],
                start=(kc == 0), stop=(kc == 7)),
                r=["wkva_s", f"x2nT:{kc}"], w=[f"S{slot}"])
        R.op("vector", lambda E, tt4=tt4, Sp=Sp: E.tensor_copy(
            out=ckvr[:, tt4 * 512:(tt4 + 1) * 512], in_=Sp[:, 0:512]),
            r=[f"S{slot}"], w=["ckvr"])
        for kc in range(8):
            R.op("tensor", lambda E, tt4=tt4, kc=kc, Sp=Sp: E.matmul(
                out=Sp[0:32, 512:1024], lhsT=wkva_s[:, kc * 160 + 128: kc * 160 + 160],
                rhs=x2nT[:, kc * 2048 + tt4 * 512: kc * 2048 + (tt4 + 1) * 512],
                start=(kc == 0), stop=(kc == 7)),
                r=["wkva_s", f"x2nT:{kc}"], w=[f"S{slot}"])
        R.op("vector", lambda E, tt4=tt4, Sp=Sp: E.tensor_copy(
            out=kropeo[:, tt4 * 512:(tt4 + 1) * 512], in_=Sp[0:32, 512:1024]),
            r=[f"S{slot}"], w=["kropeo"])

    R.op("vector", lambda E: E.stream_shuffle(out=qrs[0:32, 0:2048], in_=kropeo, mask=rot_sw),
         r=["kropeo"], w=["qrs0"])
    R.op("vector", lambda E: E.tensor_mul(out=qrs[0:32, 2048:4096], in0=kropeo, in1=cks),
         r=["kropeo", "cks"], w=["qrs1"])
    R.op("vector", lambda E: E.tensor_mul(out=qrs[0:32, 0:2048], in0=qrs[0:32, 0:2048], in1=sks),
         r=["qrs0", "sks"], w=["qrs0"])
    R.op("vector", lambda E: E.tensor_add(out=kropeo, in0=qrs[0:32, 2048:4096], in1=qrs[0:32, 0:2048]),
         r=["qrs0", "qrs1"], w=["kropeo"])

    R.op("sync", lambda E: E.dma_start(out=ccin[0:128, :], in_=ckvr[:]), r=["ckvr"], w=["ccin"], dma=True)
    R.op("sync", lambda E: E.dma_start(out=ccin[128:160, :], in_=kropeo), r=["kropeo"], w=["ccin2"], dma=True)
    R.op("gpsimd", lambda E: E.collective_compute(
        "AllGather", ALU.bypass, replica_groups=[list(range(8))],
        ins=[ccin[:]], outs=[ccout[:]]),
        r=["ccin", "ccin2"], w=["ccout"])

    for oc in range(8):
        for tt4 in range(4):
            slot = (oc * 4 + tt4) % 2
            Sp = Sg[slot]
            for kc in range(8):
                R.op("tensor", lambda E, oc=oc, tt4=tt4, kc=kc, Sp=Sp: E.matmul(
                    out=Sp[:, 0:512], lhsT=wq_s[:, kc * C + oc * 128: kc * C + (oc + 1) * 128],
                    rhs=x2nT[:, kc * 2048 + tt4 * 512: kc * 2048 + (tt4 + 1) * 512],
                    start=(kc == 0), stop=(kc == 7)),
                    r=["wq_s", f"x2nT:{kc}"], w=[f"S{slot}"])
            R.op("vector", lambda E, oc=oc, tt4=tt4, Sp=Sp: E.tensor_copy(
                out=q_fm[:, oc * 2048 + tt4 * 512: oc * 2048 + (tt4 + 1) * 512], in_=Sp[:, 0:512]),
                r=[f"S{slot}"], w=[f"q_fm:{oc}"])

    for oc in range(8):
        R.op("vector", lambda E, oc=oc: E.stream_shuffle(
            out=qrs[:, 0:2048], in_=q_fm[:, oc * 2048:(oc + 1) * 2048], mask=rot_sw),
            r=[f"q_fm:{oc}"], w=["qrs0"])
        R.op("vector", lambda E, oc=oc: E.tensor_mul(
            out=qrs[:, 2048:4096], in0=q_fm[:, oc * 2048:(oc + 1) * 2048], in1=cqs[:]),
            r=[f"q_fm:{oc}", "cqs"], w=["qrs1"])
        R.op("vector", lambda E: E.tensor_mul(out=qrs[:, 0:2048], in0=qrs[:, 0:2048], in1=sqs[:]),
             r=["qrs0", "sqs"], w=["qrs0"])
        R.op("vector", lambda E, oc=oc: E.tensor_add(
            out=q_fm[:, oc * 2048:(oc + 1) * 2048], in0=qrs[:, 2048:4096], in1=qrs[:, 0:2048]),
            r=["qrs0", "qrs1"], w=[f"q_fm:{oc}"])

    R.op("sync", lambda E: E.dma_start(out=qgin[:], in_=q_fm[:]),
         r=[f"q_fm:{oc}" for oc in range(8)], w=["qgin"], dma=True)
    R.op("gpsimd", lambda E: E.collective_compute(
        "AllGather", ALU.bypass, replica_groups=[list(range(8))],
        ins=[qgin[:]], outs=[qgout[:]]),
        r=["qgin"], w=["qgout"])

    R.op("vector", lambda E: E.tensor_mul(
        out=sqb[:, 0:128], in0=wkvbka_s[:], in1=wkvbka_s[:]),
        r=["wkvbka_s"], w=["sqb"])
    R.op("tensor", lambda E: E.matmul(out=auxp[0:1, 0:128], lhsT=onescol_s[:], rhs=sqb[:, 0:128],
                                      start=True, stop=True),
         r=["onescol_s", "sqb"], w=["auxp"])
    R.op("vector", lambda E: E.tensor_reduce(
        out=kn16[0:1, 0:1], in_=auxp[0:1, 0:32], axis=AX.X, op=ALU.add),
        r=["auxp"], w=["kn16:0"])
    R.op("vector", lambda E: E.tensor_reduce(
        out=kn16[0:1, 1:2], in_=auxp[0:1, 96:128], axis=AX.X, op=ALU.add),
        r=["auxp"], w=["kn16:1"])

    for r_ in range(8):
        R.op("sync", lambda E, r_=r_: E.dma_start(
            out=latent[:, r_ * 2048:(r_ + 1) * 2048], in_=ccout[160 * r_:160 * r_ + 128, :]),
            r=["ccout"], w=[f"latent:{r_}"], dma=True)
        R.op("sync", lambda E, r_=r_: E.dma_start(
            out=krope4[32 * (r_ // 2):32 * (r_ // 2) + 32,
                       (r_ % 2) * 2048:(r_ % 2) * 2048 + 2048],
            in_=ccout[160 * r_ + 128:160 * (r_ + 1), :]),
            r=["ccout"], w=[f"krope4:{r_}"], dma=True)

    R.op("vector", lambda E: E.memset(mstab[0:32, 0:2], 0.0), w=["ms0", "ms1"])
    for chk in range(32):
        R.op("vector", lambda E, chk=chk: E.tensor_mul(
            out=sqb[:], in0=latent[:, chk * 512:(chk + 1) * 512], in1=latent[:, chk * 512:(chk + 1) * 512]),
            r=[f"latent:{chk // 4}"], w=["sqb"])
        R.op("tensor", lambda E: E.matmul(out=auxp[0:1, 0:512], lhsT=onescol_s[:], rhs=sqb[:],
                                          start=True, stop=True),
             r=["onescol_s", "sqb"], w=["auxp"])
        R.op("vector", lambda E: E.tensor_reduce(out=mstab[0:1, 2:3], in_=auxp[0:1, 0:512],
                                                 axis=AX.X, op=ALU.max),
             r=["auxp"], w=["ms23"])
        R.op("vector", lambda E: E.tensor_max(out=mstab[0:1, 0:1], in0=mstab[0:1, 0:1], in1=mstab[0:1, 2:3]),
             r=["ms0", "ms23"], w=["ms0"])
    for chk in range(8):
        R.op("vector", lambda E, chk=chk: E.tensor_mul(
            out=sqb[:], in0=krope4[:, chk * 512:(chk + 1) * 512], in1=krope4[:, chk * 512:(chk + 1) * 512]),
            r=[f"krope4:{r}" for r in range(8)], w=["sqb"])
        R.op("tensor", lambda E: E.matmul(out=auxp[0:4, 0:512], lhsT=grp4_s[:], rhs=sqb[:],
                                          start=True, stop=True),
             r=["grp4_s", "sqb"], w=["auxp"])
        R.op("vector", lambda E: E.tensor_reduce(out=mstab[0:4, 2:3], in_=auxp[0:4, 0:512],
                                                 axis=AX.X, op=ALU.max),
             r=["auxp"], w=["ms23"])
        R.op("vector", lambda E: E.tensor_max(out=mstab[0:4, 1:2], in0=mstab[0:4, 1:2], in1=mstab[0:4, 2:3]),
             r=["ms1", "ms23"], w=["ms1"])
    R.op("vector", lambda E: E.stream_shuffle(out=mstab[0:32, 2:3], in_=mstab[0:32, 1:2], mask=rot2),
         r=["ms1"], w=["ms23"])
    R.op("vector", lambda E: E.tensor_max(out=mstab[0:32, 1:2], in0=mstab[0:32, 1:2], in1=mstab[0:32, 2:3]),
         r=["ms1", "ms23"], w=["ms1"])
    R.op("vector", lambda E: E.stream_shuffle(out=mstab[0:32, 2:3], in_=mstab[0:32, 1:2], mask=rot1),
         r=["ms1"], w=["ms23"])
    R.op("vector", lambda E: E.tensor_max(out=mstab[0:1, 1:2], in0=mstab[0:1, 1:2], in1=mstab[0:1, 2:3]),
         r=["ms1", "ms23"], w=["ms1"])

    R.barrier()

    qgv = qgout.rearrange("(s p) (c t) -> (s p c) t", p=128, t=2048)
    for s in range(8):
        R.op("gpsimd", lambda E, s=s: E.indirect_dma_start(
            out=q_all[:, s * 2048:(s + 1) * 2048], out_offset=None,
            in_=qgv[:, :],
            in_offset=IndirectOffsetOnAxis(ap=idx8_s[:, s:s + 1], axis=0)),
            r=["qgout", "idx8_s"], w=[f"q_all:{s}"], dma=True)

    R.op("vector", lambda E: E.memset(qkmax2[0:32, 0:4], 0.0), w=["qm2", "qm1", "qm3"])
    for chk in range(32):
        R.op("vector", lambda E, chk=chk: E.tensor_mul(
            out=sqb[:], in0=q_all[:, chk * 512:(chk + 1) * 512], in1=q_all[:, chk * 512:(chk + 1) * 512]),
            r=[f"q_all:{chk // 4}"], w=["sqb"])
        R.op("tensor", lambda E: E.matmul(
            out=auxp[0:2, 0:512], lhsT=blk2_s[:], rhs=sqb[:], start=True, stop=True),
            r=["blk2_s", "sqb"], w=["auxp"])
        R.op("vector", lambda E: E.tensor_reduce(
            out=qkmax2[0:2, 1:2], in_=auxp[0:2, 0:512], axis=AX.X, op=ALU.max),
            r=["auxp"], w=["qm1"])
        R.op("vector", lambda E: E.tensor_max(out=qkmax2[0:2, 0:1], in0=qkmax2[0:2, 0:1],
                                              in1=qkmax2[0:2, 1:2]),
             r=["qm1", "qm2"], w=["qm2"])
    R.op("vector", lambda E: E.tensor_scalar(
        out=kn16[0:1, 0:2], in0=kn16[0:1, 0:2], scalar1=mstab[0:1, 0:1], scalar2=mstab[0:1, 1:2],
        op0=ALU.mult, op1=ALU.add),
        r=["kn16:0", "kn16:1", "ms0", "ms1"], w=["kn16f"])
    R.op("vector", lambda E: E.tensor_copy(out=qn16[0:1, 0:1], in_=qkmax2[0:1, 0:1]),
         r=["qm2"], w=["qn16a"])
    R.op("vector", lambda E: E.stream_shuffle(out=qkmax2[0:32, 2:3], in_=qkmax2[0:32, 0:1], mask=rot1),
         r=["qm2"], w=["qm3"])
    R.op("vector", lambda E: E.tensor_copy(out=qn16[0:1, 1:2], in_=qkmax2[0:1, 2:3]),
         r=["qm3"], w=["qn16b"])
    R.op("vector", lambda E: E.tensor_mul(out=qn16[0:1, 0:2], in0=qn16[0:1, 0:2], in1=kn16[0:1, 0:2]),
         r=["qn16a", "qn16b", "kn16f"], w=["qn16f"])
    R.op("scalar", lambda E: E.activation(out=kn16[0:1, 0:2], in_=qn16[0:1, 0:2], func=AF.Ln, bias=1e-30),
         r=["qn16f", "cvals_s"], w=["kn16f2"])
    R.op("scalar", lambda E: E.activation(out=qn16[0:1, 0:2], in_=kn16[0:1, 0:2], func=AF.Exp,
                                          scale=0.5, bias=float(np.log(QK_SCALE_B))),
         r=["kn16f2", "cvals_s"], w=["qn16f2"])
    R.op("vector", lambda E: E.tensor_scalar(out=mstabb[:, 0:2], in0=qn16[0:1, 0:2], scalar1=-1.0,
                                             scalar2=None, op0=ALU.mult),
         r=["qn16f2"], w=["mstabb"])
    R.op("tensor", lambda E: E.matmul(out=auxp[:, 0:2], lhsT=ones1_s[:], rhs=mstabb[:, 0:2],
                                      start=True, stop=True),
         r=["ones1_s", "mstabb"], w=["auxp"])
    R.op("vector", lambda E: E.tensor_copy(out=negMB[:, 0:2], in_=auxp[:, 0:2]),
         r=["auxp"], w=["negMB"])

    def block_col(b, kb):
        r_ = min(kb, 15 - kb)
        qslot = 2 * b + (0 if kb < 8 else 1)
        return 2048 * r_ + 512 * qslot, r_

    for b in (0, 1):
        for kb in range(16):
            col0, r_ = block_col(b, kb)
            g4 = col0 // 4096
            kcol = col0 % 4096
            bi = b * 16 + kb
            slot = bi % 2
            Sp = Sg[slot]
            R.op("tensor", lambda E, col0=col0, Sp=Sp: E.matmul(
                out=Sp[:, 0:512], lhsT=wkvbka_s[:],
                rhs=latent[:, col0:col0 + 512], start=True, stop=False),
                r=["wkvbka_s", f"latent:{col0 // 2048}"], w=[f"S{slot}"])
            rb = 32 * g4 if g4 < 2 else 64
            rk = 32 if g4 < 2 else 64
            R.op("tensor", lambda E, g4=g4, kcol=kcol, Sp=Sp, rb=rb, rk=rk: E.matmul(
                out=Sp[:, 0:512],
                lhsT=ropesc_s[rb:rb + rk, g4 * 128:(g4 + 1) * 128],
                rhs=krope4[rb:rb + rk, kcol:kcol + 512],
                start=False, stop=True),
                r=["ropesc_s", f"krope4:{2*g4}", f"krope4:{2*g4+1}"], w=[f"S{slot}"])
            R.op("vector", lambda E, bi=bi, Sp=Sp: E.tensor_copy(
                out=ka_own[:, bi * 512:(bi + 1) * 512], in_=Sp[:, 0:512]),
                r=[f"S{slot}"], w=[f"ka:{bi}"])
            for tau in range(4):
                R.op("tensor", lambda E, tau=tau, col0=col0, Sp=Sp: E.matmul(
                    out=Sp[:, 512:578], lhsT=latent[:, col0 + tau * 128: col0 + (tau + 1) * 128],
                    rhs=wkvbtok_s[:], start=True, stop=True),
                    r=["wkvbtok_s", f"latent:{col0 // 2048}"], w=[f"S{slot}"])
                R.op("vector", lambda E, bi=bi, tau=tau, Sp=Sp: E.tensor_copy(
                    out=kvt_own[:, bi * 264 + tau * 66: bi * 264 + (tau + 1) * 66],
                    in_=Sp[:, 512:578]),
                    r=[f"S{slot}"], w=[f"kvt:{bi}"])
            R.op("vector", lambda E, bi=bi: E.memset(
                kvt_own[:, bi * 264:(bi + 1) * 264].rearrange("p (x c) -> p x c", c=33)[:, :, 32:33], 1.0),
                r=[f"kvt:{bi}"], w=[f"kvto:{bi}"])

    R.op("vector", lambda E: E.memset(qpad_he[64:128, :], 0.0), w=["qpadzh"])
    R.op("vector", lambda E: E.memset(qpad_ho[0:64, :], 0.0), w=["qpadzl"])
    for b in (0, 1):
        for jq in range(16):
            qcol0, _ = block_col(b, jq)
            par = (b * 16 + jq) % 2
            R.op("vector", lambda E, qcol0=qcol0, par=par: E.tensor_copy(
                out=qpad_he[0:64, par * 512:(par + 1) * 512],
                in_=q_all[0:64, qcol0:qcol0 + 512]),
                r=[f"q_all:{qcol0 // 2048}"], w=[f"qpe:{par}"])
            R.op("vector", lambda E, qcol0=qcol0, par=par: E.tensor_copy(
                out=qpad_ho[64:128, par * 512:(par + 1) * 512],
                in_=q_all[64:128, qcol0:qcol0 + 512]),
                r=[f"q_all:{qcol0 // 2048}"], w=[f"qpo:{par}"])
            for h2 in range(2):
                hb = 64 * h2
                qp = qpad_he if h2 == 0 else qpad_ho
                qpk = (f"qpe:{par}", "qpadzh") if h2 == 0 else (f"qpo:{par}", "qpadzl")
                specs = []
                for kb in range(jq + 1):
                    bi = b * 16 + kb
                    diag = (kb == jq)
                    for tau in range(4):
                        w = 512 - tau * 128 if diag else 512
                        scol = tau * 128 if diag else 0

                        def qk(E, Sp, col, bi=bi, tau=tau, w=w, scol=scol,
                               qp=qp, par=par):
                            return E.matmul(
                                out=Sp[:, col:col + w],
                                lhsT=ka_own[:, bi * 512 + tau * 128: bi * 512 + (tau + 1) * 128],
                                rhs=qp[:, par * 512 + scol: par * 512 + 512],
                                start=True, stop=True)

                        if diag:
                            mask, maskr = (lambda: mTi[:]), ["mTi"]
                        else:
                            mask, maskr = None, []

                        def av(E, Pp, col, st, sp2, bi=bi, tau=tau, w=w, scol=scol, hb=hb, h2=h2):
                            return E.matmul(
                                out=accp[hb:hb + 33, scol: scol + w],
                                lhsT=kvt_own[:, bi * 264 + tau * 66 + 33 * h2:
                                             bi * 264 + tau * 66 + 33 * h2 + 33],
                                rhs=Pp[:, col:col + w], start=st, stop=sp2)

                        specs.append(dict(
                            w=w, qk=qk, qkr=[f"ka:{bi}", qpk[0], qpk[1]],
                            mask=mask, maskr=maskr, av=av,
                            avr=[f"kvt:{bi}", f"kvto:{bi}"]))
                emit_flash(specs, QK_SCALE_B, (lambda h2=h2: negMB[:, h2:h2 + 1]), "negMB")
            R.op("vector", lambda E: E.tensor_copy(out=acc2[0:33, 0:512], in_=accp[0:33, :]),
                 r=["accp"], w=["acc2a"])
            R.op("vector", lambda E: E.tensor_copy(out=acc2[64:97, 512:1024], in_=accp[64:97, :]),
                 r=["accp"], w=["acc2b"])
            for h2 in range(2):
                hb = 64 * h2
                akey = "acc2a" if h2 == 0 else "acc2b"
                for t in range(4):
                    R.op("tensor", lambda E, h2=h2, hb=hb, t=t: E.transpose(
                        out=auxp[:, t * 33: t * 33 + 33],
                        in_=acc2[hb:hb + 33, h2 * 512 + t * 128: h2 * 512 + (t + 1) * 128],
                        identity=i128f[hb:hb + 33, hb:hb + 33]),
                        r=[akey, "i128f"], w=["auxp"])
                for t in range(4):
                    gt = qcol0 // 128 + t
                    R.op("vector", lambda E, t=t: E.reciprocal(
                        out=recb[:, t:t + 1], in_=auxp[:, t * 33 + 32: t * 33 + 33]),
                        r=["auxp"], w=[f"recb:{t}"])
                    R.op("vector", lambda E, t=t, gt=gt, h2=h2: E.tensor_scalar(
                        out=ytok[:, gt * 64 + 32 * h2: gt * 64 + 32 * h2 + 32],
                        in0=auxp[:, t * 33: t * 33 + 32],
                        scalar1=recb[:, t:t + 1], scalar2=None, op0=ALU.mult),
                        r=["auxp", f"recb:{t}"], w=[f"ytok:{gt}:{h2}"])

    R.op("sync", lambda E: E.dma_start(out=ygin[:], in_=ytok[:]),
         r=[f"ytok:{gt}:{h2}" for gt in range(128) for h2 in range(2)], w=["ygin"], dma=True)
    R.op("gpsimd", lambda E: E.collective_compute(
        "AllGather", ALU.bypass, replica_groups=[list(range(8))],
        ins=[ygin[:]], outs=[ygout[:]]),
        r=["ygin"], w=["ygout"])
    R.barrier()
    ygv = ygout.rearrange("(s p) (c t) -> (s p c) t", p=128, t=1024)
    for s in range(8):
        R.op("gpsimd", lambda E, s=s: E.indirect_dma_start(
            out=ytmp[:, s * 1024:(s + 1) * 1024],
            out_offset=None,
            in_=ygv[:, :],
            in_offset=IndirectOffsetOnAxis(ap=idx8_s[:, s:s + 1], axis=0)),
            r=["ygout", "idx8_s"], w=[f"ytmp:{s}"], dma=True)
    for s in range(8):
        R.op("vector", lambda E, s=s: E.tensor_copy(
            out=yfull[:].rearrange("p (t c) -> p t c", c=512)[:, :, 64 * s: 64 * s + 64],
            in_=ytmp[:, s * 1024:(s + 1) * 1024].rearrange("p (t d) -> p t d", d=64)),
            r=[f"ytmp:{s}"], w=[f"yfull:{s}"], )

    if stage == "tap_yf":
        o_yf = nc.dram_tensor("o_yf", [128, 8192], F32, kind="ExternalOutput").ap()
        R.op("gpsimd", lambda E: E.dma_start(out=o_yf[:], in_=yfull[:]),
             r=[f"yfull:{s}" for s in range(8)], w=["oyf"], dma=True)

    for u in range(4):
        R.op("sync", lambda E, u=u: E.dma_start(out=xu[:], in_=xscr[:, 4 * u * C: 4 * (u + 1) * C]),
             r=[f"xscr:{u}"], w=[f"xbl:{t}" for t in range(4)], dma=True)
        for t in range(4):
            tt = 4 * u + t
            for yc in range(4):
                slot = (t * 4 + yc) % 2
                bv = Sbf[slot]
                R.op("tensor", lambda E, tt=tt, yc=yc, bv=bv: E.transpose(
                    out=bv[:, 0:128], in_=yfull[:, tt * 512 + yc * 128: tt * 512 + (yc + 1) * 128],
                    identity=i128b[:]),
                    r=[f"yfull:{s}" for s in range(8)] + ["i128b"],
                    w=[f"S{slot}"])
                R.op("vector", lambda E, t=t, yc=yc, bv=bv: E.tensor_copy(
                    out=yfm[:, yc * 512 + t * 128: yc * 512 + (t + 1) * 128], in_=bv[:, 0:128]),
                    r=[f"S{slot}"], w=[f"yfm:{yc}"])
        for t in range(4):
            tt = 4 * u + t
            for half in range(2):
                slot = half
                Sp = Sg[slot]
                for yc in range(4):
                    R.op("tensor", lambda E, t=t, half=half, yc=yc, Sp=Sp: E.matmul(
                        out=Sp[:, 0:512],
                        lhsT=yfm[:, yc * 512 + t * 128: yc * 512 + (t + 1) * 128],
                        rhs=wo_s[:, yc * C + half * 512: yc * C + (half + 1) * 512],
                        start=(yc == 0), stop=(yc == 3)),
                        r=[f"yfm:{yc}", "wo_s"], w=[f"S{slot}"])
                R.op("vector", lambda E, t=t, half=half, tt=tt: E.tensor_scalar(
                    out=lnres[:, 0:512],
                    in0=xu[:, t * C + half * 512: t * C + (half + 1) * 512],
                    scalar1=ln2st[:, 2 * tt: 2 * tt + 1], scalar2=ln2st[:, 2 * tt + 1: 2 * tt + 2],
                    op0=ALU.mult, op1=ALU.add),
                    r=[f"xbl:{t}", f"ln2st:{tt}"], w=["lnres"])
                R.op("vector", lambda E, half=half, Sp=Sp: E.tensor_add(
                    out=xab[:, half * 512:(half + 1) * 512],
                    in0=lnres[:, 0:512], in1=Sp[:, 0:512]),
                    r=["lnres", f"S{slot}"], w=[f"xab:{half}"])
            R.op("sync", lambda E, tt=tt: E.dma_start(
                out=xascr[:, tt * C:(tt + 1) * C], in_=xab[:]),
                r=["xab:0", "xab:1"], w=[f"xascr:{tt}"], dma=True)

    R.barrier()

    for u in range(4):
        R.op("sync", lambda E, u=u: E.dma_start(out=xu[:], in_=xascr[:, 4 * u * C: 4 * (u + 1) * C]),
             r=[f"xascr:{4*u+t}" for t in range(4)], w=[f"xu:{t}" for t in range(4)], dma=True)
        for t in range(4):
            tt = 4 * u + t
            rms_inv([f"xu:{t}"], lambda t=t: xu[:, t * C:(t + 1) * C])
            R.op("vector", lambda E, t=t: E.tensor_scalar(
                out=sqb[:, 0:512], in0=xu[:, t * C: t * C + 512], scalar1=lnt[:, 2:3],
                scalar2=None, op0=ALU.mult),
                r=[f"xu:{t}", "lnt2"], w=["sqb"])
            R.op("vector", lambda E, t=t: E.tensor_scalar(
                out=silu_s[:, 0:512], in0=xu[:, t * C + 512:(t + 1) * C], scalar1=lnt[:, 2:3],
                scalar2=None, op0=ALU.mult),
                r=[f"xu:{t}", "lnt2"], w=["silu_s"])
            for cch in range(8):
                slot = cch % 2
                bv = Sbf[slot]
                if cch < 4:
                    src_fn = (lambda c=cch: sqb[:, c * 128:(c + 1) * 128])
                    skey2 = "sqb"
                else:
                    src_fn = (lambda c=cch: silu_s[:, (c - 4) * 128:(c - 4) * 128 + 128])
                    skey2 = "silu_s"
                R.op("tensor", lambda E, src_fn=src_fn, bv=bv: E.transpose(
                    out=bv[:, 0:128], in_=src_fn(), identity=i128b[:]),
                    r=[skey2, "i128b"], w=[f"S{slot}"])
                R.op("vector", lambda E, c=cch, bv=bv, tt=tt: E.tensor_copy(
                    out=xanT[:, c * 2048 + tt * 128: c * 2048 + (tt + 1) * 128], in_=bv[:, 0:128]),
                    r=[f"S{slot}"], w=[f"xanT:{cch}"])

    for G in range(4):
        slot = G % 2
        Sp = Sg[slot]
        for kc in range(8):
            R.op("tensor", lambda E, G=G, kc=kc, Sp=Sp: E.matmul(
                out=Sp[0:8, 0:512], lhsT=wgt_s[:, kc * 8:(kc + 1) * 8],
                rhs=xanT[:, kc * 2048 + G * 512: kc * 2048 + (G + 1) * 512],
                start=(kc == 0), stop=(kc == 7)),
                r=["wgt_s", f"xanT:{kc}"], w=[f"S{slot}"])
        R.op("vector", lambda E, G=G, Sp=Sp: E.tensor_scalar(
            out=gatef[0:8, G * 512:(G + 1) * 512], in0=Sp[0:8, 0:512],
            scalar1=ebias_s[0:8, 0:1], scalar2=None, op0=ALU.add),
            r=[f"S{slot}", "ebias_s"], w=[f"gatef:{G}"])
        for st4 in range(4):
            slot2 = st4 % 2
            R.op("tensor", lambda E, G=G, st4=st4, slot2=slot2: E.transpose(
                out=Sg[slot2][:, 0:8],
                in_=gatef[0:8, G * 512 + st4 * 128: G * 512 + (st4 + 1) * 128],
                identity=i128f[0:8, 0:8]),
                r=[f"gatef:{G}", "i128f"], w=[f"S{slot2}"])
            R.op("vector", lambda E, slot2=slot2: E.tensor_copy(out=gtop[:], in_=Sg[slot2][:, 0:8]),
                 r=[f"S{slot2}"], w=["glog"])
            R.op("vector", lambda E: E.max(out=gsc[:, 0:8], in_=gtop[:]),
                 r=["glog"], w=["gm"])
            R.op("vector", lambda E: E.tensor_scalar(out=gsc[:, 8:9], in0=gsc[:, 0:1], scalar1=-1.0,
                                                     scalar2=None, op0=ALU.mult),
                 r=["gm"], w=["gnm"])
            R.op("scalar", lambda E: E.activation(out=gsc[:, 16:24], in_=gtop[:], func=AF.Exp,
                                                  bias=gsc[:, 8:9]),
                 r=["glog", "gnm"], w=["ge"])
            R.op("vector", lambda E: E.tensor_scalar(out=gsc[:, 24:32], in0=gtop[:], scalar1=gsc[:, 1:2],
                                                     scalar2=None, op0=ALU.is_ge),
                 r=["glog", "gm"], w=["gsel"])
            R.op("vector", lambda E: E.tensor_mul(out=gsc[:, 16:24], in0=gsc[:, 16:24], in1=gsc[:, 24:32]),
                 r=["ge", "gsel"], w=["ge"])
            R.op("vector", lambda E: E.tensor_reduce(out=combz[:, 0:1], in_=gsc[:, 16:24],
                                                     axis=AX.X, op=ALU.add),
                 r=["ge"], w=["gz"])
            R.op("vector", lambda E: E.reciprocal(out=combz[:, 1:2], in_=combz[:, 0:1]),
                 r=["gz"], w=["gzr"])
            R.op("vector", lambda E: E.tensor_scalar(out=gtop[:], in0=gsc[:, 16:24], scalar1=combz[:, 1:2],
                                                     scalar2=None, op0=ALU.mult),
                 r=["ge", "gzr"], w=["glog"])
            slot3 = (st4 + 1) % 2
            R.op("tensor", lambda E, slot3=slot3: E.transpose(
                out=Sg[slot3][0:8, 0:128], in_=gtop[:, 0:8], identity=i128f[:, 0:128]),
                r=["glog", "i128f"], w=[f"S{slot3}"])
            R.op("vector", lambda E, G=G, st4=st4, slot3=slot3: E.tensor_copy(
                out=combf[0:8, G * 512 + st4 * 128: G * 512 + (st4 + 1) * 128],
                in_=Sg[slot3][0:8, 0:128]),
                r=[f"S{slot3}"], w=[f"combf:{G}"])

    R.barrier()
    for e in range(9):
        sl = e % 2
        wgk, wuk, wdk = f"wg{sl}", f"wu{sl}", f"wd{sl}"
        off = e * 12288
        R.op("gpsimd", lambda E, off=off, sl=sl: E.dma_start(
            out=wg_s[sl][:], in_=wexp[:, off: off + 4096]), w=[wgk], dma=True)
        R.op("gpsimd", lambda E, off=off, sl=sl: E.dma_start(
            out=wu_s[sl][:], in_=wexp[:, off + 4096: off + 8192]), w=[wuk], dma=True)
        R.op("gpsimd", lambda E, off=off, sl=sl: E.dma_start(
            out=wd_s[sl][:], in_=wexp[:, off + 8192: off + 12288]), w=[wdk], dma=True)
        for kc in range(8):
            R.op("vector", lambda E, kc=kc, sl=sl: E.tensor_scalar(
                out=wg_s[sl][:, kc * 512:(kc + 1) * 512], in0=wg_s[sl][:, kc * 512:(kc + 1) * 512],
                scalar1=wln2c_s[:, kc:kc + 1], scalar2=None, op0=ALU.mult),
                r=[wgk, "wln2c_s"], w=[wgk])
            R.op("vector", lambda E, kc=kc, sl=sl: E.tensor_scalar(
                out=wu_s[sl][:, kc * 512:(kc + 1) * 512], in0=wu_s[sl][:, kc * 512:(kc + 1) * 512],
                scalar1=wln2c_s[:, kc:kc + 1], scalar2=None, op0=ALU.mult),
                r=[wuk, "wln2c_s"], w=[wuk])
        for G in range(4):
            if e > 0:
                R.op("tensor", lambda E, e=e, G=G: E.matmul(
                    out=S0[:, 512:1024], lhsT=sel8_s[:, (e - 1) * 128: e * 128],
                    rhs=combf[0:8, G * 512:(G + 1) * 512], start=True, stop=True),
                    r=["sel8_s", f"combf:{G}"], w=["m_bc"])
            for ic in range(4):
                par = ic % 2
                gcol = 0 if par == 0 else 1024
                gk, uk2 = f"m_g{par}", f"m_u{par}"
                sqk = f"sqm{par}"
                for kc in range(8):
                    R.op("tensor", lambda E, G=G, ic=ic, kc=kc, sl=sl, gcol=gcol: E.matmul(
                        out=S0[:, gcol:gcol + 512], lhsT=wg_s[sl][:, kc * 512 + ic * 128: kc * 512 + (ic + 1) * 128],
                        rhs=xanT[:, kc * 2048 + G * 512: kc * 2048 + (G + 1) * 512],
                        start=(kc == 0), stop=(kc == 7)),
                        r=[wgk, f"xanT:{kc}"], w=[gk])
                    R.op("tensor", lambda E, G=G, ic=ic, kc=kc, sl=sl, gcol=gcol: E.matmul(
                        out=S1[:, gcol:gcol + 512], lhsT=wu_s[sl][:, kc * 512 + ic * 128: kc * 512 + (ic + 1) * 128],
                        rhs=xanT[:, kc * 2048 + G * 512: kc * 2048 + (G + 1) * 512],
                        start=(kc == 0), stop=(kc == 7)),
                        r=[wuk, f"xanT:{kc}"], w=[uk2])
                R.op("scalar", lambda E, gcol=gcol: E.activation(
                    out=silu_s[:], in_=S0[:, gcol:gcol + 512], func=AF.Silu),
                     r=[gk], w=["silu_s"])
                R.op("vector", lambda E, gcol=gcol, par=par: E.scalar_tensor_tensor(
                    out=sqb[:] if par == 0 else lnres[:, 0:512].bitcast(BF16)[:, 0:512],
                    in0=S1[:, gcol:gcol + 512], scalar=1.0, in1=silu_s[:],
                    op0=ALU.mult, op1=ALU.mult),
                    r=[uk2, "silu_s"], w=[sqk])
                mm_in = (lambda par=par: sqb[:] if par == 0 else lnres[:, 0:512].bitcast(BF16)[:, 0:512])
                if e > 0:
                    R.op("vector", lambda E, ic=ic, mm_in=mm_in: E.tensor_mul(
                        out=mid_s[:, ic * 512:(ic + 1) * 512], in0=mm_in(), in1=S0[:, 512:1024]),
                        r=[sqk, "m_bc"], w=[f"mid:{ic}"])
                else:
                    R.op("vector", lambda E, ic=ic, mm_in=mm_in: E.tensor_copy(
                        out=mid_s[:, ic * 512:(ic + 1) * 512], in_=mm_in()),
                        r=[sqk], w=[f"mid:{ic}"])
            for tsub in range(4):
                for half in range(2):
                    k8 = tsub * 2 + half
                    dps = accp if k8 % 2 == 0 else auxp
                    skey3 = "accp" if k8 % 2 == 0 else "auxp"
                    for ic in range(4):
                        R.op("tensor", lambda E, tsub=tsub, half=half, ic=ic, sl=sl,
                             dps=dps: E.matmul(
                            out=dps[:, 0:512],
                            lhsT=mid_s[:, ic * 512 + tsub * 128: ic * 512 + (tsub + 1) * 128],
                            rhs=wd_s[sl][:, ic * C + half * 512: ic * C + (half + 1) * 512],
                            start=(ic == 0), stop=(ic == 3)),
                            r=[f"mid:{ic}", wdk], w=[skey3])
                    tt = G * 4 + tsub
                    mkey = f"moe:{tt}:{half}"
                    if e == 0:
                        R.op("vector", lambda E, tt=tt, half=half, dps=dps: E.tensor_copy(
                            out=moeacc[:, tt * C + half * 512: tt * C + (half + 1) * 512],
                            in_=dps[:, 0:512]),
                            r=[skey3], w=[mkey])
                    else:
                        R.op("vector", lambda E, tt=tt, half=half, dps=dps: E.tensor_add(
                            out=moeacc[:, tt * C + half * 512: tt * C + (half + 1) * 512],
                            in0=moeacc[:, tt * C + half * 512: tt * C + (half + 1) * 512],
                            in1=dps[:, 0:512]),
                            r=[skey3, mkey], w=[mkey])
            if e == 8:
                u = G
                R.op("sync", lambda E, u=u: E.dma_start(
                    out=xu[:], in_=xscr[:, 4 * u * C: 4 * (u + 1) * C]),
                    w=[f"xu:{t}" for t in range(4)], dma=True)
                for t in range(4):
                    tt = 4 * u + t
                    R.op("sync", lambda E, tt=tt: E.dma_start(
                        out=xau[:], in_=xascr[:, tt * C:(tt + 1) * C]),
                        r=[f"xascr:{tt}"], w=["xaf"], dma=True)
                    R.op("vector", lambda E, tt=tt: E.tensor_add(
                        out=xau[:], in0=xau[:], in1=moeacc[:, tt * C:(tt + 1) * C]),
                        r=["xaf", f"moe:{tt}:0", f"moe:{tt}:1"], w=["xaf"])
                    ln_stats(["xaf"], lambda: xau[:])
                    R.op("vector", lambda E: E.tensor_scalar(
                        out=lnres[:], in0=xau[:, 0:C],
                        scalar1=lnt[:, 0:1], scalar2=lnt[:, 1:2], op0=ALU.mult, op1=ALU.add),
                        r=["xaf", "lnt0", "lnt1"], w=["lnres"])
                    R.op("vector", lambda E, t=t: E.tensor_add(
                        out=xu[:, t * C:(t + 1) * C], in0=xu[:, t * C:(t + 1) * C], in1=lnres[:]),
                        r=["lnres", f"xu:{t}"], w=[f"xu:{t}"])
                R.op("sync", lambda E, u=u: E.dma_start(
                    out=o_x[:, 4 * u * C: 4 * (u + 1) * C], in_=xu[:]),
                    r=[f"xu:{t}" for t in range(4)], w=[f"oxd:{u}"], dma=True)

    if taps:
        o_moe = nc.dram_tensor("o_moe", [128, 16 * C], F32, kind="ExternalOutput").ap()
        R.op("gpsimd", lambda E: E.dma_start(out=o_moe[:], in_=moeacc[:]),
             r=[f"moe:{tt}:{h}" for tt in range(16) for h in range(2)], w=["omoe"], dma=True)
        o_xan = nc.dram_tensor("o_xan", [128, 16384], F32, kind="ExternalOutput").ap()
        R.op("gpsimd", lambda E: E.dma_start(out=o_xan[:], in_=xanT[:]),
             r=[f"xanT:{cch}" for cch in range(8)], w=["oxan"], dma=True)
        o_comb = nc.dram_tensor("o_comb", [8, 2048], F32, kind="ExternalOutput").ap()
        R.op("gpsimd", lambda E: E.dma_start(out=o_comb[:], in_=combf[:]),
             r=[f"combf:{G}" for G in range(4)], w=["ocomb"], dma=True)
        o_xs = nc.dram_tensor("o_xs", [128, 16 * C], F32, kind="ExternalOutput").ap()
        o_xa = nc.dram_tensor("o_xa", [128, 16 * C], F32, kind="ExternalOutput").ap()
        o_q = nc.dram_tensor("o_q", [128, 16384], F32, kind="ExternalOutput").ap()
        o_qf = nc.dram_tensor("o_qf", [128, 16384], F32, kind="ExternalOutput").ap()
        R.op("gpsimd", lambda E: E.dma_start(out=o_qf[:], in_=qgin[:]),
             r=["qgin"], w=["oqf"], dma=True)
        o_cc = nc.dram_tensor("o_cc", [1280, 2048], F32, kind="ExternalOutput").ap()
        R.op("gpsimd", lambda E: E.dma_start(out=o_cc[:], in_=ccout[:]),
             r=["ccout"], w=["occ"], dma=True)
        o_yg = nc.dram_tensor("o_yg", [1024, 8192], F32, kind="ExternalOutput").ap()
        R.op("gpsimd", lambda E: E.dma_start(out=o_yg[:], in_=ygout[:]),
             r=["ygout"], w=["oyg"], dma=True)
        R.op("sync", lambda E: E.dma_start(out=o_xs[:], in_=xscr[:]),
             r=[f"xscr:{u}" for u in range(4)], w=["oxs"], dma=True)
        R.op("sync", lambda E: E.dma_start(out=o_xa[:], in_=xascr[:]),
             r=[f"xascr:{tt}" for tt in range(16)], w=["oxa"], dma=True)
        R.op("gpsimd", lambda E: E.dma_start(out=o_q[:], in_=q_all[:]),
             r=[f"q_all:{s}" for s in range(8)], w=["oq"], dma=True)
        R.op("gpsimd", lambda E: E.nop(), r=["oxs", "oxa", "oq", "oqf", "occ", "oyg", "omoe", "oxan", "ocomb"])
    R.op("gpsimd", lambda E: E.nop(), r=[f"oxd:{u}" for u in range(4)])

    R.emit(nc)
    return nc



def _build_wqkv(W, bias):
    Wp = np.zeros((1024, 5120), dtype=np.float32)
    bp = np.zeros(5120, dtype=np.float32)

    def put(dst, src64):
        Wp[:, dst:dst + 64] = W[:, src64:src64 + 64]
        bp[dst:dst + 64] = bias[src64:src64 + 64]

    for j in range(16):
        put(j * 128, 192 * j)
    for j in range(8):
        put(2048 + j * 128, 192 * (2 * j) + 64)
        put(2048 + j * 128 + 64, 192 * (2 * j + 1) + 64)
        put(3072 + j * 128, 192 * (2 * j + 1) + 64)
        put(3072 + j * 128 + 64, 192 * (2 * j) + 64)
        put(4096 + j * 128, 192 * (2 * j) + 128)
        put(4096 + j * 128 + 64, 192 * (2 * j + 1) + 128)
    return Wp, bp


def _chunkmajor(w, nk=8):
    cin, f = w.shape
    assert cin == nk * 128
    return np.ascontiguousarray(w.reshape(nk, 128, f).transpose(1, 0, 2).reshape(128, nk * f))


def _rope_tables(positions, nrows, rope_rows):
    t = np.asarray(positions, dtype=np.float32)
    d = 32
    freqs = (1.0 / THETA ** (np.arange(0, d, 2, dtype=np.float32) / d)).astype(np.float32)
    ang = np.outer(freqs, t)
    cosv, sinv = np.cos(ang).astype(np.float32), np.sin(ang).astype(np.float32)
    Ct = np.ones((nrows, len(t)), dtype=np.float32)
    St = np.zeros((nrows, len(t)), dtype=np.float32)
    for r0 in rope_rows:
        for i in range(16):
            Ct[r0 + 2 * i] = cosv[i]
            Ct[r0 + 2 * i + 1] = cosv[i]
            St[r0 + 2 * i] = -sinv[i]
            St[r0 + 2 * i + 1] = sinv[i]
    return Ct, St


def prep_core_inputs(c, inputs):
    blocks = [c, 15 - c]
    x = np.asarray(inputs["L0_attn_inputs"])
    rows = np.concatenate([x[b, 512 * j:512 * (j + 1)] for b in (0, 1) for j in blocks])
    xrows = np.ascontiguousarray(rows.reshape(16, 128, 1024).transpose(1, 0, 2).reshape(128, 16 * 1024))

    Wp, bp = _build_wqkv(np.asarray(inputs["L0_attn_qkv_kernel"]),
                         np.asarray(inputs["L0_attn_qkv_bias"]).reshape(3072))
    wqkv = _chunkmajor(Wp)
    qkvb = np.ascontiguousarray(bp.reshape(40, 128).T)

    i128 = np.eye(128, dtype=np.float32)
    maskTi = np.triu(np.ones((128, 128), dtype=np.float32))
    maskTx = np.triu(np.ones((128, 128), dtype=np.float32), k=1)
    blk2 = np.zeros((128, 2), dtype=np.float32)
    blk2[0:64, 0] = 1.0
    blk2[64:128, 1] = 1.0
    ones1 = np.ones((1, 128), dtype=np.float32)
    cvals_np = np.zeros((128, 8), dtype=np.float32)
    for ci, cv in enumerate(CONST_VALS):
        cvals_np[:, ci] = np.float32(cv)
    onescol = np.ones((128, 1), dtype=np.float32)
    grp4 = np.zeros((128, 4), dtype=np.float32)
    for g in range(4):
        grp4[32 * g:32 * (g + 1), g] = 1.0
    sel8 = np.zeros((8, 8 * 128), dtype=np.float32)
    for e in range(8):
        sel8[e, e * 128:(e + 1) * 128] = 1.0
    ropesc = np.zeros((128, 4 * 128), dtype=np.float32)
    for g in range(4):
        for r in range(32):
            ropesc[32 * g + r, g * 128 + 32 + r] = 1.0
            ropesc[32 * g + r, g * 128 + 64 + r] = 1.0

    Wq = np.asarray(inputs["L0_ffn_Wq"])
    permq = []
    for h in range(16):
        if h % 2 == 0:
            permq += list(range(64 * h, 64 * h + 64))
        else:
            permq += list(range(64 * h + 32, 64 * h + 64)) + list(range(64 * h, 64 * h + 32))
    wqp = _chunkmajor(Wq[:, permq])
    wkva = _chunkmajor(np.asarray(inputs["L0_ffn_Wkva"]))
    wln1c = np.ascontiguousarray(np.asarray(inputs["L0_ffn_w_ln1"]).reshape(8, 128).T)
    wln2c = np.ascontiguousarray(np.asarray(inputs["L0_ffn_w_ln2"]).reshape(8, 128).T)
    Wkvb = np.asarray(inputs["L0_ffn_Wkvb"])
    wkvbka = np.zeros((128, 128), dtype=np.float32)
    wkvbka[:, 0:32] = Wkvb[:, 32 * (2 * c): 32 * (2 * c) + 32]
    wkvbka[:, 96:128] = Wkvb[:, 32 * (2 * c + 1): 32 * (2 * c + 1) + 32]
    wkvbtok = np.zeros((128, 66), dtype=np.float32)
    wkvbtok[:, 0:32] = Wkvb[:, 32 * (2 * c): 32 * (2 * c) + 32]
    wkvbtok[:, 33:65] = Wkvb[:, 32 * (2 * c + 1): 32 * (2 * c + 1) + 32]
    idx8 = np.zeros((128, 8), dtype=np.int32)
    for s in range(8):
        idx8[:, s] = s * 1024 + np.arange(128) * 8 + c
    Wo = np.asarray(inputs["L0_ffn_Wo"])
    woe_rows = np.concatenate([Wo[64 * h: 64 * h + 32] for h in range(16)])
    woe = _chunkmajor(woe_rows, nk=4)
    wgt = _chunkmajor(np.asarray(inputs["L0_ffn_Wgate"]))
    ebias = np.asarray(inputs["L0_ffn_expert_bias"]).reshape(8, 1).astype(np.float32)

    positions = np.concatenate([np.arange(512 * j, 512 * (j + 1)) for b in (0, 1) for j in blocks])
    cqt, sqt = _rope_tables(positions, 128, [32, 64])
    ckt, skt = _rope_tables(positions, 32, [0])

    Wg = np.asarray(inputs["L0_ffn_Wg"])
    Wu = np.asarray(inputs["L0_ffn_Wu"])
    Wd = np.asarray(inputs["L0_ffn_Wd"])
    parts = []
    for e in range(9):
        if e == 0:
            g_, u_, d_ = (np.asarray(inputs["L0_ffn_sWg"])[0], np.asarray(inputs["L0_ffn_sWu"])[0],
                          np.asarray(inputs["L0_ffn_sWd"])[0])
        else:
            g_, u_, d_ = Wg[e - 1], Wu[e - 1], Wd[e - 1]
        parts += [_chunkmajor(g_), _chunkmajor(u_), _chunkmajor(d_, nk=4)]
    wexp = np.concatenate(parts, axis=1)

    return dict(
        xrows=xrows.astype(np.float32), wqkv=wqkv.astype(np.float32),
        qkvb=qkvb.astype(np.float32), i128bf=i128, maskTi=maskTi, maskTx=maskTx,
        blk2=blk2, ones1=ones1, cvals=cvals_np, onescol=onescol, grp4=grp4,
        sel8=sel8, ropesc=ropesc, wqp=wqp.astype(np.float32),
        wkva=wkva.astype(np.float32), wln1c=wln1c.astype(np.float32),
        wln2c=wln2c.astype(np.float32), wkvbka=wkvbka, wkvbtok=wkvbtok,
        woe=woe.astype(np.float32), wgt=wgt.astype(np.float32), ebias=ebias,
        idx8=idx8,
        cq=cqt, sq_t=sqt, ck=ckt, sk_t=skt, wexp=wexp.astype(np.float32))


def unpack_x(o_x):
    return np.ascontiguousarray(o_x.reshape(128, 16, 1024).transpose(1, 0, 2).reshape(2048, 1024))


def kernel(**inputs):
    import os
    stage = os.environ.get("MK_STAGE", "full")
    nc = build_nc(0, stage=stage)
    in_maps = [prep_core_inputs(c, inputs) for c in range(8)]
    res = run_bass_kernel_spmd(nc, in_maps, core_ids=list(range(8)))
    out = np.zeros((2, 8192, 1024), dtype=np.float32)
    for c in range(8):
        rows = unpack_x(np.asarray(res.results[c]["o_x"]))
        blocks = [c, 15 - c]
        i = 0
        for b in (0, 1):
            for j in blocks:
                out[b, 512 * j:512 * (j + 1)] = rows[i * 512:(i + 1) * 512]
                i += 1
    return out


# version 08
# speedup vs baseline: 1.0929x; 1.0337x over previous
import numpy as np
import ml_dtypes
import concourse.bass as bass
import concourse.mybir as mybir
from concourse.bass_utils import run_bass_kernel_spmd
from concourse.bass_core_rust import IndirectOffsetOnAxis

F32 = mybir.dt.float32
BF16 = mybir.dt.bfloat16
AF = mybir.ActivationFunctionType
ALU = mybir.AluOpType
AX = mybir.AxisListType

C = 1024
BLK = 512
QK_SCALE_A = 1.0 / float(np.sqrt(2.0 * C))
QK_SCALE_B = 0.125
EPS = 1e-5
NEG_BIG = -3.0e38
THETA = 100000.0
CONST_VALS = [1e-5, 1e-30, float(np.log(QK_SCALE_A)), float(np.log(QK_SCALE_B))]

ENGINES = ("tensor", "vector", "scalar", "gpsimd", "sync")


class Rec:
    def __init__(self):
        self.ops = []
        self.barriers = []

    def op(self, engine, fn, r=(), w=(), dma=False):
        assert engine in ENGINES
        self.ops.append((engine, fn, tuple(r), tuple(w), dma))

    def barrier(self):
        self.barriers.append(len(self.ops))

    def emit(self, nc):
        n = len(self.ops)
        stream, op_pos = [], []
        sidx = {}
        for (eng, fn, r, w, dma) in self.ops:
            st = ("d", eng) if dma else ("e", eng)
            stream.append(st)
            op_pos.append(sidx.get(st, 0))
            sidx[st] = op_pos[-1] + 1

        last_w, readers = {}, {}
        deps = [set() for _ in range(n)]
        bar_iter = iter(self.barriers + [n + 1])
        next_bar = next(bar_iter)
        bar_deps = []
        bar_pending = {}
        last_of_stream = {}
        for i, (eng, fn, r, w, dma) in enumerate(self.ops):
            while i >= next_bar:
                bar_deps = list(last_of_stream.values())
                bar_pending = {}
                next_bar = next(bar_iter)
            st = stream[i]
            if bar_deps and st not in bar_pending:
                for j in bar_deps:
                    if j != i:
                        deps[i].add(j)
                bar_pending[st] = True
            for k in r:
                if k in last_w:
                    deps[i].add(last_w[k])
            for k in w:
                if k in last_w:
                    deps[i].add(last_w[k])
                for j in readers.get(k, ()):
                    deps[i].add(j)
            for k in r:
                readers.setdefault(k, []).append(i)
            for k in w:
                last_w[k] = i
                readers[k] = []
            last_of_stream[st] = i

        def needs_edge(i, j):
            return not (stream[i] == stream[j] == ("e", "tensor"))

        need_inc = set()
        for j in range(n):
            for i in deps[j]:
                if needs_edge(i, j):
                    need_inc.add(i)
        for i in range(n):
            if stream[i][0] == "d":
                need_inc.add(i)

        NSLOT = 4
        rank, cnt = {}, {}
        dslot, duse = {}, {}
        for i in range(n):
            st = stream[i]
            if st[0] == "d":
                k = op_pos[i]
                dslot[i] = k % NSLOT
                duse[i] = k // NSLOT
            elif i in need_inc:
                cnt[st] = cnt.get(st, 0) + 1
                rank[i] = cnt[st]

        per_eng = {e: [] for e in ENGINES}
        for i in range(n):
            per_eng[self.ops[i][0]].append(i)

        import contextlib
        with contextlib.ExitStack() as stack:
            esem = {e: stack.enter_context(nc.semaphore(f"S_{e}")) for e in ENGINES}
            dsem = {e: [stack.enter_context(nc.semaphore(f"D_{e}_{s}")) for s in range(NSLOT)]
                    for e in ENGINES}
            block = stack.enter_context(nc.Block())

            def sem_of(st, i):
                if st[0] == "d":
                    return dsem[st[1]][dslot[i]], 16 * (duse[i] + 1)
                return esem[st[1]], rank[i]

            def make_body(eng):
                idxs = per_eng[eng]

                def body(E):
                    water = {}
                    for j in idxs:
                        _, fn, r, w, dma = self.ops[j]
                        waits = {}
                        for i in deps[j]:
                            if i not in need_inc or not needs_edge(i, j):
                                continue
                            sem, v = sem_of(stream[i], i)
                            if waits.get(sem.name, (None, 0))[1] < v:
                                waits[sem.name] = (sem, v)
                        if dma:
                            sem = dsem[eng][dslot[j]]
                            v = 16 * duse[j]
                            if v > 0 and waits.get(sem.name, (None, 0))[1] < v:
                                waits[sem.name] = (sem, v)
                        for name in sorted(waits):
                            sem, v = waits[name]
                            if water.get(name, 0) >= v:
                                continue
                            E.wait_ge(sem, v)
                            water[name] = v
                        inst = fn(E)
                        if j in need_inc:
                            st = stream[j]
                            if st[0] == "d":
                                inst.then_inc(dsem[st[1]][dslot[j]], 16)
                            else:
                                inst.then_inc(esem[st[1]], 1)
                return body

            for eng in ENGINES:
                getattr(block, eng)(make_body(eng))



def build_nc(core, stage="full", taps=False):
    nc = bass.Bass("TRN2", target_bir_lowering=False)
    R = Rec()

    def din(name, shape):
        return nc.dram_tensor(name, shape, F32, kind="ExternalInput").ap()

    xrows = din("xrows", [128, 16 * C])
    wqkv = din("wqkv", [128, 8 * 5120])
    qkvb = din("qkvb", [128, 40])
    i128bf = din("i128bf", [128, 128])
    maskTi = din("maskTi", [128, 128])
    maskTx = din("maskTx", [128, 128])
    blk2 = din("blk2", [128, 2])
    ones1 = din("ones1", [1, 128])
    cvals = din("cvals", [128, 8])
    onescol = din("onescol", [128, 1])
    grp4 = din("grp4", [128, 4])
    sel8 = din("sel8", [8, 8 * 128])
    ropesc = din("ropesc", [128, 4 * 128])
    wqp = din("wqp", [128, 8 * C])
    wkva = din("wkva", [128, 8 * 160])
    wln1c = din("wln1c", [128, 8])
    wln2c = din("wln2c", [128, 8])
    wkvbka = din("wkvbka", [128, 128])
    wkvbtok = din("wkvbtok", [128, 66])
    idx8 = nc.dram_tensor("idx8", [128, 8], mybir.dt.int32, kind="ExternalInput").ap()
    woe = din("woe", [128, 4 * C])
    wgt = din("wgt", [128, 8 * 8])
    ebias = din("ebias", [8, 1])
    cq = din("cq", [128, 2048])
    sq_t = din("sq_t", [128, 2048])
    ck = din("ck", [32, 2048])
    sk_t = din("sk_t", [32, 2048])
    wexp = din("wexp", [128, 9 * 12288])

    o_x = nc.dram_tensor("o_x", [128, 16 * C], F32, kind="ExternalOutput").ap()
    xscr = nc.dram_tensor("xscr", [128, 16 * C], F32).ap()
    xascr = nc.dram_tensor("xascr", [128, 16 * C], F32).ap()
    ccin = nc.dram_tensor("ccin", [160, 2048], BF16).ap()
    ccout = nc.dram_tensor("ccout", [8 * 160, 2048], BF16, addr_space="Shared").ap()
    qgin = nc.dram_tensor("qgin", [128, 16384], BF16).ap()
    qgout = nc.dram_tensor("qgout", [1024, 16384], BF16, addr_space="Shared").ap()
    ygin = nc.dram_tensor("ygin", [128, 8192], BF16).ap()
    ygout = nc.dram_tensor("ygout", [1024, 8192], BF16, addr_space="Shared").ap()

    ARENA = 206000
    _ar = nc.bump_sbuf(ARENA)
    ARENA_BASE = _ar[0]
    alloc_at = nc.alloc_sbuf_tensor_at
    cur = [0]

    def place(name, shape, dtype, at=None):
        nbytes = int(np.prod(shape[1:])) * (4 if dtype in (F32, mybir.dt.int32) else 2)
        if at is None:
            at = cur[0]
            cur[0] = at + ((nbytes + 63) // 64) * 64
        return alloc_at(name, shape, dtype, offset=ARENA_BASE + at)

    i128b = place("i128b", [128, 128], BF16)
    i128f = place("i128f", [128, 128], F32)
    mTi = place("mTi", [128, 128], BF16)
    mTx = place("mTx", [128, 128], BF16)
    blk2_s = place("blk2_s", [128, 2], BF16)
    ones1_s = place("ones1_s", [1, 128], BF16)
    cvals_s = place("cvals_s", [128, 8], F32)
    onescol_s = place("onescol_s", [128, 1], BF16)
    grp4_s = place("grp4_s", [128, 4], BF16)
    sel8_s = place("sel8_s", [8, 8 * 128], BF16)
    ropesc_s = place("ropesc_s", [128, 4 * 128], BF16)
    stats = place("stats", [128, 12], F32)
    mv = place("mv", [128, 2], F32)
    lnt = place("lnt", [128, 4], F32)
    lnres = place("lnres", [128, C], F32)
    recb = place("recb", [128, 4], F32)
    mstab = place("mstab", [128, 8], F32)
    mstabb = place("mstabb", [1, 16], BF16)
    negM = place("negM", [128, 1], F32)
    negMB = place("negMB", [128, 16], F32)
    qkmax2 = place("qkmax2", [128, 24], F32)
    sqb = place("sqb", [128, BLK], BF16)
    ln2st = place("ln2st", [128, 32], F32)
    Pb = [place(f"Pb{i}", [128, 1536], BF16) for i in range(2)]
    xu = place("xu", [128, 4 * C], F32)
    qn16 = place("qn16", [128, 16], F32)
    kn16 = place("kn16", [128, 16], F32)
    gtop = place("gtop", [128, 8], F32)
    gsc = place("gsc", [128, 32], F32)
    combf = place("combf", [8, 4 * BLK], BF16)
    combz = place("combz", [128, 2], F32)
    wgt_s = place("wgt_s", [128, 8 * 8], BF16)
    ebias_s = place("ebias_s", [8, 1], F32)
    wln2c_s = place("wln2c_s", [128, 8], F32)

    for ci, cv in enumerate(CONST_VALS):
        nc.const_aps.register(F32, float(cv), cvals_s[:, ci:ci + 1])

    PERSIST_END = cur[0]
    assert PERSIST_END < 45000, PERSIST_END

    cur[0] = PERSIST_END
    wqkv_s = place("wqkv_s", [128, 8 * 5120], BF16)
    qkv = place("qkv", [128, 40 * BLK], BF16)
    x1T = place("x1T", [128, 8 * BLK], BF16)
    x1u = place("x1u", [128, 4 * C], BF16)
    vtok = place("vtok", [128, 4 * 16 * 65], BF16)
    attnA = place("attnA", [128, 4 * C], F32)
    qkvb_s = place("qkvb_s", [128, 40], F32)
    accs = place("accs", [128, BLK], F32)
    assert cur[0] <= ARENA, cur[0]

    cur[0] = PERSIST_END
    latent = place("latent", [128, 16384], BF16)
    krope4 = place("krope4", [128, 4096], BF16)
    wkvbka_s = place("wkvbka_s", [128, 128], BF16)
    wkvbtok_s = place("wkvbtok_s", [128, 66], BF16)
    wo_s = place("wo_s", [128, 4 * C], BF16)
    wln1c_s = place("wln1c_s", [128, 8], F32)
    idx8_s = place("idx8_s", [128, 8], mybir.dt.int32)
    BSH_END = cur[0]

    cur[0] = BSH_END
    x2nT = place("x2nT", [128, 8 * 2048], BF16)
    q_fm = place("q_fm", [128, 8 * 2048], BF16)
    wq_s = place("wq_s", [128, 8 * C], BF16)
    wkva_s = place("wkva_s", [128, 8 * 160], BF16)
    cqs = place("cqs", [128, 2048], BF16)
    sqs = place("sqs", [128, 2048], BF16)
    qrs = place("qrs", [128, 2 * 2048], BF16)
    CKVR_AT = cur[0]
    ckvr = place("ckvr", [128, 2048], BF16)
    kropeo_t = place("kropeo_t", [32, 2048], BF16)
    cks_t = place("cks_t", [32, 2048], BF16)
    sks_t = place("sks_t", [32, 2048], BF16)
    kropeo = kropeo_t[0:32, :]
    cks = cks_t[0:32, :]
    sks = sks_t[0:32, :]
    x2f = place("x2f", [128, C], F32, at=CKVR_AT)
    assert cur[0] <= ARENA, cur[0]

    cur[0] = BSH_END
    q_all = place("q_all", [128, 16384], BF16)
    KA0A = cur[0]
    ka_own = place("ka_own", [128, 2 * 16 * 512], BF16)
    ytmp = place("ytmp", [128, 8192], BF16, at=KA0A)
    kvt_own = place("kvt_own", [128, 2 * 16 * 264], BF16)
    ytok = place("ytok", [128, 128 * 64], BF16)
    YTOK0 = cur[0] - 16384
    acc2 = place("accs2", [128, 1024], F32)
    yfm = place("yfm", [128, 4 * BLK], BF16)
    xab = place("xab", [128, C], F32)
    qpad_he = place("qpad_he", [128, 2 * 512], BF16)
    qpad_ho = place("qpad_ho", [128, 2 * 512], BF16)
    yfull = place("yfull", [128, 16 * BLK], BF16, at=YTOK0)
    assert cur[0] <= ARENA, cur[0]

    cur[0] = PERSIST_END
    xanT = place("xanT", [128, 8 * 2048], BF16)
    moeacc = place("moeacc", [128, 16 * C], F32)
    wg_s = [place(f"wg_s{i}", [128, 8 * BLK], BF16) for i in range(2)]
    wu_s = [place(f"wu_s{i}", [128, 8 * BLK], BF16) for i in range(2)]
    wd_s = [place(f"wd_s{i}", [128, 4 * C], BF16) for i in range(2)]
    mid_s = place("mid_s", [128, 4 * BLK], BF16)
    silu_s = place("silu_s", [128, BLK], BF16)
    gatef = place("gatef", [8, 4 * BLK], F32)
    xau = place("xau", [128, C], F32)
    assert cur[0] <= ARENA, cur[0]

    ps = nc.alloc_psum_tensor
    S0 = ps("S0", [128, 1536], F32)
    S1 = ps("S1", [128, 1536], F32)
    Sg = [S0, S1]
    Sbf = [S0[:].bitcast(BF16), S1[:].bitcast(BF16)]
    accp = ps("accp", [128, 512], F32)
    auxp = ps("auxp", [128, 512], F32)
    SB = [Sg[j // 3][:, (j % 3) * 512:(j % 3 + 1) * 512] for j in range(6)]
    PBk = [Pb[j // 3][:, (j % 3) * 512:(j % 3 + 1) * 512] for j in range(6)]

    R.op("gpsimd", lambda E: E.dma_start(out=wqkv_s[:], in_=wqkv[:]), w=["wqkv_s"], dma=True)
    R.op("sync", lambda E: E.dma_start(out=qkvb_s[:], in_=qkvb[:]), w=["qkvb_s"], dma=True)
    R.op("gpsimd", lambda E: E.dma_start(out=i128b[:], in_=i128bf[:]), w=["i128b"], dma=True)
    R.op("sync", lambda E: E.dma_start(out=i128f[:], in_=i128bf[:]), w=["i128f"], dma=True)
    R.op("gpsimd", lambda E: E.dma_start(out=mTi[:], in_=maskTi[:]), w=["mTi"], dma=True)
    R.op("gpsimd", lambda E: E.dma_start(out=mTx[:], in_=maskTx[:]), w=["mTx"], dma=True)
    R.op("gpsimd", lambda E: E.dma_start(out=blk2_s[:], in_=blk2[:]), w=["blk2_s"], dma=True)
    R.op("gpsimd", lambda E: E.dma_start(out=ones1_s[:], in_=ones1[:]), w=["ones1_s"], dma=True)
    R.op("sync", lambda E: E.dma_start(out=cvals_s[:], in_=cvals[:]), w=["cvals_s"], dma=True)
    R.op("gpsimd", lambda E: E.dma_start(out=onescol_s[:], in_=onescol[:]), w=["onescol_s"], dma=True)
    R.op("gpsimd", lambda E: E.dma_start(out=grp4_s[:], in_=grp4[:]), w=["grp4_s"], dma=True)
    R.op("gpsimd", lambda E: E.dma_start(out=sel8_s[:], in_=sel8[:]), w=["sel8_s"], dma=True)
    R.op("gpsimd", lambda E: E.dma_start(out=ropesc_s[:], in_=ropesc[:]), w=["ropesc_s"], dma=True)
    R.op("vector", lambda E: E.memset(mstab[:], 0.0),
         w=["ms0", "ms1", "ms23", "ms45", "ms6", "ms7"])

    def ln_stats(src_keys, src_ap_fn):
        R.op("vector", lambda E: E.bn_stats(out=stats[:, 0:6], in_=src_ap_fn()[:, 0:512]),
             r=src_keys, w=["stats"])
        R.op("vector", lambda E: E.bn_stats(out=stats[:, 6:12], in_=src_ap_fn()[:, 512:1024]),
             r=src_keys, w=["stats2"])
        R.op("vector", lambda E: E.bn_aggr(out=mv[:], in_=stats[:]), r=["stats", "stats2"], w=["mv"])
        R.op("scalar", lambda E: E.activation(out=lnt[:, 2:3], in_=mv[:, 1:2], func=AF.Ln, bias=float(EPS)),
             r=["mv", "cvals_s"], w=["lnt2"])
        R.op("scalar", lambda E: E.activation(out=lnt[:, 0:1], in_=lnt[:, 2:3], func=AF.Exp, scale=-0.5),
             r=["lnt2"], w=["lnt0"])
        R.op("vector", lambda E: E.tensor_scalar(out=lnt[:, 1:2], in0=mv[:, 0:1], scalar1=lnt[:, 0:1],
                                                 scalar2=-1.0, op0=ALU.mult, op1=ALU.mult),
             r=["mv", "lnt0"], w=["lnt1"])

    def rms_inv(src_keys, src_ap_fn):
        R.op("vector", lambda E: E.bn_stats(out=stats[:, 0:6], in_=src_ap_fn()[:, 0:512]),
             r=src_keys, w=["stats"])
        R.op("vector", lambda E: E.bn_stats(out=stats[:, 6:12], in_=src_ap_fn()[:, 512:1024]),
             r=src_keys, w=["stats2"])
        R.op("vector", lambda E: E.bn_aggr(out=mv[:], in_=stats[:]), r=["stats", "stats2"], w=["mv"])
        R.op("vector", lambda E: E.tensor_scalar(out=lnt[:, 3:4], in0=mv[:, 0:1],
                                                 scalar1=mv[:, 0:1], scalar2=None, op0=ALU.mult),
             r=["mv"], w=["lnt3"])
        R.op("vector", lambda E: E.tensor_add(out=lnt[:, 2:3], in0=mv[:, 1:2], in1=lnt[:, 3:4]),
             r=["mv", "lnt3"], w=["lnt2"])
        R.op("scalar", lambda E: E.activation(out=lnt[:, 3:4], in_=lnt[:, 2:3], func=AF.Ln, bias=float(EPS)),
             r=["lnt2", "cvals_s"], w=["lnt3"])
        R.op("scalar", lambda E: E.activation(out=lnt[:, 2:3], in_=lnt[:, 3:4], func=AF.Exp, scale=-0.5),
             r=["lnt3"], w=["lnt2"])

    def emit_flash(specs, scale_const, bias_fn, bias_key, acc_start=True, acc_stop=True):
        banks = []
        bcur, bused = [], 0
        for sp_ in specs:
            if bused + sp_["w"] > 512:
                banks.append(bcur)
                bcur, bused = [], 0
            bcur.append((sp_, bused))
            bused += sp_["w"]
        if bcur:
            banks.append(bcur)
        n_av = sum(len(b) for b in banks)
        av_cnt = [0]

        def emit_qk(j):
            s = j % 6
            for (sp_, lc) in banks[j]:
                R.op("tensor", lambda E, sp_=sp_, lc=lc, s=s: sp_["qk"](E, SB[s], lc),
                     r=sp_["qkr"], w=[f"Sb{s}"])

        def emit_exp(j):
            s = j % 6
            runs = []
            for (sp_, lc) in banks[j]:
                if runs and runs[-1][1] == lc:
                    runs[-1][1] = lc + sp_["w"]
                else:
                    runs.append([lc, lc + sp_["w"]])
            for (c0, c1) in runs:
                R.op("scalar", lambda E, c0=c0, c1=c1, s=s: E.activation(
                    out=PBk[s][:, c0:c1], in_=SB[s][:, c0:c1], func=AF.Exp,
                    scale=float(scale_const), bias=bias_fn()),
                    r=[f"Sb{s}", bias_key], w=[f"Pb{s}"])

        def emit_mask(j):
            s = j % 6
            for (sp_, lc) in banks[j]:
                if sp_["mask"] is None:
                    continue
                R.op("vector", lambda E, sp_=sp_, lc=lc, s=s: E.tensor_mul(
                    out=PBk[s][:, lc:lc + 128], in0=PBk[s][:, lc:lc + 128],
                    in1=sp_["mask"]()),
                    r=[f"Pb{s}"] + sp_["maskr"], w=[f"Pb{s}"])

        def emit_av(j):
            s = j % 6
            for (sp_, lc) in banks[j]:
                av_cnt[0] += 1
                R.op("tensor", lambda E, sp_=sp_, lc=lc, s=s,
                     st=(av_cnt[0] == 1 and acc_start),
                     sp2=(av_cnt[0] == n_av and acc_stop): sp_["av"](E, PBk[s], lc, st, sp2),
                     r=sp_["avr"] + [f"Pb{s}"], w=["accp"])

        n = len(banks)
        D = 2
        for j in range(n):
            emit_qk(j)
            emit_exp(j)
            emit_mask(j)
            if j >= D:
                emit_av(j - D)
        for j in range(max(0, n - D), n):
            emit_av(j)

    tilesA = [(0, gp, 512) for gp in range(16)]
    tilesA += [(2, gp, 256) for gp in range(16)]
    for i in range(16):
        tilesA += [(1, i, 384), (3, i, 128)]

    def kslice(gp):
        j = gp // 2
        return (16 + j) if gp % 2 == 0 else (24 + j)

    rot1 = [(i + 1) % 32 for i in range(32)]
    rot2 = [(i + 2) % 32 for i in range(32)]
    rot_sw = []
    for i in range(16):
        rot_sw += [2 * i + 1, 2 * i]

    def emit_unit_stageA(u):
        R.op("sync", lambda E, u=u: E.dma_start(out=xu[:], in_=xrows[:, 4 * u * C: 4 * (u + 1) * C]),
             w=[f"xu:{t}" for t in range(4)], dma=True)

        for t in range(4):
            ln_stats([f"xu:{t}"], lambda t=t: xu[:, t * C:(t + 1) * C])
            R.op("vector", lambda E, t=t: E.tensor_scalar(
                out=x1u[:, t * C:(t + 1) * C], in0=xu[:, t * C:(t + 1) * C],
                scalar1=lnt[:, 0:1], scalar2=lnt[:, 1:2], op0=ALU.mult, op1=ALU.add),
                r=[f"xu:{t}", "lnt0", "lnt1"], w=[f"x1u:{t}"])

        for t in range(4):
            for cch in range(8):
                slot = (t * 8 + cch) % 2
                bv = Sbf[slot]
                R.op("tensor", lambda E, t=t, c=cch, bv=bv: E.transpose(
                    out=bv[:, 0:128], in_=x1u[:, t * C + c * 128: t * C + (c + 1) * 128],
                    identity=i128b[:]),
                    r=[f"x1u:{t}", "i128b"], w=[f"Sb{3*slot}"])
                R.op("vector", lambda E, t=t, c=cch, bv=bv: E.tensor_copy(
                    out=x1T[:, c * BLK + t * 128: c * BLK + (t + 1) * 128], in_=bv[:, 0:128]),
                    r=[f"Sb{3*slot}"], w=[f"x1T:{cch}"])

        for oc in range(40):
            slot = oc % 2
            Sp = Sg[slot]
            for kc in range(8):
                R.op("tensor", lambda E, oc=oc, kc=kc, Sp=Sp: E.matmul(
                    out=Sp[:, 0:512],
                    lhsT=wqkv_s[:, kc * 5120 + oc * 128: kc * 5120 + (oc + 1) * 128],
                    rhs=x1T[:, kc * BLK:(kc + 1) * BLK],
                    start=(kc == 0), stop=(kc == 7)),
                    r=["wqkv_s", f"x1T:{kc}"], w=[f"Sb{3*slot}"])
            R.op("vector", lambda E, oc=oc, Sp=Sp: E.tensor_scalar(
                out=qkv[:, oc * BLK:(oc + 1) * BLK], in0=Sp[:, 0:512],
                scalar1=qkvb_s[:, oc:oc + 1], scalar2=None, op0=ALU.add),
                r=[f"Sb{3*slot}", "qkvb_s"], w=[f"qkv:{oc}"])

        R.op("vector", lambda E: E.memset(
            vtok[:].rearrange("p (tg c) -> p tg c", c=65)[:, :, 64:65], 1.0),
            w=["vtok_ones"])
        for t in range(4):
            for i in range(8):
                oc = 32 + i
                slot = (t * 8 + i) % 2
                bv = Sbf[slot]
                base = t * 1040
                R.op("tensor", lambda E, t=t, oc=oc, bv=bv: E.transpose(
                    out=bv[:, 0:128],
                    in_=qkv[:, oc * BLK + t * 128: oc * BLK + (t + 1) * 128],
                    identity=i128b[:]),
                    r=[f"qkv:{oc}", "i128b"], w=[f"Sb{3*slot}"])
                R.op("vector", lambda E, i=i, bv=bv, base=base: E.tensor_copy(
                    out=vtok[:, base + 65 * (2 * i): base + 65 * (2 * i) + 64],
                    in_=bv[:, 0:64]),
                    r=[f"Sb{3*slot}"], w=[f"vtok:{base // 1040}:{2*i}"])
                R.op("vector", lambda E, i=i, bv=bv, base=base: E.tensor_copy(
                    out=vtok[:, base + 65 * (2 * i + 1): base + 65 * (2 * i + 1) + 64],
                    in_=bv[:, 64:128]),
                    r=[f"Sb{3*slot}"], w=[f"vtok:{base // 1040}:{2*i+1}"])

        for ch in range(24):
            R.op("vector", lambda E, ch=ch: E.tensor_mul(
                out=sqb[:], in0=qkv[:, ch * BLK:(ch + 1) * BLK], in1=qkv[:, ch * BLK:(ch + 1) * BLK]),
                r=[f"qkv:{ch}"], w=["sqb"])
            R.op("tensor", lambda E: E.matmul(
                out=auxp[0:2, 0:512], lhsT=blk2_s[:], rhs=sqb[:], start=True, stop=True),
                r=["blk2_s", "sqb"], w=["auxp"])
            R.op("vector", lambda E, ch=ch: E.tensor_reduce(
                out=qkmax2[0:2, ch:ch + 1], in_=auxp[0:2, 0:512], axis=AX.X, op=ALU.max),
                r=["auxp"], w=[f"qkmax2:{ch}"])
        R.op("vector", lambda E: E.tensor_reduce(
            out=mstab[0:2, 0:1], in_=qkmax2[0:2, 0:16], axis=AX.X, op=ALU.max),
            r=[f"qkmax2:{ch}" for ch in range(16)], w=["ms0"])
        R.op("vector", lambda E: E.tensor_reduce(
            out=mstab[0:2, 1:2], in_=qkmax2[0:2, 16:24], axis=AX.X, op=ALU.max),
            r=[f"qkmax2:{ch}" for ch in range(16, 24)], w=["ms1"])
        R.op("vector", lambda E: E.stream_shuffle(out=mstab[0:32, 2:4], in_=mstab[0:32, 0:2], mask=rot1),
             r=["ms0", "ms1"], w=["ms23"])
        R.op("vector", lambda E: E.tensor_max(out=mstab[0:1, 4:6], in0=mstab[0:1, 0:2], in1=mstab[0:1, 2:4]),
             r=["ms0", "ms1", "ms23"], w=["ms45"])
        R.op("vector", lambda E: E.tensor_mul(out=mstab[0:1, 6:7], in0=mstab[0:1, 4:5], in1=mstab[0:1, 5:6]),
             r=["ms45"], w=["ms6"])
        R.op("scalar", lambda E: E.activation(out=mstab[0:1, 7:8], in_=mstab[0:1, 6:7], func=AF.Ln, bias=1e-30),
             r=["ms6", "cvals_s"], w=["ms7"])
        R.op("scalar", lambda E: E.activation(out=mstab[0:1, 6:7], in_=mstab[0:1, 7:8], func=AF.Exp,
                                              scale=0.5, bias=float(np.log(QK_SCALE_A))),
             r=["ms7", "cvals_s"], w=["ms6"])
        R.op("vector", lambda E: E.tensor_scalar(out=mstabb[:, 0:1], in0=mstab[0:1, 6:7], scalar1=-1.0,
                                                 scalar2=None, op0=ALU.mult),
             r=["ms6"], w=["mstabb"])
        R.op("tensor", lambda E: E.matmul(out=auxp[:, 0:1], lhsT=ones1_s[:], rhs=mstabb[:, 0:1],
                                          start=True, stop=True),
             r=["ones1_s", "mstabb"], w=["auxp"])
        R.op("vector", lambda E: E.tensor_copy(out=negM[:], in_=auxp[:, 0:1]),
             r=["auxp"], w=["negM"])

        for g in range(16):
            specs = []
            for (m, gp, w) in tilesA:
                kch = kslice(gp)
                mT, mk = (mTi, "mTi") if gp <= g else (mTx, "mTx")

                def qk(E, Sp, col, m=m, w=w, kch=kch, g=g):
                    return E.matmul(
                        out=Sp[:, col:col + w],
                        lhsT=qkv[:, kch * BLK + m * 128: kch * BLK + (m + 1) * 128],
                        rhs=qkv[:, g * BLK + m * 128: g * BLK + 512],
                        start=True, stop=True)

                def av(E, Pp, col, st, sp2, m=m, gp=gp, w=w):
                    return E.matmul(
                        out=accp[0:65, m * 128: m * 128 + w],
                        lhsT=vtok[:, m * 1040 + 65 * gp: m * 1040 + 65 * gp + 65],
                        rhs=Pp[:, col:col + w], start=st, stop=sp2)

                specs.append(dict(w=w, qk=qk, qkr=[f"qkv:{kch}", f"qkv:{g}"],
                                  mask=(lambda mT=mT: mT[:]), maskr=[mk],
                                  av=av, avr=[f"vtok:{m}:{gp}", "vtok_ones"]))
            emit_flash(specs, QK_SCALE_A, lambda: negM[:], "negM")

            R.op("vector", lambda E: E.tensor_copy(out=accs[0:65, :], in_=accp[0:65, :]),
                 r=["accp"], w=["accs"])
            for t in range(4):
                R.op("tensor", lambda E, t=t: E.transpose(
                    out=auxp[:, t * 65: t * 65 + 65], in_=accs[0:65, t * 128:(t + 1) * 128],
                    identity=i128f[0:65, 0:65]),
                    r=["accs", "i128f"], w=["auxp"])
            for t in range(4):
                R.op("vector", lambda E, t=t: E.reciprocal(
                    out=recb[:, t:t + 1], in_=auxp[:, t * 65 + 64: t * 65 + 65]),
                    r=["auxp"], w=[f"recb:{t}"])
                R.op("vector", lambda E, t=t, g=g: E.tensor_scalar(
                    out=attnA[:, t * C + g * 64: t * C + g * 64 + 64],
                    in0=auxp[:, t * 65: t * 65 + 64],
                    scalar1=recb[:, t:t + 1], scalar2=None, op0=ALU.mult),
                    r=["auxp", f"recb:{t}"], w=[f"attnA:{t}:{g}"])

        for t in range(4):
            aks = [f"attnA:{t}:{g}" for g in range(16)]
            ln_stats(aks, lambda t=t: attnA[:, t * C:(t + 1) * C])
            R.op("vector", lambda E, t=t: E.tensor_scalar(
                out=lnres[:], in0=attnA[:, t * C:(t + 1) * C],
                scalar1=lnt[:, 0:1], scalar2=lnt[:, 1:2], op0=ALU.mult, op1=ALU.add),
                r=aks + ["lnt0", "lnt1"], w=["lnres"])
            R.op("vector", lambda E, t=t: E.tensor_add(
                out=xu[:, t * C:(t + 1) * C], in0=xu[:, t * C:(t + 1) * C], in1=lnres[:]),
                r=["lnres", f"xu:{t}"], w=[f"xu:{t}"])
        dst = o_x if stage == "A" else xscr
        R.op("sync", lambda E, u=u, dst=dst: E.dma_start(
            out=dst[:, 4 * u * C: 4 * (u + 1) * C], in_=xu[:]),
            r=[f"xu:{t}" for t in range(4)], w=[f"xscr:{u}"], dma=True)

    for u in range(4):
        emit_unit_stageA(u)

    if stage == "A":
        R.op("gpsimd", lambda E: E.nop(), r=[f"xscr:{u}" for u in range(4)])
        R.emit(nc)
        return nc

    R.barrier()

    R.op("gpsimd", lambda E: E.dma_start(out=wq_s[:], in_=wqp[:]), w=["wq_s"], dma=True)
    R.op("gpsimd", lambda E: E.dma_start(out=wkva_s[:], in_=wkva[:]), w=["wkva_s"], dma=True)
    R.op("sync", lambda E: E.dma_start(out=wln1c_s[:], in_=wln1c[:]), w=["wln1c_s"], dma=True)
    R.op("sync", lambda E: E.dma_start(out=wln2c_s[:], in_=wln2c[:]), w=["wln2c_s"], dma=True)
    R.op("gpsimd", lambda E: E.dma_start(out=wkvbka_s[:], in_=wkvbka[:]), w=["wkvbka_s"], dma=True)
    R.op("gpsimd", lambda E: E.dma_start(out=wkvbtok_s[:], in_=wkvbtok[:]), w=["wkvbtok_s"], dma=True)
    R.op("sync", lambda E: E.dma_start(out=idx8_s[:], in_=idx8[:]), w=["idx8_s"], dma=True)
    R.op("gpsimd", lambda E: E.dma_start(out=wo_s[:], in_=woe[:]), w=["wo_s"], dma=True)
    R.op("gpsimd", lambda E: E.dma_start(out=wgt_s[:], in_=wgt[:]), w=["wgt_raw"], dma=True)
    R.op("sync", lambda E: E.dma_start(out=ebias_s[:], in_=ebias[:]), w=["ebias_s"], dma=True)
    R.op("gpsimd", lambda E: E.dma_start(out=cqs[:], in_=cq[:]), w=["cqs"], dma=True)
    R.op("gpsimd", lambda E: E.dma_start(out=sqs[:], in_=sq_t[:]), w=["sqs"], dma=True)
    R.op("gpsimd", lambda E: E.dma_start(out=cks, in_=ck[:]), w=["cks"], dma=True)
    R.op("gpsimd", lambda E: E.dma_start(out=sks, in_=sk_t[:]), w=["sks"], dma=True)

    for kc in range(8):
        R.op("vector", lambda E, kc=kc: E.tensor_scalar(
            out=wq_s[:, kc * C:(kc + 1) * C], in0=wq_s[:, kc * C:(kc + 1) * C],
            scalar1=wln1c_s[:, kc:kc + 1], scalar2=None, op0=ALU.mult),
            r=["wq_s", "wln1c_s"], w=["wq_s"])
        R.op("vector", lambda E, kc=kc: E.tensor_scalar(
            out=wkva_s[:, kc * 160:(kc + 1) * 160], in0=wkva_s[:, kc * 160:(kc + 1) * 160],
            scalar1=wln1c_s[:, kc:kc + 1], scalar2=None, op0=ALU.mult),
            r=["wkva_s", "wln1c_s"], w=["wkva_s"])
        R.op("vector", lambda E, kc=kc: E.tensor_scalar(
            out=wgt_s[:, kc * 8:(kc + 1) * 8], in0=wgt_s[:, kc * 8:(kc + 1) * 8],
            scalar1=wln2c_s[:, kc:kc + 1], scalar2=None, op0=ALU.mult),
            r=["wgt_raw", "wln2c_s"], w=["wgt_s"])

    for u in range(4):
        R.op("sync", lambda E, u=u: E.dma_start(out=xu[:], in_=xscr[:, 4 * u * C: 4 * (u + 1) * C]),
             r=[f"xscr:{u}"], w=[f"xu:{t}" for t in range(4)], dma=True)
        for t in range(4):
            tt = 4 * u + t
            ln_stats([f"xu:{t}"], lambda t=t: xu[:, t * C:(t + 1) * C])
            R.op("vector", lambda E, tt=tt: E.tensor_copy(
                out=ln2st[:, 2 * tt: 2 * tt + 2], in_=lnt[:, 0:2]),
                r=["lnt0", "lnt1"], w=[f"ln2st:{tt}"])
            R.op("vector", lambda E, t=t: E.tensor_scalar(
                out=x2f[:], in0=xu[:, t * C:(t + 1) * C],
                scalar1=lnt[:, 0:1], scalar2=lnt[:, 1:2], op0=ALU.mult, op1=ALU.add),
                r=[f"xu:{t}", "lnt0", "lnt1"], w=["x2f"])
            rms_inv(["x2f"], lambda: x2f)
            R.op("vector", lambda E: E.tensor_scalar(
                out=sqb[:, 0:512], in0=x2f[:, 0:512], scalar1=lnt[:, 2:3],
                scalar2=None, op0=ALU.mult),
                r=["x2f", "lnt2"], w=["sqb"])
            R.op("vector", lambda E: E.tensor_scalar(
                out=qrs[:, 0:512], in0=x2f[:, 512:1024], scalar1=lnt[:, 2:3],
                scalar2=None, op0=ALU.mult),
                r=["x2f", "lnt2"], w=["qrs0"])
            for cch in range(8):
                slot = cch % 2
                bv = Sbf[slot]
                if cch < 4:
                    src_fn = (lambda c=cch: sqb[:, c * 128:(c + 1) * 128])
                    skey2 = "sqb"
                else:
                    src_fn = (lambda c=cch: qrs[:, (c - 4) * 128:(c - 4) * 128 + 128])
                    skey2 = "qrs0"
                R.op("tensor", lambda E, src_fn=src_fn, bv=bv: E.transpose(
                    out=bv[:, 0:128], in_=src_fn(), identity=i128b[:]),
                    r=[skey2, "i128b"], w=[f"Sb{3*slot}"])
                R.op("vector", lambda E, c=cch, bv=bv, tt=tt: E.tensor_copy(
                    out=x2nT[:, c * 2048 + tt * 128: c * 2048 + (tt + 1) * 128], in_=bv[:, 0:128]),
                    r=[f"Sb{3*slot}"], w=[f"x2nT:{cch}"])

    for tt4 in range(4):
        slot = tt4 % 2
        Sp = Sg[slot]
        for kc in range(8):
            R.op("tensor", lambda E, tt4=tt4, kc=kc, Sp=Sp: E.matmul(
                out=Sp[:, 0:512], lhsT=wkva_s[:, kc * 160: kc * 160 + 128],
                rhs=x2nT[:, kc * 2048 + tt4 * 512: kc * 2048 + (tt4 + 1) * 512],
                start=(kc == 0), stop=(kc == 7)),
                r=["wkva_s", f"x2nT:{kc}"], w=[f"Sb{3*slot}"])
        R.op("vector", lambda E, tt4=tt4, Sp=Sp: E.tensor_copy(
            out=ckvr[:, tt4 * 512:(tt4 + 1) * 512], in_=Sp[:, 0:512]),
            r=[f"Sb{3*slot}"], w=["ckvr"])
        for kc in range(8):
            R.op("tensor", lambda E, tt4=tt4, kc=kc, Sp=Sp: E.matmul(
                out=Sp[0:32, 512:1024], lhsT=wkva_s[:, kc * 160 + 128: kc * 160 + 160],
                rhs=x2nT[:, kc * 2048 + tt4 * 512: kc * 2048 + (tt4 + 1) * 512],
                start=(kc == 0), stop=(kc == 7)),
                r=["wkva_s", f"x2nT:{kc}"], w=[f"Sb{3*slot+1}"])
        R.op("vector", lambda E, tt4=tt4, Sp=Sp: E.tensor_copy(
            out=kropeo[:, tt4 * 512:(tt4 + 1) * 512], in_=Sp[0:32, 512:1024]),
            r=[f"Sb{3*slot+1}"], w=["kropeo"])

    R.op("vector", lambda E: E.stream_shuffle(out=qrs[0:32, 0:2048], in_=kropeo, mask=rot_sw),
         r=["kropeo"], w=["qrs0"])
    R.op("vector", lambda E: E.tensor_mul(out=qrs[0:32, 2048:4096], in0=kropeo, in1=cks),
         r=["kropeo", "cks"], w=["qrs1"])
    R.op("vector", lambda E: E.tensor_mul(out=qrs[0:32, 0:2048], in0=qrs[0:32, 0:2048], in1=sks),
         r=["qrs0", "sks"], w=["qrs0"])
    R.op("vector", lambda E: E.tensor_add(out=kropeo, in0=qrs[0:32, 2048:4096], in1=qrs[0:32, 0:2048]),
         r=["qrs0", "qrs1"], w=["kropeo"])

    R.op("sync", lambda E: E.dma_start(out=ccin[0:128, :], in_=ckvr[:]), r=["ckvr"], w=["ccin"], dma=True)
    R.op("sync", lambda E: E.dma_start(out=ccin[128:160, :], in_=kropeo), r=["kropeo"], w=["ccin2"], dma=True)
    R.op("gpsimd", lambda E: E.collective_compute(
        "AllGather", ALU.bypass, replica_groups=[list(range(8))],
        ins=[ccin[:]], outs=[ccout[:]]),
        r=["ccin", "ccin2"], w=["ccout"])

    for oc in range(8):
        for tt4 in range(4):
            slot = (oc * 4 + tt4) % 2
            Sp = Sg[slot]
            for kc in range(8):
                R.op("tensor", lambda E, oc=oc, tt4=tt4, kc=kc, Sp=Sp: E.matmul(
                    out=Sp[:, 0:512], lhsT=wq_s[:, kc * C + oc * 128: kc * C + (oc + 1) * 128],
                    rhs=x2nT[:, kc * 2048 + tt4 * 512: kc * 2048 + (tt4 + 1) * 512],
                    start=(kc == 0), stop=(kc == 7)),
                    r=["wq_s", f"x2nT:{kc}"], w=[f"Sb{3*slot}"])
            R.op("vector", lambda E, oc=oc, tt4=tt4, Sp=Sp: E.tensor_copy(
                out=q_fm[:, oc * 2048 + tt4 * 512: oc * 2048 + (tt4 + 1) * 512], in_=Sp[:, 0:512]),
                r=[f"Sb{3*slot}"], w=[f"q_fm:{oc}"])

    for oc in range(8):
        R.op("vector", lambda E, oc=oc: E.stream_shuffle(
            out=qrs[:, 0:2048], in_=q_fm[:, oc * 2048:(oc + 1) * 2048], mask=rot_sw),
            r=[f"q_fm:{oc}"], w=["qrs0"])
        R.op("vector", lambda E, oc=oc: E.tensor_mul(
            out=qrs[:, 2048:4096], in0=q_fm[:, oc * 2048:(oc + 1) * 2048], in1=cqs[:]),
            r=[f"q_fm:{oc}", "cqs"], w=["qrs1"])
        R.op("vector", lambda E: E.tensor_mul(out=qrs[:, 0:2048], in0=qrs[:, 0:2048], in1=sqs[:]),
             r=["qrs0", "sqs"], w=["qrs0"])
        R.op("vector", lambda E, oc=oc: E.tensor_add(
            out=q_fm[:, oc * 2048:(oc + 1) * 2048], in0=qrs[:, 2048:4096], in1=qrs[:, 0:2048]),
            r=["qrs0", "qrs1"], w=[f"q_fm:{oc}"])

    R.op("sync", lambda E: E.dma_start(out=qgin[:], in_=q_fm[:]),
         r=[f"q_fm:{oc}" for oc in range(8)], w=["qgin"], dma=True)
    R.op("gpsimd", lambda E: E.collective_compute(
        "AllGather", ALU.bypass, replica_groups=[list(range(8))],
        ins=[qgin[:]], outs=[qgout[:]]),
        r=["qgin"], w=["qgout"])

    R.op("vector", lambda E: E.tensor_mul(
        out=sqb[:, 0:128], in0=wkvbka_s[:], in1=wkvbka_s[:]),
        r=["wkvbka_s"], w=["sqb"])
    R.op("tensor", lambda E: E.matmul(out=auxp[0:1, 0:128], lhsT=onescol_s[:], rhs=sqb[:, 0:128],
                                      start=True, stop=True),
         r=["onescol_s", "sqb"], w=["auxp"])
    R.op("vector", lambda E: E.tensor_reduce(
        out=kn16[0:1, 0:1], in_=auxp[0:1, 0:32], axis=AX.X, op=ALU.add),
        r=["auxp"], w=["kn16:0"])
    R.op("vector", lambda E: E.tensor_reduce(
        out=kn16[0:1, 1:2], in_=auxp[0:1, 96:128], axis=AX.X, op=ALU.add),
        r=["auxp"], w=["kn16:1"])

    for r_ in range(8):
        R.op("sync", lambda E, r_=r_: E.dma_start(
            out=latent[:, r_ * 2048:(r_ + 1) * 2048], in_=ccout[160 * r_:160 * r_ + 128, :]),
            r=["ccout"], w=[f"latent:{r_}"], dma=True)
        R.op("sync", lambda E, r_=r_: E.dma_start(
            out=krope4[32 * (r_ // 2):32 * (r_ // 2) + 32,
                       (r_ % 2) * 2048:(r_ % 2) * 2048 + 2048],
            in_=ccout[160 * r_ + 128:160 * (r_ + 1), :]),
            r=["ccout"], w=[f"krope4:{r_}"], dma=True)

    R.op("vector", lambda E: E.memset(mstab[0:32, 0:2], 0.0), w=["ms0", "ms1"])
    for chk in range(32):
        R.op("vector", lambda E, chk=chk: E.tensor_mul(
            out=sqb[:], in0=latent[:, chk * 512:(chk + 1) * 512], in1=latent[:, chk * 512:(chk + 1) * 512]),
            r=[f"latent:{chk // 4}"], w=["sqb"])
        R.op("tensor", lambda E: E.matmul(out=auxp[0:1, 0:512], lhsT=onescol_s[:], rhs=sqb[:],
                                          start=True, stop=True),
             r=["onescol_s", "sqb"], w=["auxp"])
        R.op("vector", lambda E: E.tensor_reduce(out=mstab[0:1, 2:3], in_=auxp[0:1, 0:512],
                                                 axis=AX.X, op=ALU.max),
             r=["auxp"], w=["ms23"])
        R.op("vector", lambda E: E.tensor_max(out=mstab[0:1, 0:1], in0=mstab[0:1, 0:1], in1=mstab[0:1, 2:3]),
             r=["ms0", "ms23"], w=["ms0"])
    for chk in range(8):
        R.op("vector", lambda E, chk=chk: E.tensor_mul(
            out=sqb[:], in0=krope4[:, chk * 512:(chk + 1) * 512], in1=krope4[:, chk * 512:(chk + 1) * 512]),
            r=[f"krope4:{r}" for r in range(8)], w=["sqb"])
        R.op("tensor", lambda E: E.matmul(out=auxp[0:4, 0:512], lhsT=grp4_s[:], rhs=sqb[:],
                                          start=True, stop=True),
             r=["grp4_s", "sqb"], w=["auxp"])
        R.op("vector", lambda E: E.tensor_reduce(out=mstab[0:4, 2:3], in_=auxp[0:4, 0:512],
                                                 axis=AX.X, op=ALU.max),
             r=["auxp"], w=["ms23"])
        R.op("vector", lambda E: E.tensor_max(out=mstab[0:4, 1:2], in0=mstab[0:4, 1:2], in1=mstab[0:4, 2:3]),
             r=["ms1", "ms23"], w=["ms1"])
    R.op("vector", lambda E: E.stream_shuffle(out=mstab[0:32, 2:3], in_=mstab[0:32, 1:2], mask=rot2),
         r=["ms1"], w=["ms23"])
    R.op("vector", lambda E: E.tensor_max(out=mstab[0:32, 1:2], in0=mstab[0:32, 1:2], in1=mstab[0:32, 2:3]),
         r=["ms1", "ms23"], w=["ms1"])
    R.op("vector", lambda E: E.stream_shuffle(out=mstab[0:32, 2:3], in_=mstab[0:32, 1:2], mask=rot1),
         r=["ms1"], w=["ms23"])
    R.op("vector", lambda E: E.tensor_max(out=mstab[0:1, 1:2], in0=mstab[0:1, 1:2], in1=mstab[0:1, 2:3]),
         r=["ms1", "ms23"], w=["ms1"])

    R.barrier()

    qgv = qgout.rearrange("(s p) (c t) -> (s p c) t", p=128, t=2048)
    for s in range(8):
        R.op("gpsimd", lambda E, s=s: E.indirect_dma_start(
            out=q_all[:, s * 2048:(s + 1) * 2048], out_offset=None,
            in_=qgv[:, :],
            in_offset=IndirectOffsetOnAxis(ap=idx8_s[:, s:s + 1], axis=0)),
            r=["qgout", "idx8_s"], w=[f"q_all:{s}"], dma=True)

    R.op("vector", lambda E: E.memset(qkmax2[0:32, 0:4], 0.0), w=["qm2", "qm1", "qm3"])
    for chk in range(32):
        R.op("vector", lambda E, chk=chk: E.tensor_mul(
            out=sqb[:], in0=q_all[:, chk * 512:(chk + 1) * 512], in1=q_all[:, chk * 512:(chk + 1) * 512]),
            r=[f"q_all:{chk // 4}"], w=["sqb"])
        R.op("tensor", lambda E: E.matmul(
            out=auxp[0:2, 0:512], lhsT=blk2_s[:], rhs=sqb[:], start=True, stop=True),
            r=["blk2_s", "sqb"], w=["auxp"])
        R.op("vector", lambda E: E.tensor_reduce(
            out=qkmax2[0:2, 1:2], in_=auxp[0:2, 0:512], axis=AX.X, op=ALU.max),
            r=["auxp"], w=["qm1"])
        R.op("vector", lambda E: E.tensor_max(out=qkmax2[0:2, 0:1], in0=qkmax2[0:2, 0:1],
                                              in1=qkmax2[0:2, 1:2]),
             r=["qm1", "qm2"], w=["qm2"])
    R.op("vector", lambda E: E.tensor_scalar(
        out=kn16[0:1, 0:2], in0=kn16[0:1, 0:2], scalar1=mstab[0:1, 0:1], scalar2=mstab[0:1, 1:2],
        op0=ALU.mult, op1=ALU.add),
        r=["kn16:0", "kn16:1", "ms0", "ms1"], w=["kn16f"])
    R.op("vector", lambda E: E.tensor_copy(out=qn16[0:1, 0:1], in_=qkmax2[0:1, 0:1]),
         r=["qm2"], w=["qn16a"])
    R.op("vector", lambda E: E.stream_shuffle(out=qkmax2[0:32, 2:3], in_=qkmax2[0:32, 0:1], mask=rot1),
         r=["qm2"], w=["qm3"])
    R.op("vector", lambda E: E.tensor_copy(out=qn16[0:1, 1:2], in_=qkmax2[0:1, 2:3]),
         r=["qm3"], w=["qn16b"])
    R.op("vector", lambda E: E.tensor_mul(out=qn16[0:1, 0:2], in0=qn16[0:1, 0:2], in1=kn16[0:1, 0:2]),
         r=["qn16a", "qn16b", "kn16f"], w=["qn16f"])
    R.op("scalar", lambda E: E.activation(out=kn16[0:1, 0:2], in_=qn16[0:1, 0:2], func=AF.Ln, bias=1e-30),
         r=["qn16f", "cvals_s"], w=["kn16f2"])
    R.op("scalar", lambda E: E.activation(out=qn16[0:1, 0:2], in_=kn16[0:1, 0:2], func=AF.Exp,
                                          scale=0.5, bias=float(np.log(QK_SCALE_B))),
         r=["kn16f2", "cvals_s"], w=["qn16f2"])
    R.op("vector", lambda E: E.tensor_scalar(out=mstabb[:, 0:2], in0=qn16[0:1, 0:2], scalar1=-1.0,
                                             scalar2=None, op0=ALU.mult),
         r=["qn16f2"], w=["mstabb"])
    R.op("tensor", lambda E: E.matmul(out=auxp[:, 0:2], lhsT=ones1_s[:], rhs=mstabb[:, 0:2],
                                      start=True, stop=True),
         r=["ones1_s", "mstabb"], w=["auxp"])
    R.op("vector", lambda E: E.tensor_copy(out=negMB[:, 0:2], in_=auxp[:, 0:2]),
         r=["auxp"], w=["negMB"])

    def block_col(b, kb):
        r_ = min(kb, 15 - kb)
        qslot = 2 * b + (0 if kb < 8 else 1)
        return 2048 * r_ + 512 * qslot, r_

    for b in (0, 1):
        for kb in range(16):
            col0, r_ = block_col(b, kb)
            g4 = col0 // 4096
            kcol = col0 % 4096
            bi = b * 16 + kb
            slot = bi % 2
            Sp = Sg[slot]
            R.op("tensor", lambda E, col0=col0, Sp=Sp: E.matmul(
                out=Sp[:, 0:512], lhsT=wkvbka_s[:],
                rhs=latent[:, col0:col0 + 512], start=True, stop=False),
                r=["wkvbka_s", f"latent:{col0 // 2048}"], w=[f"Sb{3*slot}"])
            rb = 32 * g4 if g4 < 2 else 64
            rk = 32 if g4 < 2 else 64
            R.op("tensor", lambda E, g4=g4, kcol=kcol, Sp=Sp, rb=rb, rk=rk: E.matmul(
                out=Sp[:, 0:512],
                lhsT=ropesc_s[rb:rb + rk, g4 * 128:(g4 + 1) * 128],
                rhs=krope4[rb:rb + rk, kcol:kcol + 512],
                start=False, stop=True),
                r=["ropesc_s", f"krope4:{2*g4}", f"krope4:{2*g4+1}"], w=[f"Sb{3*slot}"])
            R.op("vector", lambda E, bi=bi, Sp=Sp: E.tensor_copy(
                out=ka_own[:, bi * 512:(bi + 1) * 512], in_=Sp[:, 0:512]),
                r=[f"Sb{3*slot}"], w=[f"ka:{bi}"])
            for tau in range(4):
                R.op("tensor", lambda E, tau=tau, col0=col0, Sp=Sp: E.matmul(
                    out=Sp[:, 512:578], lhsT=latent[:, col0 + tau * 128: col0 + (tau + 1) * 128],
                    rhs=wkvbtok_s[:], start=True, stop=True),
                    r=["wkvbtok_s", f"latent:{col0 // 2048}"], w=[f"Sb{3*slot+1}"])
                R.op("vector", lambda E, bi=bi, tau=tau, Sp=Sp: E.tensor_copy(
                    out=kvt_own[:, bi * 264 + tau * 66: bi * 264 + (tau + 1) * 66],
                    in_=Sp[:, 512:578]),
                    r=[f"Sb{3*slot+1}"], w=[f"kvt:{bi}"])
            R.op("vector", lambda E, bi=bi: E.memset(
                kvt_own[:, bi * 264:(bi + 1) * 264].rearrange("p (x c) -> p x c", c=33)[:, :, 32:33], 1.0),
                r=[f"kvt:{bi}"], w=[f"kvto:{bi}"])

    R.op("vector", lambda E: E.memset(qpad_he[64:128, :], 0.0), w=["qpadzh"])
    R.op("vector", lambda E: E.memset(qpad_ho[0:64, :], 0.0), w=["qpadzl"])
    for b in (0, 1):
        for jq in range(16):
            qcol0, _ = block_col(b, jq)
            par = (b * 16 + jq) % 2
            R.op("vector", lambda E, qcol0=qcol0, par=par: E.tensor_copy(
                out=qpad_he[0:64, par * 512:(par + 1) * 512],
                in_=q_all[0:64, qcol0:qcol0 + 512]),
                r=[f"q_all:{qcol0 // 2048}"], w=[f"qpe:{par}"])
            R.op("vector", lambda E, qcol0=qcol0, par=par: E.tensor_copy(
                out=qpad_ho[64:128, par * 512:(par + 1) * 512],
                in_=q_all[64:128, qcol0:qcol0 + 512]),
                r=[f"q_all:{qcol0 // 2048}"], w=[f"qpo:{par}"])
            for h2 in range(2):
                hb = 64 * h2
                qp = qpad_he if h2 == 0 else qpad_ho
                qpk = (f"qpe:{par}", "qpadzh") if h2 == 0 else (f"qpo:{par}", "qpadzl")
                specs = []
                for kb in range(jq + 1):
                    bi = b * 16 + kb
                    diag = (kb == jq)
                    for tau in range(4):
                        w = 512 - tau * 128 if diag else 512
                        scol = tau * 128 if diag else 0

                        def qk(E, Sp, col, bi=bi, tau=tau, w=w, scol=scol,
                               qp=qp, par=par):
                            return E.matmul(
                                out=Sp[:, col:col + w],
                                lhsT=ka_own[:, bi * 512 + tau * 128: bi * 512 + (tau + 1) * 128],
                                rhs=qp[:, par * 512 + scol: par * 512 + 512],
                                start=True, stop=True)

                        if diag:
                            mask, maskr = (lambda: mTi[:]), ["mTi"]
                        else:
                            mask, maskr = None, []

                        def av(E, Pp, col, st, sp2, bi=bi, tau=tau, w=w, scol=scol, hb=hb, h2=h2):
                            return E.matmul(
                                out=accp[hb:hb + 33, scol: scol + w],
                                lhsT=kvt_own[:, bi * 264 + tau * 66 + 33 * h2:
                                             bi * 264 + tau * 66 + 33 * h2 + 33],
                                rhs=Pp[:, col:col + w], start=st, stop=sp2)

                        specs.append(dict(
                            w=w, qk=qk, qkr=[f"ka:{bi}", qpk[0], qpk[1]],
                            mask=mask, maskr=maskr, av=av,
                            avr=[f"kvt:{bi}", f"kvto:{bi}"]))
                emit_flash(specs, QK_SCALE_B, (lambda h2=h2: negMB[:, h2:h2 + 1]), "negMB")
            R.op("vector", lambda E: E.tensor_copy(out=acc2[0:33, 0:512], in_=accp[0:33, :]),
                 r=["accp"], w=["acc2a"])
            R.op("vector", lambda E: E.tensor_copy(out=acc2[64:97, 512:1024], in_=accp[64:97, :]),
                 r=["accp"], w=["acc2b"])
            for h2 in range(2):
                hb = 64 * h2
                akey = "acc2a" if h2 == 0 else "acc2b"
                for t in range(4):
                    R.op("tensor", lambda E, h2=h2, hb=hb, t=t: E.transpose(
                        out=auxp[:, t * 33: t * 33 + 33],
                        in_=acc2[hb:hb + 33, h2 * 512 + t * 128: h2 * 512 + (t + 1) * 128],
                        identity=i128f[hb:hb + 33, hb:hb + 33]),
                        r=[akey, "i128f"], w=["auxp"])
                for t in range(4):
                    gt = qcol0 // 128 + t
                    R.op("vector", lambda E, t=t: E.reciprocal(
                        out=recb[:, t:t + 1], in_=auxp[:, t * 33 + 32: t * 33 + 33]),
                        r=["auxp"], w=[f"recb:{t}"])
                    R.op("vector", lambda E, t=t, gt=gt, h2=h2: E.tensor_scalar(
                        out=ytok[:, gt * 64 + 32 * h2: gt * 64 + 32 * h2 + 32],
                        in0=auxp[:, t * 33: t * 33 + 32],
                        scalar1=recb[:, t:t + 1], scalar2=None, op0=ALU.mult),
                        r=["auxp", f"recb:{t}"], w=[f"ytok:{gt}:{h2}"])

    R.op("sync", lambda E: E.dma_start(out=ygin[:], in_=ytok[:]),
         r=[f"ytok:{gt}:{h2}" for gt in range(128) for h2 in range(2)], w=["ygin"], dma=True)
    R.op("gpsimd", lambda E: E.collective_compute(
        "AllGather", ALU.bypass, replica_groups=[list(range(8))],
        ins=[ygin[:]], outs=[ygout[:]]),
        r=["ygin"], w=["ygout"])
    R.barrier()
    ygv = ygout.rearrange("(s p) (c t) -> (s p c) t", p=128, t=1024)
    for s in range(8):
        R.op("gpsimd", lambda E, s=s: E.indirect_dma_start(
            out=ytmp[:, s * 1024:(s + 1) * 1024],
            out_offset=None,
            in_=ygv[:, :],
            in_offset=IndirectOffsetOnAxis(ap=idx8_s[:, s:s + 1], axis=0)),
            r=["ygout", "idx8_s"], w=[f"ytmp:{s}"], dma=True)
    for s in range(8):
        R.op("vector", lambda E, s=s: E.tensor_copy(
            out=yfull[:].rearrange("p (t c) -> p t c", c=512)[:, :, 64 * s: 64 * s + 64],
            in_=ytmp[:, s * 1024:(s + 1) * 1024].rearrange("p (t d) -> p t d", d=64)),
            r=[f"ytmp:{s}"], w=[f"yfull:{s}"], )

    if stage == "tap_yf":
        o_yf = nc.dram_tensor("o_yf", [128, 8192], F32, kind="ExternalOutput").ap()
        R.op("gpsimd", lambda E: E.dma_start(out=o_yf[:], in_=yfull[:]),
             r=[f"yfull:{s}" for s in range(8)], w=["oyf"], dma=True)

    for u in range(4):
        R.op("sync", lambda E, u=u: E.dma_start(out=xu[:], in_=xscr[:, 4 * u * C: 4 * (u + 1) * C]),
             r=[f"xscr:{u}"], w=[f"xbl:{t}" for t in range(4)], dma=True)
        for t in range(4):
            tt = 4 * u + t
            for yc in range(4):
                slot = (t * 4 + yc) % 2
                bv = Sbf[slot]
                R.op("tensor", lambda E, tt=tt, yc=yc, bv=bv: E.transpose(
                    out=bv[:, 0:128], in_=yfull[:, tt * 512 + yc * 128: tt * 512 + (yc + 1) * 128],
                    identity=i128b[:]),
                    r=[f"yfull:{s}" for s in range(8)] + ["i128b"],
                    w=[f"Sb{3*slot}"])
                R.op("vector", lambda E, t=t, yc=yc, bv=bv: E.tensor_copy(
                    out=yfm[:, yc * 512 + t * 128: yc * 512 + (t + 1) * 128], in_=bv[:, 0:128]),
                    r=[f"Sb{3*slot}"], w=[f"yfm:{yc}"])
        for t in range(4):
            tt = 4 * u + t
            for half in range(2):
                slot = half
                Sp = Sg[slot]
                for yc in range(4):
                    R.op("tensor", lambda E, t=t, half=half, yc=yc, Sp=Sp: E.matmul(
                        out=Sp[:, 0:512],
                        lhsT=yfm[:, yc * 512 + t * 128: yc * 512 + (t + 1) * 128],
                        rhs=wo_s[:, yc * C + half * 512: yc * C + (half + 1) * 512],
                        start=(yc == 0), stop=(yc == 3)),
                        r=[f"yfm:{yc}", "wo_s"], w=[f"Sb{3*slot}"])
                R.op("vector", lambda E, t=t, half=half, tt=tt: E.tensor_scalar(
                    out=lnres[:, 0:512],
                    in0=xu[:, t * C + half * 512: t * C + (half + 1) * 512],
                    scalar1=ln2st[:, 2 * tt: 2 * tt + 1], scalar2=ln2st[:, 2 * tt + 1: 2 * tt + 2],
                    op0=ALU.mult, op1=ALU.add),
                    r=[f"xbl:{t}", f"ln2st:{tt}"], w=["lnres"])
                R.op("vector", lambda E, half=half, Sp=Sp: E.tensor_add(
                    out=xab[:, half * 512:(half + 1) * 512],
                    in0=lnres[:, 0:512], in1=Sp[:, 0:512]),
                    r=["lnres", f"Sb{3*slot}"], w=[f"xab:{half}"])
            R.op("sync", lambda E, tt=tt: E.dma_start(
                out=xascr[:, tt * C:(tt + 1) * C], in_=xab[:]),
                r=["xab:0", "xab:1"], w=[f"xascr:{tt}"], dma=True)

    R.barrier()

    for u in range(4):
        R.op("sync", lambda E, u=u: E.dma_start(out=xu[:], in_=xascr[:, 4 * u * C: 4 * (u + 1) * C]),
             r=[f"xascr:{4*u+t}" for t in range(4)], w=[f"xu:{t}" for t in range(4)], dma=True)
        for t in range(4):
            tt = 4 * u + t
            rms_inv([f"xu:{t}"], lambda t=t: xu[:, t * C:(t + 1) * C])
            R.op("vector", lambda E, t=t: E.tensor_scalar(
                out=sqb[:, 0:512], in0=xu[:, t * C: t * C + 512], scalar1=lnt[:, 2:3],
                scalar2=None, op0=ALU.mult),
                r=[f"xu:{t}", "lnt2"], w=["sqb"])
            R.op("vector", lambda E, t=t: E.tensor_scalar(
                out=silu_s[:, 0:512], in0=xu[:, t * C + 512:(t + 1) * C], scalar1=lnt[:, 2:3],
                scalar2=None, op0=ALU.mult),
                r=[f"xu:{t}", "lnt2"], w=["silu_s"])
            for cch in range(8):
                slot = cch % 2
                bv = Sbf[slot]
                if cch < 4:
                    src_fn = (lambda c=cch: sqb[:, c * 128:(c + 1) * 128])
                    skey2 = "sqb"
                else:
                    src_fn = (lambda c=cch: silu_s[:, (c - 4) * 128:(c - 4) * 128 + 128])
                    skey2 = "silu_s"
                R.op("tensor", lambda E, src_fn=src_fn, bv=bv: E.transpose(
                    out=bv[:, 0:128], in_=src_fn(), identity=i128b[:]),
                    r=[skey2, "i128b"], w=[f"Sb{3*slot}"])
                R.op("vector", lambda E, c=cch, bv=bv, tt=tt: E.tensor_copy(
                    out=xanT[:, c * 2048 + tt * 128: c * 2048 + (tt + 1) * 128], in_=bv[:, 0:128]),
                    r=[f"Sb{3*slot}"], w=[f"xanT:{cch}"])

    for G in range(4):
        slot = G % 2
        Sp = Sg[slot]
        for kc in range(8):
            R.op("tensor", lambda E, G=G, kc=kc, Sp=Sp: E.matmul(
                out=Sp[0:8, 0:512], lhsT=wgt_s[:, kc * 8:(kc + 1) * 8],
                rhs=xanT[:, kc * 2048 + G * 512: kc * 2048 + (G + 1) * 512],
                start=(kc == 0), stop=(kc == 7)),
                r=["wgt_s", f"xanT:{kc}"], w=[f"Sb{3*slot}"])
        R.op("vector", lambda E, G=G, Sp=Sp: E.tensor_scalar(
            out=gatef[0:8, G * 512:(G + 1) * 512], in0=Sp[0:8, 0:512],
            scalar1=ebias_s[0:8, 0:1], scalar2=None, op0=ALU.add),
            r=[f"Sb{3*slot}", "ebias_s"], w=[f"gatef:{G}"])
        for st4 in range(4):
            slot2 = st4 % 2
            R.op("tensor", lambda E, G=G, st4=st4, slot2=slot2: E.transpose(
                out=Sg[slot2][:, 0:8],
                in_=gatef[0:8, G * 512 + st4 * 128: G * 512 + (st4 + 1) * 128],
                identity=i128f[0:8, 0:8]),
                r=[f"gatef:{G}", "i128f"], w=[f"Sb{3*slot2}"])
            R.op("vector", lambda E, slot2=slot2: E.tensor_copy(out=gtop[:], in_=Sg[slot2][:, 0:8]),
                 r=[f"Sb{3*slot2}"], w=["glog"])
            R.op("vector", lambda E: E.max(out=gsc[:, 0:8], in_=gtop[:]),
                 r=["glog"], w=["gm"])
            R.op("vector", lambda E: E.tensor_scalar(out=gsc[:, 8:9], in0=gsc[:, 0:1], scalar1=-1.0,
                                                     scalar2=None, op0=ALU.mult),
                 r=["gm"], w=["gnm"])
            R.op("scalar", lambda E: E.activation(out=gsc[:, 16:24], in_=gtop[:], func=AF.Exp,
                                                  bias=gsc[:, 8:9]),
                 r=["glog", "gnm"], w=["ge"])
            R.op("vector", lambda E: E.tensor_scalar(out=gsc[:, 24:32], in0=gtop[:], scalar1=gsc[:, 1:2],
                                                     scalar2=None, op0=ALU.is_ge),
                 r=["glog", "gm"], w=["gsel"])
            R.op("vector", lambda E: E.tensor_mul(out=gsc[:, 16:24], in0=gsc[:, 16:24], in1=gsc[:, 24:32]),
                 r=["ge", "gsel"], w=["ge"])
            R.op("vector", lambda E: E.tensor_reduce(out=combz[:, 0:1], in_=gsc[:, 16:24],
                                                     axis=AX.X, op=ALU.add),
                 r=["ge"], w=["gz"])
            R.op("vector", lambda E: E.reciprocal(out=combz[:, 1:2], in_=combz[:, 0:1]),
                 r=["gz"], w=["gzr"])
            R.op("vector", lambda E: E.tensor_scalar(out=gtop[:], in0=gsc[:, 16:24], scalar1=combz[:, 1:2],
                                                     scalar2=None, op0=ALU.mult),
                 r=["ge", "gzr"], w=["glog"])
            slot3 = (st4 + 1) % 2
            R.op("tensor", lambda E, slot3=slot3: E.transpose(
                out=Sg[slot3][0:8, 0:128], in_=gtop[:, 0:8], identity=i128f[:, 0:128]),
                r=["glog", "i128f"], w=[f"Sb{3*slot3}"])
            R.op("vector", lambda E, G=G, st4=st4, slot3=slot3: E.tensor_copy(
                out=combf[0:8, G * 512 + st4 * 128: G * 512 + (st4 + 1) * 128],
                in_=Sg[slot3][0:8, 0:128]),
                r=[f"Sb{3*slot3}"], w=[f"combf:{G}"])

    R.barrier()
    for e in range(9):
        sl = e % 2
        wgk, wuk, wdk = f"wg{sl}", f"wu{sl}", f"wd{sl}"
        off = e * 12288
        R.op("gpsimd", lambda E, off=off, sl=sl: E.dma_start(
            out=wg_s[sl][:], in_=wexp[:, off: off + 4096]), w=[wgk], dma=True)
        R.op("gpsimd", lambda E, off=off, sl=sl: E.dma_start(
            out=wu_s[sl][:], in_=wexp[:, off + 4096: off + 8192]), w=[wuk], dma=True)
        R.op("gpsimd", lambda E, off=off, sl=sl: E.dma_start(
            out=wd_s[sl][:], in_=wexp[:, off + 8192: off + 12288]), w=[wdk], dma=True)
        for kc in range(8):
            R.op("vector", lambda E, kc=kc, sl=sl: E.tensor_scalar(
                out=wg_s[sl][:, kc * 512:(kc + 1) * 512], in0=wg_s[sl][:, kc * 512:(kc + 1) * 512],
                scalar1=wln2c_s[:, kc:kc + 1], scalar2=None, op0=ALU.mult),
                r=[wgk, "wln2c_s"], w=[wgk])
            R.op("vector", lambda E, kc=kc, sl=sl: E.tensor_scalar(
                out=wu_s[sl][:, kc * 512:(kc + 1) * 512], in0=wu_s[sl][:, kc * 512:(kc + 1) * 512],
                scalar1=wln2c_s[:, kc:kc + 1], scalar2=None, op0=ALU.mult),
                r=[wuk, "wln2c_s"], w=[wuk])
        for G in range(4):
            if e > 0:
                R.op("tensor", lambda E, e=e, G=G: E.matmul(
                    out=S0[:, 512:1024], lhsT=sel8_s[:, (e - 1) * 128: e * 128],
                    rhs=combf[0:8, G * 512:(G + 1) * 512], start=True, stop=True),
                    r=["sel8_s", f"combf:{G}"], w=["m_bc"])
            for ic in range(4):
                par = ic % 2
                gcol = 0 if par == 0 else 1024
                gk, uk2 = f"m_g{par}", f"m_u{par}"
                sqk = f"sqm{par}"
                for kc in range(8):
                    R.op("tensor", lambda E, G=G, ic=ic, kc=kc, sl=sl, gcol=gcol: E.matmul(
                        out=S0[:, gcol:gcol + 512], lhsT=wg_s[sl][:, kc * 512 + ic * 128: kc * 512 + (ic + 1) * 128],
                        rhs=xanT[:, kc * 2048 + G * 512: kc * 2048 + (G + 1) * 512],
                        start=(kc == 0), stop=(kc == 7)),
                        r=[wgk, f"xanT:{kc}"], w=[gk])
                    R.op("tensor", lambda E, G=G, ic=ic, kc=kc, sl=sl, gcol=gcol: E.matmul(
                        out=S1[:, gcol:gcol + 512], lhsT=wu_s[sl][:, kc * 512 + ic * 128: kc * 512 + (ic + 1) * 128],
                        rhs=xanT[:, kc * 2048 + G * 512: kc * 2048 + (G + 1) * 512],
                        start=(kc == 0), stop=(kc == 7)),
                        r=[wuk, f"xanT:{kc}"], w=[uk2])
                R.op("scalar", lambda E, gcol=gcol: E.activation(
                    out=silu_s[:], in_=S0[:, gcol:gcol + 512], func=AF.Silu),
                     r=[gk], w=["silu_s"])
                R.op("vector", lambda E, gcol=gcol, par=par: E.scalar_tensor_tensor(
                    out=sqb[:] if par == 0 else lnres[:, 0:512].bitcast(BF16)[:, 0:512],
                    in0=S1[:, gcol:gcol + 512], scalar=1.0, in1=silu_s[:],
                    op0=ALU.mult, op1=ALU.mult),
                    r=[uk2, "silu_s"], w=[sqk])
                mm_in = (lambda par=par: sqb[:] if par == 0 else lnres[:, 0:512].bitcast(BF16)[:, 0:512])
                if e > 0:
                    R.op("vector", lambda E, ic=ic, mm_in=mm_in: E.tensor_mul(
                        out=mid_s[:, ic * 512:(ic + 1) * 512], in0=mm_in(), in1=S0[:, 512:1024]),
                        r=[sqk, "m_bc"], w=[f"mid:{ic}"])
                else:
                    R.op("vector", lambda E, ic=ic, mm_in=mm_in: E.tensor_copy(
                        out=mid_s[:, ic * 512:(ic + 1) * 512], in_=mm_in()),
                        r=[sqk], w=[f"mid:{ic}"])
            for tsub in range(4):
                for half in range(2):
                    k8 = tsub * 2 + half
                    dps = accp if k8 % 2 == 0 else auxp
                    skey3 = "accp" if k8 % 2 == 0 else "auxp"
                    for ic in range(4):
                        R.op("tensor", lambda E, tsub=tsub, half=half, ic=ic, sl=sl,
                             dps=dps: E.matmul(
                            out=dps[:, 0:512],
                            lhsT=mid_s[:, ic * 512 + tsub * 128: ic * 512 + (tsub + 1) * 128],
                            rhs=wd_s[sl][:, ic * C + half * 512: ic * C + (half + 1) * 512],
                            start=(ic == 0), stop=(ic == 3)),
                            r=[f"mid:{ic}", wdk], w=[skey3])
                    tt = G * 4 + tsub
                    mkey = f"moe:{tt}:{half}"
                    if e == 0:
                        R.op("vector", lambda E, tt=tt, half=half, dps=dps: E.tensor_copy(
                            out=moeacc[:, tt * C + half * 512: tt * C + (half + 1) * 512],
                            in_=dps[:, 0:512]),
                            r=[skey3], w=[mkey])
                    else:
                        R.op("vector", lambda E, tt=tt, half=half, dps=dps: E.tensor_add(
                            out=moeacc[:, tt * C + half * 512: tt * C + (half + 1) * 512],
                            in0=moeacc[:, tt * C + half * 512: tt * C + (half + 1) * 512],
                            in1=dps[:, 0:512]),
                            r=[skey3, mkey], w=[mkey])
            if e == 8:
                u = G
                R.op("sync", lambda E, u=u: E.dma_start(
                    out=xu[:], in_=xscr[:, 4 * u * C: 4 * (u + 1) * C]),
                    w=[f"xu:{t}" for t in range(4)], dma=True)
                for t in range(4):
                    tt = 4 * u + t
                    R.op("sync", lambda E, tt=tt: E.dma_start(
                        out=xau[:], in_=xascr[:, tt * C:(tt + 1) * C]),
                        r=[f"xascr:{tt}"], w=["xaf"], dma=True)
                    R.op("vector", lambda E, tt=tt: E.tensor_add(
                        out=xau[:], in0=xau[:], in1=moeacc[:, tt * C:(tt + 1) * C]),
                        r=["xaf", f"moe:{tt}:0", f"moe:{tt}:1"], w=["xaf"])
                    ln_stats(["xaf"], lambda: xau[:])
                    R.op("vector", lambda E: E.tensor_scalar(
                        out=lnres[:], in0=xau[:, 0:C],
                        scalar1=lnt[:, 0:1], scalar2=lnt[:, 1:2], op0=ALU.mult, op1=ALU.add),
                        r=["xaf", "lnt0", "lnt1"], w=["lnres"])
                    R.op("vector", lambda E, t=t: E.tensor_add(
                        out=xu[:, t * C:(t + 1) * C], in0=xu[:, t * C:(t + 1) * C], in1=lnres[:]),
                        r=["lnres", f"xu:{t}"], w=[f"xu:{t}"])
                R.op("sync", lambda E, u=u: E.dma_start(
                    out=o_x[:, 4 * u * C: 4 * (u + 1) * C], in_=xu[:]),
                    r=[f"xu:{t}" for t in range(4)], w=[f"oxd:{u}"], dma=True)

    if taps:
        o_moe = nc.dram_tensor("o_moe", [128, 16 * C], F32, kind="ExternalOutput").ap()
        R.op("gpsimd", lambda E: E.dma_start(out=o_moe[:], in_=moeacc[:]),
             r=[f"moe:{tt}:{h}" for tt in range(16) for h in range(2)], w=["omoe"], dma=True)
        o_xan = nc.dram_tensor("o_xan", [128, 16384], F32, kind="ExternalOutput").ap()
        R.op("gpsimd", lambda E: E.dma_start(out=o_xan[:], in_=xanT[:]),
             r=[f"xanT:{cch}" for cch in range(8)], w=["oxan"], dma=True)
        o_comb = nc.dram_tensor("o_comb", [8, 2048], F32, kind="ExternalOutput").ap()
        R.op("gpsimd", lambda E: E.dma_start(out=o_comb[:], in_=combf[:]),
             r=[f"combf:{G}" for G in range(4)], w=["ocomb"], dma=True)
        o_xs = nc.dram_tensor("o_xs", [128, 16 * C], F32, kind="ExternalOutput").ap()
        o_xa = nc.dram_tensor("o_xa", [128, 16 * C], F32, kind="ExternalOutput").ap()
        o_q = nc.dram_tensor("o_q", [128, 16384], F32, kind="ExternalOutput").ap()
        o_qf = nc.dram_tensor("o_qf", [128, 16384], F32, kind="ExternalOutput").ap()
        R.op("gpsimd", lambda E: E.dma_start(out=o_qf[:], in_=qgin[:]),
             r=["qgin"], w=["oqf"], dma=True)
        o_cc = nc.dram_tensor("o_cc", [1280, 2048], F32, kind="ExternalOutput").ap()
        R.op("gpsimd", lambda E: E.dma_start(out=o_cc[:], in_=ccout[:]),
             r=["ccout"], w=["occ"], dma=True)
        o_yg = nc.dram_tensor("o_yg", [1024, 8192], F32, kind="ExternalOutput").ap()
        R.op("gpsimd", lambda E: E.dma_start(out=o_yg[:], in_=ygout[:]),
             r=["ygout"], w=["oyg"], dma=True)
        R.op("sync", lambda E: E.dma_start(out=o_xs[:], in_=xscr[:]),
             r=[f"xscr:{u}" for u in range(4)], w=["oxs"], dma=True)
        R.op("sync", lambda E: E.dma_start(out=o_xa[:], in_=xascr[:]),
             r=[f"xascr:{tt}" for tt in range(16)], w=["oxa"], dma=True)
        R.op("gpsimd", lambda E: E.dma_start(out=o_q[:], in_=q_all[:]),
             r=[f"q_all:{s}" for s in range(8)], w=["oq"], dma=True)
        R.op("gpsimd", lambda E: E.nop(), r=["oxs", "oxa", "oq", "oqf", "occ", "oyg", "omoe", "oxan", "ocomb"])
    R.op("gpsimd", lambda E: E.nop(), r=[f"oxd:{u}" for u in range(4)])

    R.emit(nc)
    return nc



def _build_wqkv(W, bias):
    Wp = np.zeros((1024, 5120), dtype=np.float32)
    bp = np.zeros(5120, dtype=np.float32)

    def put(dst, src64):
        Wp[:, dst:dst + 64] = W[:, src64:src64 + 64]
        bp[dst:dst + 64] = bias[src64:src64 + 64]

    for j in range(16):
        put(j * 128, 192 * j)
    for j in range(8):
        put(2048 + j * 128, 192 * (2 * j) + 64)
        put(2048 + j * 128 + 64, 192 * (2 * j + 1) + 64)
        put(3072 + j * 128, 192 * (2 * j + 1) + 64)
        put(3072 + j * 128 + 64, 192 * (2 * j) + 64)
        put(4096 + j * 128, 192 * (2 * j) + 128)
        put(4096 + j * 128 + 64, 192 * (2 * j + 1) + 128)
    return Wp, bp


def _chunkmajor(w, nk=8):
    cin, f = w.shape
    assert cin == nk * 128
    return np.ascontiguousarray(w.reshape(nk, 128, f).transpose(1, 0, 2).reshape(128, nk * f))


def _rope_tables(positions, nrows, rope_rows):
    t = np.asarray(positions, dtype=np.float32)
    d = 32
    freqs = (1.0 / THETA ** (np.arange(0, d, 2, dtype=np.float32) / d)).astype(np.float32)
    ang = np.outer(freqs, t)
    cosv, sinv = np.cos(ang).astype(np.float32), np.sin(ang).astype(np.float32)
    Ct = np.ones((nrows, len(t)), dtype=np.float32)
    St = np.zeros((nrows, len(t)), dtype=np.float32)
    for r0 in rope_rows:
        for i in range(16):
            Ct[r0 + 2 * i] = cosv[i]
            Ct[r0 + 2 * i + 1] = cosv[i]
            St[r0 + 2 * i] = -sinv[i]
            St[r0 + 2 * i + 1] = sinv[i]
    return Ct, St


def prep_core_inputs(c, inputs):
    blocks = [c, 15 - c]
    x = np.asarray(inputs["L0_attn_inputs"])
    rows = np.concatenate([x[b, 512 * j:512 * (j + 1)] for b in (0, 1) for j in blocks])
    xrows = np.ascontiguousarray(rows.reshape(16, 128, 1024).transpose(1, 0, 2).reshape(128, 16 * 1024))

    Wp, bp = _build_wqkv(np.asarray(inputs["L0_attn_qkv_kernel"]),
                         np.asarray(inputs["L0_attn_qkv_bias"]).reshape(3072))
    wqkv = _chunkmajor(Wp)
    qkvb = np.ascontiguousarray(bp.reshape(40, 128).T)

    i128 = np.eye(128, dtype=np.float32)
    maskTi = np.triu(np.ones((128, 128), dtype=np.float32))
    maskTx = np.triu(np.ones((128, 128), dtype=np.float32), k=1)
    blk2 = np.zeros((128, 2), dtype=np.float32)
    blk2[0:64, 0] = 1.0
    blk2[64:128, 1] = 1.0
    ones1 = np.ones((1, 128), dtype=np.float32)
    cvals_np = np.zeros((128, 8), dtype=np.float32)
    for ci, cv in enumerate(CONST_VALS):
        cvals_np[:, ci] = np.float32(cv)
    onescol = np.ones((128, 1), dtype=np.float32)
    grp4 = np.zeros((128, 4), dtype=np.float32)
    for g in range(4):
        grp4[32 * g:32 * (g + 1), g] = 1.0
    sel8 = np.zeros((8, 8 * 128), dtype=np.float32)
    for e in range(8):
        sel8[e, e * 128:(e + 1) * 128] = 1.0
    ropesc = np.zeros((128, 4 * 128), dtype=np.float32)
    for g in range(4):
        for r in range(32):
            ropesc[32 * g + r, g * 128 + 32 + r] = 1.0
            ropesc[32 * g + r, g * 128 + 64 + r] = 1.0

    Wq = np.asarray(inputs["L0_ffn_Wq"])
    permq = []
    for h in range(16):
        if h % 2 == 0:
            permq += list(range(64 * h, 64 * h + 64))
        else:
            permq += list(range(64 * h + 32, 64 * h + 64)) + list(range(64 * h, 64 * h + 32))
    wqp = _chunkmajor(Wq[:, permq])
    wkva = _chunkmajor(np.asarray(inputs["L0_ffn_Wkva"]))
    wln1c = np.ascontiguousarray(np.asarray(inputs["L0_ffn_w_ln1"]).reshape(8, 128).T)
    wln2c = np.ascontiguousarray(np.asarray(inputs["L0_ffn_w_ln2"]).reshape(8, 128).T)
    Wkvb = np.asarray(inputs["L0_ffn_Wkvb"])
    wkvbka = np.zeros((128, 128), dtype=np.float32)
    wkvbka[:, 0:32] = Wkvb[:, 32 * (2 * c): 32 * (2 * c) + 32]
    wkvbka[:, 96:128] = Wkvb[:, 32 * (2 * c + 1): 32 * (2 * c + 1) + 32]
    wkvbtok = np.zeros((128, 66), dtype=np.float32)
    wkvbtok[:, 0:32] = Wkvb[:, 32 * (2 * c): 32 * (2 * c) + 32]
    wkvbtok[:, 33:65] = Wkvb[:, 32 * (2 * c + 1): 32 * (2 * c + 1) + 32]
    idx8 = np.zeros((128, 8), dtype=np.int32)
    for s in range(8):
        idx8[:, s] = s * 1024 + np.arange(128) * 8 + c
    Wo = np.asarray(inputs["L0_ffn_Wo"])
    woe_rows = np.concatenate([Wo[64 * h: 64 * h + 32] for h in range(16)])
    woe = _chunkmajor(woe_rows, nk=4)
    wgt = _chunkmajor(np.asarray(inputs["L0_ffn_Wgate"]))
    ebias = np.asarray(inputs["L0_ffn_expert_bias"]).reshape(8, 1).astype(np.float32)

    positions = np.concatenate([np.arange(512 * j, 512 * (j + 1)) for b in (0, 1) for j in blocks])
    cqt, sqt = _rope_tables(positions, 128, [32, 64])
    ckt, skt = _rope_tables(positions, 32, [0])

    Wg = np.asarray(inputs["L0_ffn_Wg"])
    Wu = np.asarray(inputs["L0_ffn_Wu"])
    Wd = np.asarray(inputs["L0_ffn_Wd"])
    parts = []
    for e in range(9):
        if e == 0:
            g_, u_, d_ = (np.asarray(inputs["L0_ffn_sWg"])[0], np.asarray(inputs["L0_ffn_sWu"])[0],
                          np.asarray(inputs["L0_ffn_sWd"])[0])
        else:
            g_, u_, d_ = Wg[e - 1], Wu[e - 1], Wd[e - 1]
        parts += [_chunkmajor(g_), _chunkmajor(u_), _chunkmajor(d_, nk=4)]
    wexp = np.concatenate(parts, axis=1)

    return dict(
        xrows=xrows.astype(np.float32), wqkv=wqkv.astype(np.float32),
        qkvb=qkvb.astype(np.float32), i128bf=i128, maskTi=maskTi, maskTx=maskTx,
        blk2=blk2, ones1=ones1, cvals=cvals_np, onescol=onescol, grp4=grp4,
        sel8=sel8, ropesc=ropesc, wqp=wqp.astype(np.float32),
        wkva=wkva.astype(np.float32), wln1c=wln1c.astype(np.float32),
        wln2c=wln2c.astype(np.float32), wkvbka=wkvbka, wkvbtok=wkvbtok,
        woe=woe.astype(np.float32), wgt=wgt.astype(np.float32), ebias=ebias,
        idx8=idx8,
        cq=cqt, sq_t=sqt, ck=ckt, sk_t=skt, wexp=wexp.astype(np.float32))


def unpack_x(o_x):
    return np.ascontiguousarray(o_x.reshape(128, 16, 1024).transpose(1, 0, 2).reshape(2048, 1024))


def kernel(**inputs):
    import os
    stage = os.environ.get("MK_STAGE", "full")
    nc = build_nc(0, stage=stage)
    in_maps = [prep_core_inputs(c, inputs) for c in range(8)]
    res = run_bass_kernel_spmd(nc, in_maps, core_ids=list(range(8)))
    out = np.zeros((2, 8192, 1024), dtype=np.float32)
    for c in range(8):
        rows = unpack_x(np.asarray(res.results[c]["o_x"]))
        blocks = [c, 15 - c]
        i = 0
        for b in (0, 1):
            for j in blocks:
                out[b, 512 * j:512 * (j + 1)] = rows[i * 512:(i + 1) * 512]
                i += 1
    return out


# version 09 (final)
# speedup vs baseline: 1.1312x; 1.0351x over previous
import numpy as np
import ml_dtypes
import concourse.bass as bass
import concourse.mybir as mybir
from concourse.bass_utils import run_bass_kernel_spmd
from concourse.bass_core_rust import IndirectOffsetOnAxis

F32 = mybir.dt.float32
BF16 = mybir.dt.bfloat16
AF = mybir.ActivationFunctionType
ALU = mybir.AluOpType
AX = mybir.AxisListType

C = 1024
BLK = 512
QK_SCALE_A = 1.0 / float(np.sqrt(2.0 * C))
QK_SCALE_B = 0.125
EPS = 1e-5
NEG_BIG = -3.0e38
THETA = 100000.0
CONST_VALS = [1e-5, 1e-30, float(np.log(QK_SCALE_A)), float(np.log(QK_SCALE_B))]

ENGINES = ("tensor", "vector", "scalar", "gpsimd", "sync")


class Rec:
    def __init__(self):
        self.ops = []
        self.barriers = []

    def op(self, engine, fn, r=(), w=(), dma=False):
        assert engine in ENGINES
        self.ops.append((engine, fn, tuple(r), tuple(w), dma))

    def barrier(self):
        self.barriers.append(len(self.ops))

    def emit(self, nc):
        n = len(self.ops)
        stream, op_pos = [], []
        sidx = {}
        for (eng, fn, r, w, dma) in self.ops:
            st = ("d", eng) if dma else ("e", eng)
            stream.append(st)
            op_pos.append(sidx.get(st, 0))
            sidx[st] = op_pos[-1] + 1

        last_w, readers = {}, {}
        deps = [set() for _ in range(n)]
        bar_iter = iter(self.barriers + [n + 1])
        next_bar = next(bar_iter)
        bar_deps = []
        bar_pending = {}
        last_of_stream = {}
        for i, (eng, fn, r, w, dma) in enumerate(self.ops):
            while i >= next_bar:
                bar_deps = list(last_of_stream.values())
                bar_pending = {}
                next_bar = next(bar_iter)
            st = stream[i]
            if bar_deps and st not in bar_pending:
                for j in bar_deps:
                    if j != i:
                        deps[i].add(j)
                bar_pending[st] = True
            for k in r:
                if k in last_w:
                    deps[i].add(last_w[k])
            for k in w:
                if k in last_w:
                    deps[i].add(last_w[k])
                for j in readers.get(k, ()):
                    deps[i].add(j)
            for k in r:
                readers.setdefault(k, []).append(i)
            for k in w:
                last_w[k] = i
                readers[k] = []
            last_of_stream[st] = i

        def needs_edge(i, j):
            return not (stream[i] == stream[j] == ("e", "tensor"))

        need_inc = set()
        for j in range(n):
            for i in deps[j]:
                if needs_edge(i, j):
                    need_inc.add(i)
        for i in range(n):
            if stream[i][0] == "d":
                need_inc.add(i)

        NSLOT = 4
        rank, cnt = {}, {}
        dslot, duse = {}, {}
        for i in range(n):
            st = stream[i]
            if st[0] == "d":
                k = op_pos[i]
                dslot[i] = k % NSLOT
                duse[i] = k // NSLOT
            elif i in need_inc:
                cnt[st] = cnt.get(st, 0) + 1
                rank[i] = cnt[st]

        per_eng = {e: [] for e in ENGINES}
        for i in range(n):
            per_eng[self.ops[i][0]].append(i)

        import contextlib
        with contextlib.ExitStack() as stack:
            esem = {e: stack.enter_context(nc.semaphore(f"S_{e}")) for e in ENGINES}
            dsem = {e: [stack.enter_context(nc.semaphore(f"D_{e}_{s}")) for s in range(NSLOT)]
                    for e in ENGINES}
            block = stack.enter_context(nc.Block())

            def sem_of(st, i):
                if st[0] == "d":
                    return dsem[st[1]][dslot[i]], 16 * (duse[i] + 1)
                return esem[st[1]], rank[i]

            def make_body(eng):
                idxs = per_eng[eng]

                def body(E):
                    water = {}
                    for j in idxs:
                        _, fn, r, w, dma = self.ops[j]
                        waits = {}
                        for i in deps[j]:
                            if i not in need_inc or not needs_edge(i, j):
                                continue
                            sem, v = sem_of(stream[i], i)
                            if waits.get(sem.name, (None, 0))[1] < v:
                                waits[sem.name] = (sem, v)
                        if dma:
                            sem = dsem[eng][dslot[j]]
                            v = 16 * duse[j]
                            if v > 0 and waits.get(sem.name, (None, 0))[1] < v:
                                waits[sem.name] = (sem, v)
                        for name in sorted(waits):
                            sem, v = waits[name]
                            if water.get(name, 0) >= v:
                                continue
                            E.wait_ge(sem, v)
                            water[name] = v
                        inst = fn(E)
                        if j in need_inc:
                            st = stream[j]
                            if st[0] == "d":
                                inst.then_inc(dsem[st[1]][dslot[j]], 16)
                            else:
                                inst.then_inc(esem[st[1]], 1)
                return body

            for eng in ENGINES:
                getattr(block, eng)(make_body(eng))



def build_nc(core, stage="full", taps=False):
    nc = bass.Bass("TRN2", target_bir_lowering=False)
    R = Rec()

    def din(name, shape):
        return nc.dram_tensor(name, shape, F32, kind="ExternalInput").ap()

    xrows = din("xrows", [128, 16 * C])
    wqkv = din("wqkv", [128, 8 * 5120])
    qkvb = din("qkvb", [128, 40])
    i128bf = din("i128bf", [128, 128])
    maskTi = din("maskTi", [128, 128])
    maskTx = din("maskTx", [128, 128])
    blk2 = din("blk2", [128, 2])
    ones1 = din("ones1", [1, 128])
    cvals = din("cvals", [128, 8])
    onescol = din("onescol", [128, 1])
    grp4 = din("grp4", [128, 4])
    sel8 = din("sel8", [8, 8 * 128])
    ropesc = din("ropesc", [128, 4 * 128])
    wqp = din("wqp", [128, 8 * C])
    wkva = din("wkva", [128, 8 * 160])
    wln1c = din("wln1c", [128, 8])
    wln2c = din("wln2c", [128, 8])
    wkvbka = din("wkvbka", [128, 128])
    wkvbtok = din("wkvbtok", [128, 66])
    idx8 = nc.dram_tensor("idx8", [128, 8], mybir.dt.int32, kind="ExternalInput").ap()
    woe = din("woe", [128, 4 * C])
    wgt = din("wgt", [128, 8 * 8])
    ebias = din("ebias", [8, 1])
    cq = din("cq", [128, 2048])
    sq_t = din("sq_t", [128, 2048])
    ck = din("ck", [32, 2048])
    sk_t = din("sk_t", [32, 2048])
    wexp = din("wexp", [128, 9 * 12288])

    o_x = nc.dram_tensor("o_x", [128, 16 * C], F32, kind="ExternalOutput").ap()
    xscr = nc.dram_tensor("xscr", [128, 16 * C], F32).ap()
    xascr = nc.dram_tensor("xascr", [128, 16 * C], F32).ap()
    ccin = nc.dram_tensor("ccin", [160, 2048], BF16).ap()
    ccout = nc.dram_tensor("ccout", [8 * 160, 2048], BF16, addr_space="Shared").ap()
    qgin = nc.dram_tensor("qgin", [128, 16384], BF16).ap()
    qgout = nc.dram_tensor("qgout", [1024, 16384], BF16, addr_space="Shared").ap()
    ygin = nc.dram_tensor("ygin", [128, 8192], BF16).ap()
    ygout = nc.dram_tensor("ygout", [1024, 8192], BF16, addr_space="Shared").ap()

    ARENA = 206000
    _ar = nc.bump_sbuf(ARENA)
    ARENA_BASE = _ar[0]
    alloc_at = nc.alloc_sbuf_tensor_at
    cur = [0]

    def place(name, shape, dtype, at=None):
        nbytes = int(np.prod(shape[1:])) * (4 if dtype in (F32, mybir.dt.int32) else 2)
        if at is None:
            at = cur[0]
            cur[0] = at + ((nbytes + 63) // 64) * 64
        return alloc_at(name, shape, dtype, offset=ARENA_BASE + at)

    i128b = place("i128b", [128, 128], BF16)
    i128f = place("i128f", [128, 128], F32)
    mTi = place("mTi", [128, 128], BF16)
    mTx = place("mTx", [128, 128], BF16)
    blk2_s = place("blk2_s", [128, 2], BF16)
    ones1_s = place("ones1_s", [1, 128], BF16)
    cvals_s = place("cvals_s", [128, 8], F32)
    onescol_s = place("onescol_s", [128, 1], BF16)
    grp4_s = place("grp4_s", [128, 4], BF16)
    sel8_s = place("sel8_s", [8, 8 * 128], BF16)
    ropesc_s = place("ropesc_s", [128, 4 * 128], BF16)
    stats = place("stats", [128, 12], F32)
    mv = place("mv", [128, 2], F32)
    lnt = place("lnt", [128, 4], F32)
    lnres = place("lnres", [128, C], F32)
    recb = place("recb", [128, 4], F32)
    mstab = place("mstab", [128, 8], F32)
    mstabb = place("mstabb", [1, 16], BF16)
    negM = place("negM", [128, 1], F32)
    negMB = place("negMB", [128, 16], F32)
    qkmax2 = place("qkmax2", [128, 24], F32)
    sqb = place("sqb", [128, BLK], BF16)
    ln2st = place("ln2st", [128, 32], F32)
    Pb = [place(f"Pb{i}", [128, 1536], BF16) for i in range(2)]
    xu = place("xu", [128, 4 * C], F32)
    qn16 = place("qn16", [128, 16], F32)
    kn16 = place("kn16", [128, 16], F32)
    gtop = place("gtop", [128, 8], F32)
    gsc = place("gsc", [128, 32], F32)
    combf = place("combf", [8, 4 * BLK], BF16)
    combz = place("combz", [128, 2], F32)
    wgt_s = place("wgt_s", [128, 8 * 8], BF16)
    ebias_s = place("ebias_s", [8, 1], F32)
    wln2c_s = place("wln2c_s", [128, 8], F32)

    for ci, cv in enumerate(CONST_VALS):
        nc.const_aps.register(F32, float(cv), cvals_s[:, ci:ci + 1])

    PERSIST_END = cur[0]
    assert PERSIST_END < 45000, PERSIST_END

    cur[0] = PERSIST_END
    wqkv_s = place("wqkv_s", [128, 8 * 5120], BF16)
    qkv = place("qkv", [128, 40 * BLK], BF16)
    x1T = place("x1T", [128, 8 * BLK], BF16)
    x1u = place("x1u", [128, 4 * C], BF16)
    vtok = place("vtok", [128, 4 * 16 * 65], BF16)
    attnA = place("attnA", [128, 4 * C], F32)
    qkvb_s = place("qkvb_s", [128, 40], F32)
    accs = place("accs", [128, BLK], F32)
    assert cur[0] <= ARENA, cur[0]

    cur[0] = PERSIST_END
    latent = place("latent", [128, 16384], BF16)
    krope4 = place("krope4", [128, 4096], BF16)
    wkvbka_s = place("wkvbka_s", [128, 128], BF16)
    wkvbtok_s = place("wkvbtok_s", [128, 66], BF16)
    wo_s = place("wo_s", [128, 4 * C], BF16)
    wln1c_s = place("wln1c_s", [128, 8], F32)
    idx8_s = place("idx8_s", [128, 8], mybir.dt.int32)
    BSH_END = cur[0]

    cur[0] = BSH_END
    x2nT = place("x2nT", [128, 8 * 2048], BF16)
    q_fm = place("q_fm", [128, 8 * 2048], BF16)
    wq_s = place("wq_s", [128, 8 * C], BF16)
    wkva_s = place("wkva_s", [128, 8 * 160], BF16)
    cqs = place("cqs", [128, 2048], BF16)
    sqs = place("sqs", [128, 2048], BF16)
    qrs = place("qrs", [128, 2 * 2048], BF16)
    CKVR_AT = cur[0]
    ckvr = place("ckvr", [128, 2048], BF16)
    kropeo_t = place("kropeo_t", [32, 2048], BF16)
    cks_t = place("cks_t", [32, 2048], BF16)
    sks_t = place("sks_t", [32, 2048], BF16)
    kropeo = kropeo_t[0:32, :]
    cks = cks_t[0:32, :]
    sks = sks_t[0:32, :]
    x2f = place("x2f", [128, C], F32, at=CKVR_AT)
    assert cur[0] <= ARENA, cur[0]

    cur[0] = BSH_END
    q_all = place("q_all", [128, 16384], BF16)
    KA0A = cur[0]
    ka_own = place("ka_own", [128, 2 * 16 * 512], BF16)
    ytmp = place("ytmp", [128, 8192], BF16, at=KA0A)
    kvt_own = place("kvt_own", [128, 2 * 16 * 264], BF16)
    ytok = place("ytok", [128, 128 * 64], BF16)
    YTOK0 = cur[0] - 16384
    acc2 = place("accs2", [128, 1024], F32)
    yfm = place("yfm", [128, 4 * BLK], BF16)
    xab = place("xab", [128, C], F32)
    qpad_he = place("qpad_he", [128, 2 * 512], BF16)
    qpad_ho = place("qpad_ho", [128, 2 * 512], BF16)
    yfull = place("yfull", [128, 16 * BLK], BF16, at=YTOK0)
    assert cur[0] <= ARENA, cur[0]

    cur[0] = PERSIST_END
    xanT = place("xanT", [128, 8 * 2048], BF16)
    moeacc = place("moeacc", [128, 16 * C], F32)
    wg_s = [place(f"wg_s{i}", [128, 8 * BLK], BF16) for i in range(2)]
    wu_s = [place(f"wu_s{i}", [128, 8 * BLK], BF16) for i in range(2)]
    wd_s = [place(f"wd_s{i}", [128, 4 * C], BF16) for i in range(2)]
    mid_s = place("mid_s", [128, 4 * BLK], BF16)
    silu_s = place("silu_s", [128, BLK], BF16)
    gatef = place("gatef", [8, 4 * BLK], F32)
    xau = place("xau", [128, C], F32)
    assert cur[0] <= ARENA, cur[0]

    ps = nc.alloc_psum_tensor
    S0 = ps("S0", [128, 1536], F32)
    S1 = ps("S1", [128, 1536], F32)
    Sg = [S0, S1]
    Sbf = [S0[:].bitcast(BF16), S1[:].bitcast(BF16)]
    accp = ps("accp", [128, 512], F32)
    auxp = ps("auxp", [128, 512], F32)
    SB = [Sg[j // 3][:, (j % 3) * 512:(j % 3 + 1) * 512] for j in range(6)]
    PBk = [Pb[j // 3][:, (j % 3) * 512:(j % 3 + 1) * 512] for j in range(6)]

    R.op("gpsimd", lambda E: E.dma_start(out=wqkv_s[:], in_=wqkv[:]), w=["wqkv_s"], dma=True)
    R.op("sync", lambda E: E.dma_start(out=qkvb_s[:], in_=qkvb[:]), w=["qkvb_s"], dma=True)
    R.op("gpsimd", lambda E: E.dma_start(out=i128b[:], in_=i128bf[:]), w=["i128b"], dma=True)
    R.op("sync", lambda E: E.dma_start(out=i128f[:], in_=i128bf[:]), w=["i128f"], dma=True)
    R.op("gpsimd", lambda E: E.dma_start(out=mTi[:], in_=maskTi[:]), w=["mTi"], dma=True)
    R.op("gpsimd", lambda E: E.dma_start(out=mTx[:], in_=maskTx[:]), w=["mTx"], dma=True)
    R.op("gpsimd", lambda E: E.dma_start(out=blk2_s[:], in_=blk2[:]), w=["blk2_s"], dma=True)
    R.op("gpsimd", lambda E: E.dma_start(out=ones1_s[:], in_=ones1[:]), w=["ones1_s"], dma=True)
    R.op("sync", lambda E: E.dma_start(out=cvals_s[:], in_=cvals[:]), w=["cvals_s"], dma=True)
    R.op("gpsimd", lambda E: E.dma_start(out=onescol_s[:], in_=onescol[:]), w=["onescol_s"], dma=True)
    R.op("gpsimd", lambda E: E.dma_start(out=grp4_s[:], in_=grp4[:]), w=["grp4_s"], dma=True)
    R.op("gpsimd", lambda E: E.dma_start(out=sel8_s[:], in_=sel8[:]), w=["sel8_s"], dma=True)
    R.op("gpsimd", lambda E: E.dma_start(out=ropesc_s[:], in_=ropesc[:]), w=["ropesc_s"], dma=True)
    R.op("vector", lambda E: E.memset(mstab[:], 0.0),
         w=["ms0", "ms1", "ms23", "ms45", "ms6", "ms7"])

    def ln_stats(src_keys, src_ap_fn):
        R.op("vector", lambda E: E.bn_stats(out=stats[:, 0:6], in_=src_ap_fn()[:, 0:512]),
             r=src_keys, w=["stats"])
        R.op("vector", lambda E: E.bn_stats(out=stats[:, 6:12], in_=src_ap_fn()[:, 512:1024]),
             r=src_keys, w=["stats2"])
        R.op("vector", lambda E: E.bn_aggr(out=mv[:], in_=stats[:]), r=["stats", "stats2"], w=["mv"])
        R.op("scalar", lambda E: E.activation(out=lnt[:, 2:3], in_=mv[:, 1:2], func=AF.Ln, bias=float(EPS)),
             r=["mv", "cvals_s"], w=["lnt2"])
        R.op("scalar", lambda E: E.activation(out=lnt[:, 0:1], in_=lnt[:, 2:3], func=AF.Exp, scale=-0.5),
             r=["lnt2"], w=["lnt0"])
        R.op("vector", lambda E: E.tensor_scalar(out=lnt[:, 1:2], in0=mv[:, 0:1], scalar1=lnt[:, 0:1],
                                                 scalar2=-1.0, op0=ALU.mult, op1=ALU.mult),
             r=["mv", "lnt0"], w=["lnt1"])

    def rms_inv(src_keys, src_ap_fn):
        R.op("vector", lambda E: E.bn_stats(out=stats[:, 0:6], in_=src_ap_fn()[:, 0:512]),
             r=src_keys, w=["stats"])
        R.op("vector", lambda E: E.bn_stats(out=stats[:, 6:12], in_=src_ap_fn()[:, 512:1024]),
             r=src_keys, w=["stats2"])
        R.op("vector", lambda E: E.bn_aggr(out=mv[:], in_=stats[:]), r=["stats", "stats2"], w=["mv"])
        R.op("vector", lambda E: E.tensor_scalar(out=lnt[:, 3:4], in0=mv[:, 0:1],
                                                 scalar1=mv[:, 0:1], scalar2=None, op0=ALU.mult),
             r=["mv"], w=["lnt3"])
        R.op("vector", lambda E: E.tensor_add(out=lnt[:, 2:3], in0=mv[:, 1:2], in1=lnt[:, 3:4]),
             r=["mv", "lnt3"], w=["lnt2"])
        R.op("scalar", lambda E: E.activation(out=lnt[:, 3:4], in_=lnt[:, 2:3], func=AF.Ln, bias=float(EPS)),
             r=["lnt2", "cvals_s"], w=["lnt3"])
        R.op("scalar", lambda E: E.activation(out=lnt[:, 2:3], in_=lnt[:, 3:4], func=AF.Exp, scale=-0.5),
             r=["lnt3"], w=["lnt2"])

    def emit_flash(specs, scale_const, bias_fn, bias_key, acc_start=True, acc_stop=True):
        banks = []
        bcur, bused = [], 0
        for sp_ in specs:
            if bused + sp_["w"] > 512:
                banks.append(bcur)
                bcur, bused = [], 0
            bcur.append((sp_, bused))
            bused += sp_["w"]
        if bcur:
            banks.append(bcur)
        n_av = sum(len(b) for b in banks)
        av_cnt = [0]

        def emit_qk(j):
            s = j % 6
            for (sp_, lc) in banks[j]:
                R.op("tensor", lambda E, sp_=sp_, lc=lc, s=s: sp_["qk"](E, SB[s], lc),
                     r=sp_["qkr"], w=[f"Sb{s}"])

        def emit_exp_grp(g, n):
            js = list(range(3 * g, min(3 * g + 3, n)))
            tens = g % 2
            runs, rkeys, wkeys = [], [], []
            for j in js:
                s = j % 6
                rkeys.append(f"Sb{s}")
                wkeys.append(f"Pb{s}")
                for (sp_, lc) in banks[j]:
                    col = (j % 3) * 512 + lc
                    if runs and runs[-1][1] == col:
                        runs[-1][1] = col + sp_["w"]
                    else:
                        runs.append([col, col + sp_["w"]])
            for (c0, c1) in runs:
                R.op("scalar", lambda E, c0=c0, c1=c1, tens=tens: E.activation(
                    out=Pb[tens][:, c0:c1], in_=Sg[tens][:, c0:c1], func=AF.Exp,
                    scale=float(scale_const), bias=bias_fn()),
                    r=rkeys + [bias_key], w=wkeys)

        def emit_mask(j):
            s = j % 6
            for (sp_, lc) in banks[j]:
                if sp_["mask"] is None:
                    continue
                R.op("vector", lambda E, sp_=sp_, lc=lc, s=s: E.tensor_mul(
                    out=PBk[s][:, lc:lc + 128], in0=PBk[s][:, lc:lc + 128],
                    in1=sp_["mask"]()),
                    r=[f"Pb{s}"] + sp_["maskr"], w=[f"Pb{s}"])

        def emit_av(j):
            s = j % 6
            for (sp_, lc) in banks[j]:
                av_cnt[0] += 1
                R.op("tensor", lambda E, sp_=sp_, lc=lc, s=s,
                     st=(av_cnt[0] == 1 and acc_start),
                     sp2=(av_cnt[0] == n_av and acc_stop): sp_["av"](E, PBk[s], lc, st, sp2),
                     r=sp_["avr"] + [f"Pb{s}"], w=["accp"])

        n = len(banks)
        ngrp = (n + 2) // 3
        for g in range(ngrp):
            js = list(range(3 * g, min(3 * g + 3, n)))
            for j in js:
                emit_qk(j)
            emit_exp_grp(g, n)
            for j in js:
                emit_mask(j)
            if g >= 1:
                for j in range(3 * (g - 1), 3 * g):
                    emit_av(j)
        for j in range(3 * (ngrp - 1), n):
            emit_av(j)

    tilesA = [(0, gp, 512) for gp in range(16)]
    tilesA += [(2, gp, 256) for gp in range(16)]
    for i in range(16):
        tilesA += [(1, i, 384), (3, i, 128)]

    def kslice(gp):
        j = gp // 2
        return (16 + j) if gp % 2 == 0 else (24 + j)

    rot1 = [(i + 1) % 32 for i in range(32)]
    rot2 = [(i + 2) % 32 for i in range(32)]
    rot_sw = []
    for i in range(16):
        rot_sw += [2 * i + 1, 2 * i]

    def emit_unit_stageA(u):
        R.op("sync", lambda E, u=u: E.dma_start(out=xu[:], in_=xrows[:, 4 * u * C: 4 * (u + 1) * C]),
             w=[f"xu:{t}" for t in range(4)], dma=True)

        for t in range(4):
            ln_stats([f"xu:{t}"], lambda t=t: xu[:, t * C:(t + 1) * C])
            R.op("vector", lambda E, t=t: E.tensor_scalar(
                out=x1u[:, t * C:(t + 1) * C], in0=xu[:, t * C:(t + 1) * C],
                scalar1=lnt[:, 0:1], scalar2=lnt[:, 1:2], op0=ALU.mult, op1=ALU.add),
                r=[f"xu:{t}", "lnt0", "lnt1"], w=[f"x1u:{t}"])

        for t in range(4):
            for cch in range(8):
                slot = (t * 8 + cch) % 2
                bv = Sbf[slot]
                R.op("tensor", lambda E, t=t, c=cch, bv=bv: E.transpose(
                    out=bv[:, 0:128], in_=x1u[:, t * C + c * 128: t * C + (c + 1) * 128],
                    identity=i128b[:]),
                    r=[f"x1u:{t}", "i128b"], w=[f"Sb{3*slot}"])
                R.op("vector", lambda E, t=t, c=cch, bv=bv: E.tensor_copy(
                    out=x1T[:, c * BLK + t * 128: c * BLK + (t + 1) * 128], in_=bv[:, 0:128]),
                    r=[f"Sb{3*slot}"], w=[f"x1T:{cch}"])

        for oc in range(40):
            slot = oc % 2
            Sp = Sg[slot]
            for kc in range(8):
                R.op("tensor", lambda E, oc=oc, kc=kc, Sp=Sp: E.matmul(
                    out=Sp[:, 0:512],
                    lhsT=wqkv_s[:, kc * 5120 + oc * 128: kc * 5120 + (oc + 1) * 128],
                    rhs=x1T[:, kc * BLK:(kc + 1) * BLK],
                    start=(kc == 0), stop=(kc == 7)),
                    r=["wqkv_s", f"x1T:{kc}"], w=[f"Sb{3*slot}"])
            R.op("vector", lambda E, oc=oc, Sp=Sp: E.tensor_scalar(
                out=qkv[:, oc * BLK:(oc + 1) * BLK], in0=Sp[:, 0:512],
                scalar1=qkvb_s[:, oc:oc + 1], scalar2=None, op0=ALU.add),
                r=[f"Sb{3*slot}", "qkvb_s"], w=[f"qkv:{oc}"])

        R.op("vector", lambda E: E.memset(
            vtok[:].rearrange("p (tg c) -> p tg c", c=65)[:, :, 64:65], 1.0),
            w=["vtok_ones"])
        for t in range(4):
            for i in range(8):
                oc = 32 + i
                slot = (t * 8 + i) % 2
                bv = Sbf[slot]
                base = t * 1040
                R.op("tensor", lambda E, t=t, oc=oc, bv=bv: E.transpose(
                    out=bv[:, 0:128],
                    in_=qkv[:, oc * BLK + t * 128: oc * BLK + (t + 1) * 128],
                    identity=i128b[:]),
                    r=[f"qkv:{oc}", "i128b"], w=[f"Sb{3*slot}"])
                R.op("vector", lambda E, i=i, bv=bv, base=base: E.tensor_copy(
                    out=vtok[:, base + 65 * (2 * i): base + 65 * (2 * i) + 64],
                    in_=bv[:, 0:64]),
                    r=[f"Sb{3*slot}"], w=[f"vtok:{base // 1040}:{2*i}"])
                R.op("vector", lambda E, i=i, bv=bv, base=base: E.tensor_copy(
                    out=vtok[:, base + 65 * (2 * i + 1): base + 65 * (2 * i + 1) + 64],
                    in_=bv[:, 64:128]),
                    r=[f"Sb{3*slot}"], w=[f"vtok:{base // 1040}:{2*i+1}"])

        for ch in range(24):
            R.op("vector", lambda E, ch=ch: E.tensor_mul(
                out=sqb[:], in0=qkv[:, ch * BLK:(ch + 1) * BLK], in1=qkv[:, ch * BLK:(ch + 1) * BLK]),
                r=[f"qkv:{ch}"], w=["sqb"])
            R.op("tensor", lambda E: E.matmul(
                out=auxp[0:2, 0:512], lhsT=blk2_s[:], rhs=sqb[:], start=True, stop=True),
                r=["blk2_s", "sqb"], w=["auxp"])
            R.op("vector", lambda E, ch=ch: E.tensor_reduce(
                out=qkmax2[0:2, ch:ch + 1], in_=auxp[0:2, 0:512], axis=AX.X, op=ALU.max),
                r=["auxp"], w=[f"qkmax2:{ch}"])
        R.op("vector", lambda E: E.tensor_reduce(
            out=mstab[0:2, 0:1], in_=qkmax2[0:2, 0:16], axis=AX.X, op=ALU.max),
            r=[f"qkmax2:{ch}" for ch in range(16)], w=["ms0"])
        R.op("vector", lambda E: E.tensor_reduce(
            out=mstab[0:2, 1:2], in_=qkmax2[0:2, 16:24], axis=AX.X, op=ALU.max),
            r=[f"qkmax2:{ch}" for ch in range(16, 24)], w=["ms1"])
        R.op("vector", lambda E: E.stream_shuffle(out=mstab[0:32, 2:4], in_=mstab[0:32, 0:2], mask=rot1),
             r=["ms0", "ms1"], w=["ms23"])
        R.op("vector", lambda E: E.tensor_max(out=mstab[0:1, 4:6], in0=mstab[0:1, 0:2], in1=mstab[0:1, 2:4]),
             r=["ms0", "ms1", "ms23"], w=["ms45"])
        R.op("vector", lambda E: E.tensor_mul(out=mstab[0:1, 6:7], in0=mstab[0:1, 4:5], in1=mstab[0:1, 5:6]),
             r=["ms45"], w=["ms6"])
        R.op("scalar", lambda E: E.activation(out=mstab[0:1, 7:8], in_=mstab[0:1, 6:7], func=AF.Ln, bias=1e-30),
             r=["ms6", "cvals_s"], w=["ms7"])
        R.op("scalar", lambda E: E.activation(out=mstab[0:1, 6:7], in_=mstab[0:1, 7:8], func=AF.Exp,
                                              scale=0.5, bias=float(np.log(QK_SCALE_A))),
             r=["ms7", "cvals_s"], w=["ms6"])
        R.op("vector", lambda E: E.tensor_scalar(out=mstabb[:, 0:1], in0=mstab[0:1, 6:7], scalar1=-1.0,
                                                 scalar2=None, op0=ALU.mult),
             r=["ms6"], w=["mstabb"])
        R.op("tensor", lambda E: E.matmul(out=auxp[:, 0:1], lhsT=ones1_s[:], rhs=mstabb[:, 0:1],
                                          start=True, stop=True),
             r=["ones1_s", "mstabb"], w=["auxp"])
        R.op("vector", lambda E: E.tensor_copy(out=negM[:], in_=auxp[:, 0:1]),
             r=["auxp"], w=["negM"])

        for g in range(16):
            specs = []
            for (m, gp, w) in tilesA:
                kch = kslice(gp)
                mT, mk = (mTi, "mTi") if gp <= g else (mTx, "mTx")

                def qk(E, Sp, col, m=m, w=w, kch=kch, g=g):
                    return E.matmul(
                        out=Sp[:, col:col + w],
                        lhsT=qkv[:, kch * BLK + m * 128: kch * BLK + (m + 1) * 128],
                        rhs=qkv[:, g * BLK + m * 128: g * BLK + 512],
                        start=True, stop=True)

                def av(E, Pp, col, st, sp2, m=m, gp=gp, w=w):
                    return E.matmul(
                        out=accp[0:65, m * 128: m * 128 + w],
                        lhsT=vtok[:, m * 1040 + 65 * gp: m * 1040 + 65 * gp + 65],
                        rhs=Pp[:, col:col + w], start=st, stop=sp2)

                specs.append(dict(w=w, qk=qk, qkr=[f"qkv:{kch}", f"qkv:{g}"],
                                  mask=(lambda mT=mT: mT[:]), maskr=[mk],
                                  av=av, avr=[f"vtok:{m}:{gp}", "vtok_ones"]))
            emit_flash(specs, QK_SCALE_A, lambda: negM[:], "negM")

            R.op("vector", lambda E: E.tensor_copy(out=accs[0:65, :], in_=accp[0:65, :]),
                 r=["accp"], w=["accs"])
            for t in range(4):
                R.op("tensor", lambda E, t=t: E.transpose(
                    out=auxp[:, t * 65: t * 65 + 65], in_=accs[0:65, t * 128:(t + 1) * 128],
                    identity=i128f[0:65, 0:65]),
                    r=["accs", "i128f"], w=["auxp"])
            for t in range(4):
                R.op("vector", lambda E, t=t: E.reciprocal(
                    out=recb[:, t:t + 1], in_=auxp[:, t * 65 + 64: t * 65 + 65]),
                    r=["auxp"], w=[f"recb:{t}"])
                R.op("vector", lambda E, t=t, g=g: E.tensor_scalar(
                    out=attnA[:, t * C + g * 64: t * C + g * 64 + 64],
                    in0=auxp[:, t * 65: t * 65 + 64],
                    scalar1=recb[:, t:t + 1], scalar2=None, op0=ALU.mult),
                    r=["auxp", f"recb:{t}"], w=[f"attnA:{t}:{g}"])

        for t in range(4):
            aks = [f"attnA:{t}:{g}" for g in range(16)]
            ln_stats(aks, lambda t=t: attnA[:, t * C:(t + 1) * C])
            R.op("vector", lambda E, t=t: E.tensor_scalar(
                out=lnres[:], in0=attnA[:, t * C:(t + 1) * C],
                scalar1=lnt[:, 0:1], scalar2=lnt[:, 1:2], op0=ALU.mult, op1=ALU.add),
                r=aks + ["lnt0", "lnt1"], w=["lnres"])
            R.op("vector", lambda E, t=t: E.tensor_add(
                out=xu[:, t * C:(t + 1) * C], in0=xu[:, t * C:(t + 1) * C], in1=lnres[:]),
                r=["lnres", f"xu:{t}"], w=[f"xu:{t}"])
        dst = o_x if stage == "A" else xscr
        R.op("sync", lambda E, u=u, dst=dst: E.dma_start(
            out=dst[:, 4 * u * C: 4 * (u + 1) * C], in_=xu[:]),
            r=[f"xu:{t}" for t in range(4)], w=[f"xscr:{u}"], dma=True)

    for u in range(4):
        emit_unit_stageA(u)

    if stage == "A":
        R.op("gpsimd", lambda E: E.nop(), r=[f"xscr:{u}" for u in range(4)])
        R.emit(nc)
        return nc

    R.barrier()

    R.op("gpsimd", lambda E: E.dma_start(out=wq_s[:], in_=wqp[:]), w=["wq_s"], dma=True)
    R.op("gpsimd", lambda E: E.dma_start(out=wkva_s[:], in_=wkva[:]), w=["wkva_s"], dma=True)
    R.op("sync", lambda E: E.dma_start(out=wln1c_s[:], in_=wln1c[:]), w=["wln1c_s"], dma=True)
    R.op("sync", lambda E: E.dma_start(out=wln2c_s[:], in_=wln2c[:]), w=["wln2c_s"], dma=True)
    R.op("gpsimd", lambda E: E.dma_start(out=wkvbka_s[:], in_=wkvbka[:]), w=["wkvbka_s"], dma=True)
    R.op("gpsimd", lambda E: E.dma_start(out=wkvbtok_s[:], in_=wkvbtok[:]), w=["wkvbtok_s"], dma=True)
    R.op("sync", lambda E: E.dma_start(out=idx8_s[:], in_=idx8[:]), w=["idx8_s"], dma=True)
    R.op("gpsimd", lambda E: E.dma_start(out=wo_s[:], in_=woe[:]), w=["wo_s"], dma=True)
    R.op("gpsimd", lambda E: E.dma_start(out=wgt_s[:], in_=wgt[:]), w=["wgt_raw"], dma=True)
    R.op("sync", lambda E: E.dma_start(out=ebias_s[:], in_=ebias[:]), w=["ebias_s"], dma=True)
    R.op("gpsimd", lambda E: E.dma_start(out=cqs[:], in_=cq[:]), w=["cqs"], dma=True)
    R.op("gpsimd", lambda E: E.dma_start(out=sqs[:], in_=sq_t[:]), w=["sqs"], dma=True)
    R.op("gpsimd", lambda E: E.dma_start(out=cks, in_=ck[:]), w=["cks"], dma=True)
    R.op("gpsimd", lambda E: E.dma_start(out=sks, in_=sk_t[:]), w=["sks"], dma=True)

    for kc in range(8):
        R.op("vector", lambda E, kc=kc: E.tensor_scalar(
            out=wq_s[:, kc * C:(kc + 1) * C], in0=wq_s[:, kc * C:(kc + 1) * C],
            scalar1=wln1c_s[:, kc:kc + 1], scalar2=None, op0=ALU.mult),
            r=["wq_s", "wln1c_s"], w=["wq_s"])
        R.op("vector", lambda E, kc=kc: E.tensor_scalar(
            out=wkva_s[:, kc * 160:(kc + 1) * 160], in0=wkva_s[:, kc * 160:(kc + 1) * 160],
            scalar1=wln1c_s[:, kc:kc + 1], scalar2=None, op0=ALU.mult),
            r=["wkva_s", "wln1c_s"], w=["wkva_s"])
        R.op("vector", lambda E, kc=kc: E.tensor_scalar(
            out=wgt_s[:, kc * 8:(kc + 1) * 8], in0=wgt_s[:, kc * 8:(kc + 1) * 8],
            scalar1=wln2c_s[:, kc:kc + 1], scalar2=None, op0=ALU.mult),
            r=["wgt_raw", "wln2c_s"], w=["wgt_s"])

    for u in range(4):
        R.op("sync", lambda E, u=u: E.dma_start(out=xu[:], in_=xscr[:, 4 * u * C: 4 * (u + 1) * C]),
             r=[f"xscr:{u}"], w=[f"xu:{t}" for t in range(4)], dma=True)
        for t in range(4):
            tt = 4 * u + t
            ln_stats([f"xu:{t}"], lambda t=t: xu[:, t * C:(t + 1) * C])
            R.op("vector", lambda E, tt=tt: E.tensor_copy(
                out=ln2st[:, 2 * tt: 2 * tt + 2], in_=lnt[:, 0:2]),
                r=["lnt0", "lnt1"], w=[f"ln2st:{tt}"])
            R.op("vector", lambda E, t=t: E.tensor_scalar(
                out=x2f[:], in0=xu[:, t * C:(t + 1) * C],
                scalar1=lnt[:, 0:1], scalar2=lnt[:, 1:2], op0=ALU.mult, op1=ALU.add),
                r=[f"xu:{t}", "lnt0", "lnt1"], w=["x2f"])
            rms_inv(["x2f"], lambda: x2f)
            R.op("vector", lambda E: E.tensor_scalar(
                out=sqb[:, 0:512], in0=x2f[:, 0:512], scalar1=lnt[:, 2:3],
                scalar2=None, op0=ALU.mult),
                r=["x2f", "lnt2"], w=["sqb"])
            R.op("vector", lambda E: E.tensor_scalar(
                out=qrs[:, 0:512], in0=x2f[:, 512:1024], scalar1=lnt[:, 2:3],
                scalar2=None, op0=ALU.mult),
                r=["x2f", "lnt2"], w=["qrs0"])
            for cch in range(8):
                slot = cch % 2
                bv = Sbf[slot]
                if cch < 4:
                    src_fn = (lambda c=cch: sqb[:, c * 128:(c + 1) * 128])
                    skey2 = "sqb"
                else:
                    src_fn = (lambda c=cch: qrs[:, (c - 4) * 128:(c - 4) * 128 + 128])
                    skey2 = "qrs0"
                R.op("tensor", lambda E, src_fn=src_fn, bv=bv: E.transpose(
                    out=bv[:, 0:128], in_=src_fn(), identity=i128b[:]),
                    r=[skey2, "i128b"], w=[f"Sb{3*slot}"])
                R.op("vector", lambda E, c=cch, bv=bv, tt=tt: E.tensor_copy(
                    out=x2nT[:, c * 2048 + tt * 128: c * 2048 + (tt + 1) * 128], in_=bv[:, 0:128]),
                    r=[f"Sb{3*slot}"], w=[f"x2nT:{cch}"])

    for tt4 in range(4):
        slot = tt4 % 2
        Sp = Sg[slot]
        for kc in range(8):
            R.op("tensor", lambda E, tt4=tt4, kc=kc, Sp=Sp: E.matmul(
                out=Sp[:, 0:512], lhsT=wkva_s[:, kc * 160: kc * 160 + 128],
                rhs=x2nT[:, kc * 2048 + tt4 * 512: kc * 2048 + (tt4 + 1) * 512],
                start=(kc == 0), stop=(kc == 7)),
                r=["wkva_s", f"x2nT:{kc}"], w=[f"Sb{3*slot}"])
        R.op("vector", lambda E, tt4=tt4, Sp=Sp: E.tensor_copy(
            out=ckvr[:, tt4 * 512:(tt4 + 1) * 512], in_=Sp[:, 0:512]),
            r=[f"Sb{3*slot}"], w=["ckvr"])
        for kc in range(8):
            R.op("tensor", lambda E, tt4=tt4, kc=kc, Sp=Sp: E.matmul(
                out=Sp[0:32, 512:1024], lhsT=wkva_s[:, kc * 160 + 128: kc * 160 + 160],
                rhs=x2nT[:, kc * 2048 + tt4 * 512: kc * 2048 + (tt4 + 1) * 512],
                start=(kc == 0), stop=(kc == 7)),
                r=["wkva_s", f"x2nT:{kc}"], w=[f"Sb{3*slot+1}"])
        R.op("vector", lambda E, tt4=tt4, Sp=Sp: E.tensor_copy(
            out=kropeo[:, tt4 * 512:(tt4 + 1) * 512], in_=Sp[0:32, 512:1024]),
            r=[f"Sb{3*slot+1}"], w=["kropeo"])

    R.op("vector", lambda E: E.stream_shuffle(out=qrs[0:32, 0:2048], in_=kropeo, mask=rot_sw),
         r=["kropeo"], w=["qrs0"])
    R.op("vector", lambda E: E.tensor_mul(out=qrs[0:32, 2048:4096], in0=kropeo, in1=cks),
         r=["kropeo", "cks"], w=["qrs1"])
    R.op("vector", lambda E: E.tensor_mul(out=qrs[0:32, 0:2048], in0=qrs[0:32, 0:2048], in1=sks),
         r=["qrs0", "sks"], w=["qrs0"])
    R.op("vector", lambda E: E.tensor_add(out=kropeo, in0=qrs[0:32, 2048:4096], in1=qrs[0:32, 0:2048]),
         r=["qrs0", "qrs1"], w=["kropeo"])

    R.op("sync", lambda E: E.dma_start(out=ccin[0:128, :], in_=ckvr[:]), r=["ckvr"], w=["ccin"], dma=True)
    R.op("sync", lambda E: E.dma_start(out=ccin[128:160, :], in_=kropeo), r=["kropeo"], w=["ccin2"], dma=True)
    R.op("gpsimd", lambda E: E.collective_compute(
        "AllGather", ALU.bypass, replica_groups=[list(range(8))],
        ins=[ccin[:]], outs=[ccout[:]]),
        r=["ccin", "ccin2"], w=["ccout"])

    for oc in range(8):
        for tt4 in range(4):
            slot = (oc * 4 + tt4) % 2
            Sp = Sg[slot]
            for kc in range(8):
                R.op("tensor", lambda E, oc=oc, tt4=tt4, kc=kc, Sp=Sp: E.matmul(
                    out=Sp[:, 0:512], lhsT=wq_s[:, kc * C + oc * 128: kc * C + (oc + 1) * 128],
                    rhs=x2nT[:, kc * 2048 + tt4 * 512: kc * 2048 + (tt4 + 1) * 512],
                    start=(kc == 0), stop=(kc == 7)),
                    r=["wq_s", f"x2nT:{kc}"], w=[f"Sb{3*slot}"])
            R.op("vector", lambda E, oc=oc, tt4=tt4, Sp=Sp: E.tensor_copy(
                out=q_fm[:, oc * 2048 + tt4 * 512: oc * 2048 + (tt4 + 1) * 512], in_=Sp[:, 0:512]),
                r=[f"Sb{3*slot}"], w=[f"q_fm:{oc}"])

    for oc in range(8):
        R.op("vector", lambda E, oc=oc: E.stream_shuffle(
            out=qrs[:, 0:2048], in_=q_fm[:, oc * 2048:(oc + 1) * 2048], mask=rot_sw),
            r=[f"q_fm:{oc}"], w=["qrs0"])
        R.op("vector", lambda E, oc=oc: E.tensor_mul(
            out=qrs[:, 2048:4096], in0=q_fm[:, oc * 2048:(oc + 1) * 2048], in1=cqs[:]),
            r=[f"q_fm:{oc}", "cqs"], w=["qrs1"])
        R.op("vector", lambda E: E.tensor_mul(out=qrs[:, 0:2048], in0=qrs[:, 0:2048], in1=sqs[:]),
             r=["qrs0", "sqs"], w=["qrs0"])
        R.op("vector", lambda E, oc=oc: E.tensor_add(
            out=q_fm[:, oc * 2048:(oc + 1) * 2048], in0=qrs[:, 2048:4096], in1=qrs[:, 0:2048]),
            r=["qrs0", "qrs1"], w=[f"q_fm:{oc}"])

    R.op("sync", lambda E: E.dma_start(out=qgin[:], in_=q_fm[:]),
         r=[f"q_fm:{oc}" for oc in range(8)], w=["qgin"], dma=True)
    R.op("gpsimd", lambda E: E.collective_compute(
        "AllGather", ALU.bypass, replica_groups=[list(range(8))],
        ins=[qgin[:]], outs=[qgout[:]]),
        r=["qgin"], w=["qgout"])

    R.op("vector", lambda E: E.tensor_mul(
        out=sqb[:, 0:128], in0=wkvbka_s[:], in1=wkvbka_s[:]),
        r=["wkvbka_s"], w=["sqb"])
    R.op("tensor", lambda E: E.matmul(out=auxp[0:1, 0:128], lhsT=onescol_s[:], rhs=sqb[:, 0:128],
                                      start=True, stop=True),
         r=["onescol_s", "sqb"], w=["auxp"])
    R.op("vector", lambda E: E.tensor_reduce(
        out=kn16[0:1, 0:1], in_=auxp[0:1, 0:32], axis=AX.X, op=ALU.add),
        r=["auxp"], w=["kn16:0"])
    R.op("vector", lambda E: E.tensor_reduce(
        out=kn16[0:1, 1:2], in_=auxp[0:1, 96:128], axis=AX.X, op=ALU.add),
        r=["auxp"], w=["kn16:1"])

    for r_ in range(8):
        R.op("sync", lambda E, r_=r_: E.dma_start(
            out=latent[:, r_ * 2048:(r_ + 1) * 2048], in_=ccout[160 * r_:160 * r_ + 128, :]),
            r=["ccout"], w=[f"latent:{r_}"], dma=True)
        R.op("sync", lambda E, r_=r_: E.dma_start(
            out=krope4[32 * (r_ // 2):32 * (r_ // 2) + 32,
                       (r_ % 2) * 2048:(r_ % 2) * 2048 + 2048],
            in_=ccout[160 * r_ + 128:160 * (r_ + 1), :]),
            r=["ccout"], w=[f"krope4:{r_}"], dma=True)

    R.op("vector", lambda E: E.memset(mstab[0:32, 0:2], 0.0), w=["ms0", "ms1"])
    for chk in range(32):
        R.op("vector", lambda E, chk=chk: E.tensor_mul(
            out=sqb[:], in0=latent[:, chk * 512:(chk + 1) * 512], in1=latent[:, chk * 512:(chk + 1) * 512]),
            r=[f"latent:{chk // 4}"], w=["sqb"])
        R.op("tensor", lambda E: E.matmul(out=auxp[0:1, 0:512], lhsT=onescol_s[:], rhs=sqb[:],
                                          start=True, stop=True),
             r=["onescol_s", "sqb"], w=["auxp"])
        R.op("vector", lambda E: E.tensor_reduce(out=mstab[0:1, 2:3], in_=auxp[0:1, 0:512],
                                                 axis=AX.X, op=ALU.max),
             r=["auxp"], w=["ms23"])
        R.op("vector", lambda E: E.tensor_max(out=mstab[0:1, 0:1], in0=mstab[0:1, 0:1], in1=mstab[0:1, 2:3]),
             r=["ms0", "ms23"], w=["ms0"])
    for chk in range(8):
        R.op("vector", lambda E, chk=chk: E.tensor_mul(
            out=sqb[:], in0=krope4[:, chk * 512:(chk + 1) * 512], in1=krope4[:, chk * 512:(chk + 1) * 512]),
            r=[f"krope4:{r}" for r in range(8)], w=["sqb"])
        R.op("tensor", lambda E: E.matmul(out=auxp[0:4, 0:512], lhsT=grp4_s[:], rhs=sqb[:],
                                          start=True, stop=True),
             r=["grp4_s", "sqb"], w=["auxp"])
        R.op("vector", lambda E: E.tensor_reduce(out=mstab[0:4, 2:3], in_=auxp[0:4, 0:512],
                                                 axis=AX.X, op=ALU.max),
             r=["auxp"], w=["ms23"])
        R.op("vector", lambda E: E.tensor_max(out=mstab[0:4, 1:2], in0=mstab[0:4, 1:2], in1=mstab[0:4, 2:3]),
             r=["ms1", "ms23"], w=["ms1"])
    R.op("vector", lambda E: E.stream_shuffle(out=mstab[0:32, 2:3], in_=mstab[0:32, 1:2], mask=rot2),
         r=["ms1"], w=["ms23"])
    R.op("vector", lambda E: E.tensor_max(out=mstab[0:32, 1:2], in0=mstab[0:32, 1:2], in1=mstab[0:32, 2:3]),
         r=["ms1", "ms23"], w=["ms1"])
    R.op("vector", lambda E: E.stream_shuffle(out=mstab[0:32, 2:3], in_=mstab[0:32, 1:2], mask=rot1),
         r=["ms1"], w=["ms23"])
    R.op("vector", lambda E: E.tensor_max(out=mstab[0:1, 1:2], in0=mstab[0:1, 1:2], in1=mstab[0:1, 2:3]),
         r=["ms1", "ms23"], w=["ms1"])

    R.barrier()

    qgv = qgout.rearrange("(s p) (c t) -> (s p c) t", p=128, t=2048)
    for s in range(8):
        R.op("gpsimd", lambda E, s=s: E.indirect_dma_start(
            out=q_all[:, s * 2048:(s + 1) * 2048], out_offset=None,
            in_=qgv[:, :],
            in_offset=IndirectOffsetOnAxis(ap=idx8_s[:, s:s + 1], axis=0)),
            r=["qgout", "idx8_s"], w=[f"q_all:{s}"], dma=True)

    R.op("vector", lambda E: E.memset(qkmax2[0:32, 0:4], 0.0), w=["qm2", "qm1", "qm3"])
    for chk in range(32):
        R.op("vector", lambda E, chk=chk: E.tensor_mul(
            out=sqb[:], in0=q_all[:, chk * 512:(chk + 1) * 512], in1=q_all[:, chk * 512:(chk + 1) * 512]),
            r=[f"q_all:{chk // 4}"], w=["sqb"])
        R.op("tensor", lambda E: E.matmul(
            out=auxp[0:2, 0:512], lhsT=blk2_s[:], rhs=sqb[:], start=True, stop=True),
            r=["blk2_s", "sqb"], w=["auxp"])
        R.op("vector", lambda E: E.tensor_reduce(
            out=qkmax2[0:2, 1:2], in_=auxp[0:2, 0:512], axis=AX.X, op=ALU.max),
            r=["auxp"], w=["qm1"])
        R.op("vector", lambda E: E.tensor_max(out=qkmax2[0:2, 0:1], in0=qkmax2[0:2, 0:1],
                                              in1=qkmax2[0:2, 1:2]),
             r=["qm1", "qm2"], w=["qm2"])
    R.op("vector", lambda E: E.tensor_scalar(
        out=kn16[0:1, 0:2], in0=kn16[0:1, 0:2], scalar1=mstab[0:1, 0:1], scalar2=mstab[0:1, 1:2],
        op0=ALU.mult, op1=ALU.add),
        r=["kn16:0", "kn16:1", "ms0", "ms1"], w=["kn16f"])
    R.op("vector", lambda E: E.tensor_copy(out=qn16[0:1, 0:1], in_=qkmax2[0:1, 0:1]),
         r=["qm2"], w=["qn16a"])
    R.op("vector", lambda E: E.stream_shuffle(out=qkmax2[0:32, 2:3], in_=qkmax2[0:32, 0:1], mask=rot1),
         r=["qm2"], w=["qm3"])
    R.op("vector", lambda E: E.tensor_copy(out=qn16[0:1, 1:2], in_=qkmax2[0:1, 2:3]),
         r=["qm3"], w=["qn16b"])
    R.op("vector", lambda E: E.tensor_mul(out=qn16[0:1, 0:2], in0=qn16[0:1, 0:2], in1=kn16[0:1, 0:2]),
         r=["qn16a", "qn16b", "kn16f"], w=["qn16f"])
    R.op("scalar", lambda E: E.activation(out=kn16[0:1, 0:2], in_=qn16[0:1, 0:2], func=AF.Ln, bias=1e-30),
         r=["qn16f", "cvals_s"], w=["kn16f2"])
    R.op("scalar", lambda E: E.activation(out=qn16[0:1, 0:2], in_=kn16[0:1, 0:2], func=AF.Exp,
                                          scale=0.5, bias=float(np.log(QK_SCALE_B))),
         r=["kn16f2", "cvals_s"], w=["qn16f2"])
    R.op("vector", lambda E: E.tensor_scalar(out=mstabb[:, 0:2], in0=qn16[0:1, 0:2], scalar1=-1.0,
                                             scalar2=None, op0=ALU.mult),
         r=["qn16f2"], w=["mstabb"])
    R.op("tensor", lambda E: E.matmul(out=auxp[:, 0:2], lhsT=ones1_s[:], rhs=mstabb[:, 0:2],
                                      start=True, stop=True),
         r=["ones1_s", "mstabb"], w=["auxp"])
    R.op("vector", lambda E: E.tensor_copy(out=negMB[:, 0:2], in_=auxp[:, 0:2]),
         r=["auxp"], w=["negMB"])

    def block_col(b, kb):
        r_ = min(kb, 15 - kb)
        qslot = 2 * b + (0 if kb < 8 else 1)
        return 2048 * r_ + 512 * qslot, r_

    for b in (0, 1):
        for kb in range(16):
            col0, r_ = block_col(b, kb)
            g4 = col0 // 4096
            kcol = col0 % 4096
            bi = b * 16 + kb
            slot = bi % 2
            Sp = Sg[slot]
            R.op("tensor", lambda E, col0=col0, Sp=Sp: E.matmul(
                out=Sp[:, 0:512], lhsT=wkvbka_s[:],
                rhs=latent[:, col0:col0 + 512], start=True, stop=False),
                r=["wkvbka_s", f"latent:{col0 // 2048}"], w=[f"Sb{3*slot}"])
            rb = 32 * g4 if g4 < 2 else 64
            rk = 32 if g4 < 2 else 64
            R.op("tensor", lambda E, g4=g4, kcol=kcol, Sp=Sp, rb=rb, rk=rk: E.matmul(
                out=Sp[:, 0:512],
                lhsT=ropesc_s[rb:rb + rk, g4 * 128:(g4 + 1) * 128],
                rhs=krope4[rb:rb + rk, kcol:kcol + 512],
                start=False, stop=True),
                r=["ropesc_s", f"krope4:{2*g4}", f"krope4:{2*g4+1}"], w=[f"Sb{3*slot}"])
            R.op("vector", lambda E, bi=bi, Sp=Sp: E.tensor_copy(
                out=ka_own[:, bi * 512:(bi + 1) * 512], in_=Sp[:, 0:512]),
                r=[f"Sb{3*slot}"], w=[f"ka:{bi}"])
            for tau in range(4):
                R.op("tensor", lambda E, tau=tau, col0=col0, Sp=Sp: E.matmul(
                    out=Sp[:, 512:578], lhsT=latent[:, col0 + tau * 128: col0 + (tau + 1) * 128],
                    rhs=wkvbtok_s[:], start=True, stop=True),
                    r=["wkvbtok_s", f"latent:{col0 // 2048}"], w=[f"Sb{3*slot+1}"])
                R.op("vector", lambda E, bi=bi, tau=tau, Sp=Sp: E.tensor_copy(
                    out=kvt_own[:, bi * 264 + tau * 66: bi * 264 + (tau + 1) * 66],
                    in_=Sp[:, 512:578]),
                    r=[f"Sb{3*slot+1}"], w=[f"kvt:{bi}"])
            R.op("vector", lambda E, bi=bi: E.memset(
                kvt_own[:, bi * 264:(bi + 1) * 264].rearrange("p (x c) -> p x c", c=33)[:, :, 32:33], 1.0),
                r=[f"kvt:{bi}"], w=[f"kvto:{bi}"])

    R.op("vector", lambda E: E.memset(qpad_he[64:128, :], 0.0), w=["qpadzh"])
    R.op("vector", lambda E: E.memset(qpad_ho[0:64, :], 0.0), w=["qpadzl"])
    for b in (0, 1):
        for jq in range(16):
            qcol0, _ = block_col(b, jq)
            par = (b * 16 + jq) % 2
            R.op("vector", lambda E, qcol0=qcol0, par=par: E.tensor_copy(
                out=qpad_he[0:64, par * 512:(par + 1) * 512],
                in_=q_all[0:64, qcol0:qcol0 + 512]),
                r=[f"q_all:{qcol0 // 2048}"], w=[f"qpe:{par}"])
            R.op("vector", lambda E, qcol0=qcol0, par=par: E.tensor_copy(
                out=qpad_ho[64:128, par * 512:(par + 1) * 512],
                in_=q_all[64:128, qcol0:qcol0 + 512]),
                r=[f"q_all:{qcol0 // 2048}"], w=[f"qpo:{par}"])
            for h2 in range(2):
                hb = 64 * h2
                qp = qpad_he if h2 == 0 else qpad_ho
                qpk = (f"qpe:{par}", "qpadzh") if h2 == 0 else (f"qpo:{par}", "qpadzl")
                specs = []
                for kb in range(jq + 1):
                    bi = b * 16 + kb
                    diag = (kb == jq)
                    for tau in range(4):
                        w = 512 - tau * 128 if diag else 512
                        scol = tau * 128 if diag else 0

                        def qk(E, Sp, col, bi=bi, tau=tau, w=w, scol=scol,
                               qp=qp, par=par):
                            return E.matmul(
                                out=Sp[:, col:col + w],
                                lhsT=ka_own[:, bi * 512 + tau * 128: bi * 512 + (tau + 1) * 128],
                                rhs=qp[:, par * 512 + scol: par * 512 + 512],
                                start=True, stop=True)

                        if diag:
                            mask, maskr = (lambda: mTi[:]), ["mTi"]
                        else:
                            mask, maskr = None, []

                        def av(E, Pp, col, st, sp2, bi=bi, tau=tau, w=w, scol=scol, hb=hb, h2=h2):
                            return E.matmul(
                                out=accp[hb:hb + 33, scol: scol + w],
                                lhsT=kvt_own[:, bi * 264 + tau * 66 + 33 * h2:
                                             bi * 264 + tau * 66 + 33 * h2 + 33],
                                rhs=Pp[:, col:col + w], start=st, stop=sp2)

                        specs.append(dict(
                            w=w, qk=qk, qkr=[f"ka:{bi}", qpk[0], qpk[1]],
                            mask=mask, maskr=maskr, av=av,
                            avr=[f"kvt:{bi}", f"kvto:{bi}"]))
                emit_flash(specs, QK_SCALE_B, (lambda h2=h2: negMB[:, h2:h2 + 1]), "negMB")
            R.op("vector", lambda E: E.tensor_copy(out=acc2[0:33, 0:512], in_=accp[0:33, :]),
                 r=["accp"], w=["acc2a"])
            R.op("vector", lambda E: E.tensor_copy(out=acc2[64:97, 512:1024], in_=accp[64:97, :]),
                 r=["accp"], w=["acc2b"])
            for h2 in range(2):
                hb = 64 * h2
                akey = "acc2a" if h2 == 0 else "acc2b"
                for t in range(4):
                    R.op("tensor", lambda E, h2=h2, hb=hb, t=t: E.transpose(
                        out=auxp[:, t * 33: t * 33 + 33],
                        in_=acc2[hb:hb + 33, h2 * 512 + t * 128: h2 * 512 + (t + 1) * 128],
                        identity=i128f[hb:hb + 33, hb:hb + 33]),
                        r=[akey, "i128f"], w=["auxp"])
                for t in range(4):
                    gt = qcol0 // 128 + t
                    R.op("vector", lambda E, t=t: E.reciprocal(
                        out=recb[:, t:t + 1], in_=auxp[:, t * 33 + 32: t * 33 + 33]),
                        r=["auxp"], w=[f"recb:{t}"])
                    R.op("vector", lambda E, t=t, gt=gt, h2=h2: E.tensor_scalar(
                        out=ytok[:, gt * 64 + 32 * h2: gt * 64 + 32 * h2 + 32],
                        in0=auxp[:, t * 33: t * 33 + 32],
                        scalar1=recb[:, t:t + 1], scalar2=None, op0=ALU.mult),
                        r=["auxp", f"recb:{t}"], w=[f"ytok:{gt}:{h2}"])

    R.op("sync", lambda E: E.dma_start(out=ygin[:], in_=ytok[:]),
         r=[f"ytok:{gt}:{h2}" for gt in range(128) for h2 in range(2)], w=["ygin"], dma=True)
    R.op("gpsimd", lambda E: E.collective_compute(
        "AllGather", ALU.bypass, replica_groups=[list(range(8))],
        ins=[ygin[:]], outs=[ygout[:]]),
        r=["ygin"], w=["ygout"])
    R.barrier()
    ygv = ygout.rearrange("(s p) (c t) -> (s p c) t", p=128, t=1024)
    for s in range(8):
        R.op("gpsimd", lambda E, s=s: E.indirect_dma_start(
            out=ytmp[:, s * 1024:(s + 1) * 1024],
            out_offset=None,
            in_=ygv[:, :],
            in_offset=IndirectOffsetOnAxis(ap=idx8_s[:, s:s + 1], axis=0)),
            r=["ygout", "idx8_s"], w=[f"ytmp:{s}"], dma=True)
    for s in range(8):
        R.op("vector", lambda E, s=s: E.tensor_copy(
            out=yfull[:].rearrange("p (t c) -> p t c", c=512)[:, :, 64 * s: 64 * s + 64],
            in_=ytmp[:, s * 1024:(s + 1) * 1024].rearrange("p (t d) -> p t d", d=64)),
            r=[f"ytmp:{s}"], w=[f"yfull:{s}"], )

    if stage == "tap_yf":
        o_yf = nc.dram_tensor("o_yf", [128, 8192], F32, kind="ExternalOutput").ap()
        R.op("gpsimd", lambda E: E.dma_start(out=o_yf[:], in_=yfull[:]),
             r=[f"yfull:{s}" for s in range(8)], w=["oyf"], dma=True)

    for u in range(4):
        R.op("sync", lambda E, u=u: E.dma_start(out=xu[:], in_=xscr[:, 4 * u * C: 4 * (u + 1) * C]),
             r=[f"xscr:{u}"], w=[f"xbl:{t}" for t in range(4)], dma=True)
        for t in range(4):
            tt = 4 * u + t
            for yc in range(4):
                slot = (t * 4 + yc) % 2
                bv = Sbf[slot]
                R.op("tensor", lambda E, tt=tt, yc=yc, bv=bv: E.transpose(
                    out=bv[:, 0:128], in_=yfull[:, tt * 512 + yc * 128: tt * 512 + (yc + 1) * 128],
                    identity=i128b[:]),
                    r=[f"yfull:{s}" for s in range(8)] + ["i128b"],
                    w=[f"Sb{3*slot}"])
                R.op("vector", lambda E, t=t, yc=yc, bv=bv: E.tensor_copy(
                    out=yfm[:, yc * 512 + t * 128: yc * 512 + (t + 1) * 128], in_=bv[:, 0:128]),
                    r=[f"Sb{3*slot}"], w=[f"yfm:{yc}"])
        for t in range(4):
            tt = 4 * u + t
            for half in range(2):
                slot = half
                Sp = Sg[slot]
                for yc in range(4):
                    R.op("tensor", lambda E, t=t, half=half, yc=yc, Sp=Sp: E.matmul(
                        out=Sp[:, 0:512],
                        lhsT=yfm[:, yc * 512 + t * 128: yc * 512 + (t + 1) * 128],
                        rhs=wo_s[:, yc * C + half * 512: yc * C + (half + 1) * 512],
                        start=(yc == 0), stop=(yc == 3)),
                        r=[f"yfm:{yc}", "wo_s"], w=[f"Sb{3*slot}"])
                R.op("vector", lambda E, t=t, half=half, tt=tt: E.tensor_scalar(
                    out=lnres[:, 0:512],
                    in0=xu[:, t * C + half * 512: t * C + (half + 1) * 512],
                    scalar1=ln2st[:, 2 * tt: 2 * tt + 1], scalar2=ln2st[:, 2 * tt + 1: 2 * tt + 2],
                    op0=ALU.mult, op1=ALU.add),
                    r=[f"xbl:{t}", f"ln2st:{tt}"], w=["lnres"])
                R.op("vector", lambda E, half=half, Sp=Sp: E.tensor_add(
                    out=xab[:, half * 512:(half + 1) * 512],
                    in0=lnres[:, 0:512], in1=Sp[:, 0:512]),
                    r=["lnres", f"Sb{3*slot}"], w=[f"xab:{half}"])
            R.op("sync", lambda E, tt=tt: E.dma_start(
                out=xascr[:, tt * C:(tt + 1) * C], in_=xab[:]),
                r=["xab:0", "xab:1"], w=[f"xascr:{tt}"], dma=True)

    R.barrier()

    for u in range(4):
        R.op("sync", lambda E, u=u: E.dma_start(out=xu[:], in_=xascr[:, 4 * u * C: 4 * (u + 1) * C]),
             r=[f"xascr:{4*u+t}" for t in range(4)], w=[f"xu:{t}" for t in range(4)], dma=True)
        for t in range(4):
            tt = 4 * u + t
            rms_inv([f"xu:{t}"], lambda t=t: xu[:, t * C:(t + 1) * C])
            R.op("vector", lambda E, t=t: E.tensor_scalar(
                out=sqb[:, 0:512], in0=xu[:, t * C: t * C + 512], scalar1=lnt[:, 2:3],
                scalar2=None, op0=ALU.mult),
                r=[f"xu:{t}", "lnt2"], w=["sqb"])
            R.op("vector", lambda E, t=t: E.tensor_scalar(
                out=silu_s[:, 0:512], in0=xu[:, t * C + 512:(t + 1) * C], scalar1=lnt[:, 2:3],
                scalar2=None, op0=ALU.mult),
                r=[f"xu:{t}", "lnt2"], w=["silu_s"])
            for cch in range(8):
                slot = cch % 2
                bv = Sbf[slot]
                if cch < 4:
                    src_fn = (lambda c=cch: sqb[:, c * 128:(c + 1) * 128])
                    skey2 = "sqb"
                else:
                    src_fn = (lambda c=cch: silu_s[:, (c - 4) * 128:(c - 4) * 128 + 128])
                    skey2 = "silu_s"
                R.op("tensor", lambda E, src_fn=src_fn, bv=bv: E.transpose(
                    out=bv[:, 0:128], in_=src_fn(), identity=i128b[:]),
                    r=[skey2, "i128b"], w=[f"Sb{3*slot}"])
                R.op("vector", lambda E, c=cch, bv=bv, tt=tt: E.tensor_copy(
                    out=xanT[:, c * 2048 + tt * 128: c * 2048 + (tt + 1) * 128], in_=bv[:, 0:128]),
                    r=[f"Sb{3*slot}"], w=[f"xanT:{cch}"])

    for G in range(4):
        slot = G % 2
        Sp = Sg[slot]
        for kc in range(8):
            R.op("tensor", lambda E, G=G, kc=kc, Sp=Sp: E.matmul(
                out=Sp[0:8, 0:512], lhsT=wgt_s[:, kc * 8:(kc + 1) * 8],
                rhs=xanT[:, kc * 2048 + G * 512: kc * 2048 + (G + 1) * 512],
                start=(kc == 0), stop=(kc == 7)),
                r=["wgt_s", f"xanT:{kc}"], w=[f"Sb{3*slot}"])
        R.op("vector", lambda E, G=G, Sp=Sp: E.tensor_scalar(
            out=gatef[0:8, G * 512:(G + 1) * 512], in0=Sp[0:8, 0:512],
            scalar1=ebias_s[0:8, 0:1], scalar2=None, op0=ALU.add),
            r=[f"Sb{3*slot}", "ebias_s"], w=[f"gatef:{G}"])
        for st4 in range(4):
            slot2 = st4 % 2
            R.op("tensor", lambda E, G=G, st4=st4, slot2=slot2: E.transpose(
                out=Sg[slot2][:, 0:8],
                in_=gatef[0:8, G * 512 + st4 * 128: G * 512 + (st4 + 1) * 128],
                identity=i128f[0:8, 0:8]),
                r=[f"gatef:{G}", "i128f"], w=[f"Sb{3*slot2}"])
            R.op("vector", lambda E, slot2=slot2: E.tensor_copy(out=gtop[:], in_=Sg[slot2][:, 0:8]),
                 r=[f"Sb{3*slot2}"], w=["glog"])
            R.op("vector", lambda E: E.max(out=gsc[:, 0:8], in_=gtop[:]),
                 r=["glog"], w=["gm"])
            R.op("vector", lambda E: E.tensor_scalar(out=gsc[:, 8:9], in0=gsc[:, 0:1], scalar1=-1.0,
                                                     scalar2=None, op0=ALU.mult),
                 r=["gm"], w=["gnm"])
            R.op("scalar", lambda E: E.activation(out=gsc[:, 16:24], in_=gtop[:], func=AF.Exp,
                                                  bias=gsc[:, 8:9]),
                 r=["glog", "gnm"], w=["ge"])
            R.op("vector", lambda E: E.tensor_scalar(out=gsc[:, 24:32], in0=gtop[:], scalar1=gsc[:, 1:2],
                                                     scalar2=None, op0=ALU.is_ge),
                 r=["glog", "gm"], w=["gsel"])
            R.op("vector", lambda E: E.tensor_mul(out=gsc[:, 16:24], in0=gsc[:, 16:24], in1=gsc[:, 24:32]),
                 r=["ge", "gsel"], w=["ge"])
            R.op("vector", lambda E: E.tensor_reduce(out=combz[:, 0:1], in_=gsc[:, 16:24],
                                                     axis=AX.X, op=ALU.add),
                 r=["ge"], w=["gz"])
            R.op("vector", lambda E: E.reciprocal(out=combz[:, 1:2], in_=combz[:, 0:1]),
                 r=["gz"], w=["gzr"])
            R.op("vector", lambda E: E.tensor_scalar(out=gtop[:], in0=gsc[:, 16:24], scalar1=combz[:, 1:2],
                                                     scalar2=None, op0=ALU.mult),
                 r=["ge", "gzr"], w=["glog"])
            slot3 = (st4 + 1) % 2
            R.op("tensor", lambda E, slot3=slot3: E.transpose(
                out=Sg[slot3][0:8, 0:128], in_=gtop[:, 0:8], identity=i128f[:, 0:128]),
                r=["glog", "i128f"], w=[f"Sb{3*slot3}"])
            R.op("vector", lambda E, G=G, st4=st4, slot3=slot3: E.tensor_copy(
                out=combf[0:8, G * 512 + st4 * 128: G * 512 + (st4 + 1) * 128],
                in_=Sg[slot3][0:8, 0:128]),
                r=[f"Sb{3*slot3}"], w=[f"combf:{G}"])

    R.barrier()
    for e in range(9):
        sl = e % 2
        wgk, wuk, wdk = f"wg{sl}", f"wu{sl}", f"wd{sl}"
        off = e * 12288
        R.op("gpsimd", lambda E, off=off, sl=sl: E.dma_start(
            out=wg_s[sl][:], in_=wexp[:, off: off + 4096]), w=[wgk], dma=True)
        R.op("gpsimd", lambda E, off=off, sl=sl: E.dma_start(
            out=wu_s[sl][:], in_=wexp[:, off + 4096: off + 8192]), w=[wuk], dma=True)
        R.op("gpsimd", lambda E, off=off, sl=sl: E.dma_start(
            out=wd_s[sl][:], in_=wexp[:, off + 8192: off + 12288]), w=[wdk], dma=True)
        for kc in range(8):
            R.op("vector", lambda E, kc=kc, sl=sl: E.tensor_scalar(
                out=wg_s[sl][:, kc * 512:(kc + 1) * 512], in0=wg_s[sl][:, kc * 512:(kc + 1) * 512],
                scalar1=wln2c_s[:, kc:kc + 1], scalar2=None, op0=ALU.mult),
                r=[wgk, "wln2c_s"], w=[wgk])
            R.op("vector", lambda E, kc=kc, sl=sl: E.tensor_scalar(
                out=wu_s[sl][:, kc * 512:(kc + 1) * 512], in0=wu_s[sl][:, kc * 512:(kc + 1) * 512],
                scalar1=wln2c_s[:, kc:kc + 1], scalar2=None, op0=ALU.mult),
                r=[wuk, "wln2c_s"], w=[wuk])
        for G in range(4):
            if e > 0:
                R.op("tensor", lambda E, e=e, G=G: E.matmul(
                    out=S0[:, 512:1024], lhsT=sel8_s[:, (e - 1) * 128: e * 128],
                    rhs=combf[0:8, G * 512:(G + 1) * 512], start=True, stop=True),
                    r=["sel8_s", f"combf:{G}"], w=["m_bc"])
            for ic in range(4):
                par = ic % 2
                gcol = 0 if par == 0 else 1024
                gk, uk2 = f"m_g{par}", f"m_u{par}"
                sqk = f"sqm{par}"
                for kc in range(8):
                    R.op("tensor", lambda E, G=G, ic=ic, kc=kc, sl=sl, gcol=gcol: E.matmul(
                        out=S0[:, gcol:gcol + 512], lhsT=wg_s[sl][:, kc * 512 + ic * 128: kc * 512 + (ic + 1) * 128],
                        rhs=xanT[:, kc * 2048 + G * 512: kc * 2048 + (G + 1) * 512],
                        start=(kc == 0), stop=(kc == 7)),
                        r=[wgk, f"xanT:{kc}"], w=[gk])
                    R.op("tensor", lambda E, G=G, ic=ic, kc=kc, sl=sl, gcol=gcol: E.matmul(
                        out=S1[:, gcol:gcol + 512], lhsT=wu_s[sl][:, kc * 512 + ic * 128: kc * 512 + (ic + 1) * 128],
                        rhs=xanT[:, kc * 2048 + G * 512: kc * 2048 + (G + 1) * 512],
                        start=(kc == 0), stop=(kc == 7)),
                        r=[wuk, f"xanT:{kc}"], w=[uk2])
                R.op("scalar", lambda E, gcol=gcol: E.activation(
                    out=silu_s[:], in_=S0[:, gcol:gcol + 512], func=AF.Silu),
                     r=[gk], w=["silu_s"])
                R.op("vector", lambda E, gcol=gcol, par=par: E.scalar_tensor_tensor(
                    out=sqb[:] if par == 0 else lnres[:, 0:512].bitcast(BF16)[:, 0:512],
                    in0=S1[:, gcol:gcol + 512], scalar=1.0, in1=silu_s[:],
                    op0=ALU.mult, op1=ALU.mult),
                    r=[uk2, "silu_s"], w=[sqk])
                mm_in = (lambda par=par: sqb[:] if par == 0 else lnres[:, 0:512].bitcast(BF16)[:, 0:512])
                if e > 0:
                    R.op("vector", lambda E, ic=ic, mm_in=mm_in: E.tensor_mul(
                        out=mid_s[:, ic * 512:(ic + 1) * 512], in0=mm_in(), in1=S0[:, 512:1024]),
                        r=[sqk, "m_bc"], w=[f"mid:{ic}"])
                else:
                    R.op("vector", lambda E, ic=ic, mm_in=mm_in: E.tensor_copy(
                        out=mid_s[:, ic * 512:(ic + 1) * 512], in_=mm_in()),
                        r=[sqk], w=[f"mid:{ic}"])
            for tsub in range(4):
                for half in range(2):
                    k8 = tsub * 2 + half
                    dps = accp if k8 % 2 == 0 else auxp
                    skey3 = "accp" if k8 % 2 == 0 else "auxp"
                    for ic in range(4):
                        R.op("tensor", lambda E, tsub=tsub, half=half, ic=ic, sl=sl,
                             dps=dps: E.matmul(
                            out=dps[:, 0:512],
                            lhsT=mid_s[:, ic * 512 + tsub * 128: ic * 512 + (tsub + 1) * 128],
                            rhs=wd_s[sl][:, ic * C + half * 512: ic * C + (half + 1) * 512],
                            start=(ic == 0), stop=(ic == 3)),
                            r=[f"mid:{ic}", wdk], w=[skey3])
                    tt = G * 4 + tsub
                    mkey = f"moe:{tt}:{half}"
                    if e == 0:
                        R.op("vector", lambda E, tt=tt, half=half, dps=dps: E.tensor_copy(
                            out=moeacc[:, tt * C + half * 512: tt * C + (half + 1) * 512],
                            in_=dps[:, 0:512]),
                            r=[skey3], w=[mkey])
                    else:
                        R.op("vector", lambda E, tt=tt, half=half, dps=dps: E.tensor_add(
                            out=moeacc[:, tt * C + half * 512: tt * C + (half + 1) * 512],
                            in0=moeacc[:, tt * C + half * 512: tt * C + (half + 1) * 512],
                            in1=dps[:, 0:512]),
                            r=[skey3, mkey], w=[mkey])
            if e == 8:
                u = G
                R.op("sync", lambda E, u=u: E.dma_start(
                    out=xu[:], in_=xscr[:, 4 * u * C: 4 * (u + 1) * C]),
                    w=[f"xu:{t}" for t in range(4)], dma=True)
                for t in range(4):
                    tt = 4 * u + t
                    R.op("sync", lambda E, tt=tt: E.dma_start(
                        out=xau[:], in_=xascr[:, tt * C:(tt + 1) * C]),
                        r=[f"xascr:{tt}"], w=["xaf"], dma=True)
                    R.op("vector", lambda E, tt=tt: E.tensor_add(
                        out=xau[:], in0=xau[:], in1=moeacc[:, tt * C:(tt + 1) * C]),
                        r=["xaf", f"moe:{tt}:0", f"moe:{tt}:1"], w=["xaf"])
                    ln_stats(["xaf"], lambda: xau[:])
                    R.op("vector", lambda E: E.tensor_scalar(
                        out=lnres[:], in0=xau[:, 0:C],
                        scalar1=lnt[:, 0:1], scalar2=lnt[:, 1:2], op0=ALU.mult, op1=ALU.add),
                        r=["xaf", "lnt0", "lnt1"], w=["lnres"])
                    R.op("vector", lambda E, t=t: E.tensor_add(
                        out=xu[:, t * C:(t + 1) * C], in0=xu[:, t * C:(t + 1) * C], in1=lnres[:]),
                        r=["lnres", f"xu:{t}"], w=[f"xu:{t}"])
                R.op("sync", lambda E, u=u: E.dma_start(
                    out=o_x[:, 4 * u * C: 4 * (u + 1) * C], in_=xu[:]),
                    r=[f"xu:{t}" for t in range(4)], w=[f"oxd:{u}"], dma=True)

    if taps:
        o_moe = nc.dram_tensor("o_moe", [128, 16 * C], F32, kind="ExternalOutput").ap()
        R.op("gpsimd", lambda E: E.dma_start(out=o_moe[:], in_=moeacc[:]),
             r=[f"moe:{tt}:{h}" for tt in range(16) for h in range(2)], w=["omoe"], dma=True)
        o_xan = nc.dram_tensor("o_xan", [128, 16384], F32, kind="ExternalOutput").ap()
        R.op("gpsimd", lambda E: E.dma_start(out=o_xan[:], in_=xanT[:]),
             r=[f"xanT:{cch}" for cch in range(8)], w=["oxan"], dma=True)
        o_comb = nc.dram_tensor("o_comb", [8, 2048], F32, kind="ExternalOutput").ap()
        R.op("gpsimd", lambda E: E.dma_start(out=o_comb[:], in_=combf[:]),
             r=[f"combf:{G}" for G in range(4)], w=["ocomb"], dma=True)
        o_xs = nc.dram_tensor("o_xs", [128, 16 * C], F32, kind="ExternalOutput").ap()
        o_xa = nc.dram_tensor("o_xa", [128, 16 * C], F32, kind="ExternalOutput").ap()
        o_q = nc.dram_tensor("o_q", [128, 16384], F32, kind="ExternalOutput").ap()
        o_qf = nc.dram_tensor("o_qf", [128, 16384], F32, kind="ExternalOutput").ap()
        R.op("gpsimd", lambda E: E.dma_start(out=o_qf[:], in_=qgin[:]),
             r=["qgin"], w=["oqf"], dma=True)
        o_cc = nc.dram_tensor("o_cc", [1280, 2048], F32, kind="ExternalOutput").ap()
        R.op("gpsimd", lambda E: E.dma_start(out=o_cc[:], in_=ccout[:]),
             r=["ccout"], w=["occ"], dma=True)
        o_yg = nc.dram_tensor("o_yg", [1024, 8192], F32, kind="ExternalOutput").ap()
        R.op("gpsimd", lambda E: E.dma_start(out=o_yg[:], in_=ygout[:]),
             r=["ygout"], w=["oyg"], dma=True)
        R.op("sync", lambda E: E.dma_start(out=o_xs[:], in_=xscr[:]),
             r=[f"xscr:{u}" for u in range(4)], w=["oxs"], dma=True)
        R.op("sync", lambda E: E.dma_start(out=o_xa[:], in_=xascr[:]),
             r=[f"xascr:{tt}" for tt in range(16)], w=["oxa"], dma=True)
        R.op("gpsimd", lambda E: E.dma_start(out=o_q[:], in_=q_all[:]),
             r=[f"q_all:{s}" for s in range(8)], w=["oq"], dma=True)
        R.op("gpsimd", lambda E: E.nop(), r=["oxs", "oxa", "oq", "oqf", "occ", "oyg", "omoe", "oxan", "ocomb"])
    R.op("gpsimd", lambda E: E.nop(), r=[f"oxd:{u}" for u in range(4)])

    R.emit(nc)
    return nc



def _build_wqkv(W, bias):
    Wp = np.zeros((1024, 5120), dtype=np.float32)
    bp = np.zeros(5120, dtype=np.float32)

    def put(dst, src64):
        Wp[:, dst:dst + 64] = W[:, src64:src64 + 64]
        bp[dst:dst + 64] = bias[src64:src64 + 64]

    for j in range(16):
        put(j * 128, 192 * j)
    for j in range(8):
        put(2048 + j * 128, 192 * (2 * j) + 64)
        put(2048 + j * 128 + 64, 192 * (2 * j + 1) + 64)
        put(3072 + j * 128, 192 * (2 * j + 1) + 64)
        put(3072 + j * 128 + 64, 192 * (2 * j) + 64)
        put(4096 + j * 128, 192 * (2 * j) + 128)
        put(4096 + j * 128 + 64, 192 * (2 * j + 1) + 128)
    return Wp, bp


def _chunkmajor(w, nk=8):
    cin, f = w.shape
    assert cin == nk * 128
    return np.ascontiguousarray(w.reshape(nk, 128, f).transpose(1, 0, 2).reshape(128, nk * f))


def _rope_tables(positions, nrows, rope_rows):
    t = np.asarray(positions, dtype=np.float32)
    d = 32
    freqs = (1.0 / THETA ** (np.arange(0, d, 2, dtype=np.float32) / d)).astype(np.float32)
    ang = np.outer(freqs, t)
    cosv, sinv = np.cos(ang).astype(np.float32), np.sin(ang).astype(np.float32)
    Ct = np.ones((nrows, len(t)), dtype=np.float32)
    St = np.zeros((nrows, len(t)), dtype=np.float32)
    for r0 in rope_rows:
        for i in range(16):
            Ct[r0 + 2 * i] = cosv[i]
            Ct[r0 + 2 * i + 1] = cosv[i]
            St[r0 + 2 * i] = -sinv[i]
            St[r0 + 2 * i + 1] = sinv[i]
    return Ct, St


def prep_core_inputs(c, inputs):
    blocks = [c, 15 - c]
    x = np.asarray(inputs["L0_attn_inputs"])
    rows = np.concatenate([x[b, 512 * j:512 * (j + 1)] for b in (0, 1) for j in blocks])
    xrows = np.ascontiguousarray(rows.reshape(16, 128, 1024).transpose(1, 0, 2).reshape(128, 16 * 1024))

    Wp, bp = _build_wqkv(np.asarray(inputs["L0_attn_qkv_kernel"]),
                         np.asarray(inputs["L0_attn_qkv_bias"]).reshape(3072))
    wqkv = _chunkmajor(Wp)
    qkvb = np.ascontiguousarray(bp.reshape(40, 128).T)

    i128 = np.eye(128, dtype=np.float32)
    maskTi = np.triu(np.ones((128, 128), dtype=np.float32))
    maskTx = np.triu(np.ones((128, 128), dtype=np.float32), k=1)
    blk2 = np.zeros((128, 2), dtype=np.float32)
    blk2[0:64, 0] = 1.0
    blk2[64:128, 1] = 1.0
    ones1 = np.ones((1, 128), dtype=np.float32)
    cvals_np = np.zeros((128, 8), dtype=np.float32)
    for ci, cv in enumerate(CONST_VALS):
        cvals_np[:, ci] = np.float32(cv)
    onescol = np.ones((128, 1), dtype=np.float32)
    grp4 = np.zeros((128, 4), dtype=np.float32)
    for g in range(4):
        grp4[32 * g:32 * (g + 1), g] = 1.0
    sel8 = np.zeros((8, 8 * 128), dtype=np.float32)
    for e in range(8):
        sel8[e, e * 128:(e + 1) * 128] = 1.0
    ropesc = np.zeros((128, 4 * 128), dtype=np.float32)
    for g in range(4):
        for r in range(32):
            ropesc[32 * g + r, g * 128 + 32 + r] = 1.0
            ropesc[32 * g + r, g * 128 + 64 + r] = 1.0

    Wq = np.asarray(inputs["L0_ffn_Wq"])
    permq = []
    for h in range(16):
        if h % 2 == 0:
            permq += list(range(64 * h, 64 * h + 64))
        else:
            permq += list(range(64 * h + 32, 64 * h + 64)) + list(range(64 * h, 64 * h + 32))
    wqp = _chunkmajor(Wq[:, permq])
    wkva = _chunkmajor(np.asarray(inputs["L0_ffn_Wkva"]))
    wln1c = np.ascontiguousarray(np.asarray(inputs["L0_ffn_w_ln1"]).reshape(8, 128).T)
    wln2c = np.ascontiguousarray(np.asarray(inputs["L0_ffn_w_ln2"]).reshape(8, 128).T)
    Wkvb = np.asarray(inputs["L0_ffn_Wkvb"])
    wkvbka = np.zeros((128, 128), dtype=np.float32)
    wkvbka[:, 0:32] = Wkvb[:, 32 * (2 * c): 32 * (2 * c) + 32]
    wkvbka[:, 96:128] = Wkvb[:, 32 * (2 * c + 1): 32 * (2 * c + 1) + 32]
    wkvbtok = np.zeros((128, 66), dtype=np.float32)
    wkvbtok[:, 0:32] = Wkvb[:, 32 * (2 * c): 32 * (2 * c) + 32]
    wkvbtok[:, 33:65] = Wkvb[:, 32 * (2 * c + 1): 32 * (2 * c + 1) + 32]
    idx8 = np.zeros((128, 8), dtype=np.int32)
    for s in range(8):
        idx8[:, s] = s * 1024 + np.arange(128) * 8 + c
    Wo = np.asarray(inputs["L0_ffn_Wo"])
    woe_rows = np.concatenate([Wo[64 * h: 64 * h + 32] for h in range(16)])
    woe = _chunkmajor(woe_rows, nk=4)
    wgt = _chunkmajor(np.asarray(inputs["L0_ffn_Wgate"]))
    ebias = np.asarray(inputs["L0_ffn_expert_bias"]).reshape(8, 1).astype(np.float32)

    positions = np.concatenate([np.arange(512 * j, 512 * (j + 1)) for b in (0, 1) for j in blocks])
    cqt, sqt = _rope_tables(positions, 128, [32, 64])
    ckt, skt = _rope_tables(positions, 32, [0])

    Wg = np.asarray(inputs["L0_ffn_Wg"])
    Wu = np.asarray(inputs["L0_ffn_Wu"])
    Wd = np.asarray(inputs["L0_ffn_Wd"])
    parts = []
    for e in range(9):
        if e == 0:
            g_, u_, d_ = (np.asarray(inputs["L0_ffn_sWg"])[0], np.asarray(inputs["L0_ffn_sWu"])[0],
                          np.asarray(inputs["L0_ffn_sWd"])[0])
        else:
            g_, u_, d_ = Wg[e - 1], Wu[e - 1], Wd[e - 1]
        parts += [_chunkmajor(g_), _chunkmajor(u_), _chunkmajor(d_, nk=4)]
    wexp = np.concatenate(parts, axis=1)

    return dict(
        xrows=xrows.astype(np.float32), wqkv=wqkv.astype(np.float32),
        qkvb=qkvb.astype(np.float32), i128bf=i128, maskTi=maskTi, maskTx=maskTx,
        blk2=blk2, ones1=ones1, cvals=cvals_np, onescol=onescol, grp4=grp4,
        sel8=sel8, ropesc=ropesc, wqp=wqp.astype(np.float32),
        wkva=wkva.astype(np.float32), wln1c=wln1c.astype(np.float32),
        wln2c=wln2c.astype(np.float32), wkvbka=wkvbka, wkvbtok=wkvbtok,
        woe=woe.astype(np.float32), wgt=wgt.astype(np.float32), ebias=ebias,
        idx8=idx8,
        cq=cqt, sq_t=sqt, ck=ckt, sk_t=skt, wexp=wexp.astype(np.float32))


def unpack_x(o_x):
    return np.ascontiguousarray(o_x.reshape(128, 16, 1024).transpose(1, 0, 2).reshape(2048, 1024))


def kernel(**inputs):
    import os
    stage = os.environ.get("MK_STAGE", "full")
    nc = build_nc(0, stage=stage)
    in_maps = [prep_core_inputs(c, inputs) for c in range(8)]
    res = run_bass_kernel_spmd(nc, in_maps, core_ids=list(range(8)))
    out = np.zeros((2, 8192, 1024), dtype=np.float32)
    for c in range(8):
        rows = unpack_x(np.asarray(res.results[c]["o_x"]))
        blocks = [c, 15 - c]
        i = 0
        for b in (0, 1):
            for j in blocks:
                out[b, 512 * j:512 * (j + 1)] = rows[i * 512:(i + 1) * 512]
                i += 1
    return out
